# version 06
# baseline (speedup 1.0000x reference)
import os
import numpy as np
import ml_dtypes
import concourse.bass as bass
import concourse.mybir as mybir
from concourse.bass_utils import run_bass_kernel_spmd

F32 = mybir.dt.float32
BF16 = mybir.dt.bfloat16
AF = mybir.ActivationFunctionType
ALU = mybir.AluOpType
AX = mybir.AxisListType

T = 2048
NT = 4
TT = 512
NTC = 16
NF = 8
D = 1024
C = 128
H = 16
DH = 64
MAGIC = 12582912.0
TWO_PI = 6.283185307179586
ENGS = ("sync", "gpsimd", "tensor", "vector", "scalar")
GROUPS8 = [[0, 1], [2, 3], [4, 5], [6, 7]]


class Sem:
    __slots__ = ("h", "n")

    def __init__(self, nc, name):
        self.h = nc.alloc_semaphore(name)
        self.n = 0


class PT:
    __slots__ = ("ap", "free", "bf")

    def __init__(self, ap, bf=None):
        self.ap = ap
        self.free = None
        self.bf = bf


class KB:
    def __init__(self):
        nc = self.nc = bass.Bass("TRN2", target_bir_lowering=False, debug=False)
        self.q = {e: [] for e in ENGS}
        self.wm = {e: {} for e in ENGS}
        self.sem = {}
        for name in ("t", "v", "s", "p", "d1", "d2", "cc", "st", "wA", "wB", "w2s", "dbg", "tr", "sp"):
            self.sem[name] = Sem(nc, "sem_" + name)
        self.rowtok = None
        self.rtok_w13 = None
        self.rtok_w2 = None
        self.scope = "init"

    def phase(self):
        return [(self.sem[n], self.sem[n].n) for n in ("t", "v", "s")]

    def em(self, eng, fn, waits=(), inc=None, k=1):
        wl = []
        wm = self.wm[eng]
        for tokv in waits:
            if tokv is None:
                continue
            s, v = tokv
            if v <= 0:
                continue
            if wm.get(s, 0) < v:
                wm[s] = v
                wl.append((s.h, v))
        tok = None
        hk = None
        if inc is not None:
            inc.n += k
            tok = (inc, inc.n)
            hk = (inc.h, k)

        def run(e, wl=wl, fn=fn, hk=hk):
            for h, v in wl:
                e.wait_ge(h, v)
            inst = fn(e)
            if hk is not None:
                inst.then_inc(*hk)
        run._scope = self.scope
        self.q[eng].append(run)
        return tok

    def dma(self, qeng, out, in_, waits=(), stream=None):
        sem = self.sem[stream] if stream else (
            self.sem["d1"] if qeng == "sync" else self.sem["d2"])
        return self.em(qeng, lambda e: e.dma_start(out=out, in_=in_),
                       waits=waits, inc=sem, k=16)

    def mm(self, out, lhsT, rhs, start, stop, waits=(), inc=True, skip_gc=False,
           force_inc=False):
        return self.em(
            "tensor",
            lambda e: e.matmul(out, lhsT, rhs, start=start, stop=stop,
                               skip_group_check=skip_gc),
            waits=waits, inc=self.sem["t"] if (force_inc or (inc and stop)) else None)

    def mm_group(self, pt, parts, waits=(), skip_gc=False, out=None):
        n = len(parts)
        tok = None
        oap = pt.ap if out is None else out
        for j, (lhsT, rhs) in enumerate(parts):
            w = (list(waits) + [pt.free]) if j == 0 else []
            tok = self.mm(oap, lhsT, rhs, start=(j == 0), stop=(j == n - 1),
                          waits=w, inc=True, skip_gc=skip_gc)
        return tok

    def trans(self, out_ap, in_ap, ident, waits=()):
        return self.em("tensor", lambda e: e.transpose(out_ap, in_ap, ident),
                       waits=waits, inc=self.sem["t"])

    def vec(self, fn, waits=(), inc=True):
        return self.em("vector", fn, waits=waits,
                       inc=self.sem["v"] if inc else None)

    def sca(self, fn, waits=(), inc=True):
        return self.em("scalar", fn, waits=waits,
                       inc=self.sem["s"] if inc else None)

    def gp(self, fn, waits=()):
        return self.em("gpsimd", fn, waits=waits, inc=self.sem["p"])

    def finish(self):
        nc = self.nc
        q = self.q

        def runall(e, fns):
            cur = None
            for f in fns:
                s = getattr(f, "_scope", None)
                if s != cur:
                    if cur is not None:
                        nc.pop_named_scope(cur)
                    if s is not None:
                        nc.push_named_scope(s)
                    cur = s
                f(e)
            if cur is not None:
                nc.pop_named_scope(cur)
        with nc.Block() as blk:
            @blk.sync
            def _(e):
                runall(e, q["sync"])

            @blk.gpsimd
            def _(e):
                runall(e, q["gpsimd"])

            @blk.tensor
            def _(e):
                runall(e, q["tensor"])

            @blk.vector
            def _(e):
                runall(e, q["vector"])

            @blk.scalar
            def _(e):
                runall(e, q["scalar"])
        return nc


def build(n_stop=6, groups=GROUPS8, dbg=()):
    kb = KB()
    nc = kb.nc
    sem = kb.sem
    sb = nc.alloc_sbuf_tensor
    ps = nc.alloc_psum_tensor

    def di(name, shape):
        return nc.dram_tensor(name, shape, F32, kind="ExternalInput").ap()

    x_in = di("x_fm", [D, T])
    tidxrow = di("tidxrow", [1, T])
    tidxcol = di("tidxcol", [128, NTC])
    oddmask = di("oddmask", [128, 1])
    gw_t = di("gw_t", [D, 16])
    ebias_tile = di("ebias_tile", [128, 16])
    w1_t = di("w1_t", [16, D, D])
    w2_t = di("w2_t", [16, D, D])
    w3_t = di("w3_t", [16, D, D])
    sw1_t = di("sw1_t", [D, D])
    sw2_t = di("sw2_t", [D, D])
    sw3_t = di("sw3_t", [D, D])
    aW_t = di("aW_t", [D, D])
    knwcol = di("knwcol", [128, 1])
    onw_c = di("onw_c", [128, NF])
    onb_c = di("onb_c", [128, NF])
    lamrow = di("lamrow", [1, 16])
    etarow = di("etarow", [1, 16])
    Wqkvu_t = di("Wqkvu_t", [D, 4 * D])
    bq_col = di("bq_col", [128, 32])
    bkv_tile = di("bkv_tile", [128, 2 * D])
    Wout_t = di("Wout_t", [D, D])
    boutcol = di("boutcol", [128, NF])
    thetarow = di("thetarow", [1, D])
    thetacol_in = di("thetacol_in", [128, NF])
    lnwcol = di("lnwcol", [128, NF])
    lnbcol = di("lnbcol", [128, NF])
    Ws1 = di("Ws1", [D, 4 * D])
    bs1col = di("bs1col", [128, 32])
    Ws2 = di("Ws2", [2 * D, D])
    bs2col = di("bs2col", [128, NF])
    We1 = di("We1", [12, D, 6 * D])
    be1col = di("be1col", [128, 12 * 48])
    We2 = di("We2", [12, 3 * D, D])
    be2col = di("be2col", [128, 12 * NF])
    Wr = di("Wr", [D, 12])
    br_tile = di("br_tile", [128, 12])

    xout = nc.dram_tensor("xout", [D, T], F32, kind="ExternalOutput").ap()

    cc_s_in = nc.dram_tensor("cc_s_in", [128, 8 * DH], F32, kind="Internal")
    cc_s_out = nc.dram_tensor("cc_s_out", [256, 8 * DH], F32, kind="Internal")
    cc_m_in = nc.dram_tensor("cc_m_in", [128, H * DH], F32, kind="Internal")
    cc_m_out = nc.dram_tensor("cc_m_out", [128, H * DH], F32, kind="Internal")
    u_sp = nc.dram_tensor("u_sp", [NTC * 128, 2048], BF16, kind="Internal")
    a_sp = nc.dram_tensor("a_sp", [NTC * 128, 8 * DH], F32, kind="Internal")
    b_sp = nc.dram_tensor("b_sp", [NTC * 128, 8 * DH], F32, kind="Internal")

    ident_f = nc.inline_tensor(np.eye(128, dtype=np.float32), "ident_f").ap()
    ident_b = nc.inline_tensor(np.eye(128, dtype=np.float32).astype(ml_dtypes.bfloat16), "ident_b").ap()
    ij = np.arange(128)[:, None] - np.arange(128)[None, :]
    iju_h = nc.inline_tensor(np.maximum(-ij, 0).astype(ml_dtypes.bfloat16), "iju").ap()
    ijl_h = nc.inline_tensor(np.maximum(ij, 0).astype(ml_dtypes.bfloat16), "ijl").ap()
    mu_h = nc.inline_tensor((ij < 0).astype(ml_dtypes.bfloat16), "mu").ap()
    ml_h = nc.inline_tensor((ij > 0).astype(ml_dtypes.bfloat16), "ml").ap()
    ones_r_f = nc.inline_tensor(np.ones((1, 128), dtype=np.float32), "ones_r_f").ap()
    ones_r_b = nc.inline_tensor(np.ones((1, 128), dtype=np.float32).astype(ml_dtypes.bfloat16), "ones_r_b").ap()
    ones_c_f = nc.inline_tensor(np.ones((128, 1), dtype=np.float32), "ones_c_f").ap()
    ones_c_b = nc.inline_tensor(np.ones((128, 1), dtype=np.float32).astype(ml_dtypes.bfloat16), "ones_c_b").ap()
    icol1_h = nc.inline_tensor(np.arange(1, 129, dtype=np.float32)[None, :], "icol1").ap()
    icolrev_h = nc.inline_tensor(np.arange(127, -1, -1, dtype=np.float32)[None, :], "icolrev").ap()
    h2np = np.zeros((128, 2), dtype=np.float32)
    h2np[0:64, 0] = 1.0
    h2np[64:128, 1] = 1.0
    h2_h = nc.inline_tensor(h2np, "h2c").ap()
    p2_h = nc.inline_tensor(np.ascontiguousarray(h2np.T).astype(ml_dtypes.bfloat16), "p2c").ap()
    cvals = [0.0, 1.0, -1.0, 0.5, 1e-5, float(np.finfo(np.float32).eps)]
    consts_h = nc.inline_tensor(
        np.tile(np.array(cvals, dtype=np.float32)[None, :], (128, 1)), "constsh").ap()

    X = sb("X", [128, NF * T], F32)
    XN = sb("XN", [128, NF * T], BF16)
    ACT = sb("ACT", [128, NF * T], BF16)
    SCR = sb("SCR", [128, NF * T], BF16)
    R5A = sb("R5A", [128, 8192], BF16)
    R5B = sb("R5B", [128, 8192], BF16)
    a_r5a = nc.lookup_mloc(R5A).addr
    a_r5b = nc.lookup_mloc(R5B).addr
    a_scr = nc.lookup_mloc(SCR).addr
    a_xn = nc.lookup_mloc(XN).addr
    a_act = nc.lookup_mloc(ACT).addr
    at = nc.alloc_sbuf_tensor_at
    w2b = at("w2b", [128, NF * D], BF16, offset=a_r5a)
    w13b = at("w13b", [128, 2 * 2 * NF * 128], BF16, offset=a_r5b)
    s_grep = at("s_grep", [128, T], BF16, offset=a_r5b + 8192)

    s_consts = sb("s_consts", [128, len(cvals)], F32)
    for i, v in enumerate(cvals):
        nc.const_aps.register(F32, v, s_consts[:, i:i + 1])
    s_idf = sb("s_idf", [128, 128], F32)
    s_idb = sb("s_idb", [128, 128], BF16)
    s_iju = sb("s_iju", [128, 128], BF16)
    s_ijl = sb("s_ijl", [128, 128], BF16)
    s_mu = sb("s_mu", [128, 128], BF16)
    s_ml = sb("s_ml", [128, 128], BF16)
    s_or_f = sb("s_or_f", [1, 128], F32)
    s_or_b = sb("s_or_b", [1, 128], BF16)
    s_oc_f = sb("s_oc_f", [128, 1], F32)
    s_oc_b = sb("s_oc_b", [128, 1], BF16)
    s_ic1 = sb("s_ic1", [1, 128], F32)
    s_icr = sb("s_icr", [1, 128], F32)
    s_odd = sb("s_odd", [128, 1], F32)
    s_h2 = sb("s_h2", [128, 2], F32)
    s_p2 = sb("s_p2", [2, 128], BF16)
    s_gw = sb("s_gw", [128, NF * 16], BF16)
    s_eb = sb("s_eb", [128, 16], F32)
    s_wrb = sb("s_wrb", [128, NF * 12], BF16)
    s_br = sb("s_br", [128, 12], F32)
    s_brs = sb("s_brs", [128, 12], F32)
    s_knw = sb("s_knw", [128, 1], F32)
    s_thc = sb("s_thc", [128, NF], F32)
    s_tic = sb("s_tic", [128, NTC], F32)
    s_lnw = sb("s_lnw", [128, NF], F32)
    s_lnb = sb("s_lnb", [128, NF], F32)
    s_boc = sb("s_boc", [128, NF], F32)
    bq_colS = sb("bq_colS", [128, 32], F32)
    s_onw = sb("s_onw", [128, NF], F32)
    s_onb = sb("s_onb", [128, NF], F32)
    s_be1 = at("s_be1", [128, 12 * 48], BF16, offset=a_r5b + 13312)
    s_be2 = at("s_be2", [128, 12 * NF], F32, offset=a_r5b + 14464)
    s_bs1 = at("s_bs1", [128, 32], F32, offset=a_r5b + 14848)
    s_bs2 = at("s_bs2", [128, NF], F32, offset=a_r5b + 14976)
    s_lam = sb("s_lam", [1, 16], F32)
    s_eta = sb("s_eta", [1, 16], F32)
    s_beta = sb("s_beta", [1, 16], F32)
    s_logl = sb("s_logl", [1, 16], F32)
    s_lamC = sb("s_lamC", [1, 16], F32)
    s_betac = sb("s_betac", [128, 16], F32)
    s_nbetac = sb("s_nbetac", [128, 16], F32)
    s_loglc = sb("s_loglc", [128, 16], F32)
    s_lamCc = sb("s_lamCc", [128, 16], F32)
    s_powl = sb("s_powl", [128, 16], F32)
    s_powr = sb("s_powr", [128, 16], F32)
    s_powlT = sb("s_powlT", [16, 128], F32)
    s_gT = sb("s_gT", [16, T], BF16)
    s_rstd = sb("s_rstd", [1, TT], BF16)
    s_mrs = sb("s_mrs", [1, TT], BF16)
    s_mean = sb("s_mean", [1, TT], BF16)
    s_rt = sb("s_rt", [128, 66], F32)
    s_tq = s_rt[:, 50:66]
    s_S = at("s_S", [128, 8 * DH], F32, offset=a_scr)
    s_Sexp = at("s_Sexp", [128, 8 * DH], F32, offset=a_scr + 2048)
    s_kc = at("s_kc", [128, NF * C], BF16, offset=a_scr + 4096)
    s_yc = at("s_yc", [128, NF * C], F32, offset=a_scr + 8192)
    _o = a_scr + 12288
    s_rmsr = at("s_rmsr", [2, NF * C], BF16, offset=_o); _o += 2048
    s_lamCI = at("s_lamCI", [128, 8 * DH], BF16, offset=_o); _o += 1024
    s_prep = at("s_prep", [128, 8 * C], BF16, offset=_o); _o += 2048
    s_usl = at("s_usl", [128, 2048], BF16, offset=_o); _o += 4096
    s_At = at("s_At", [128, 8 * DH], F32, offset=_o); _o += 2048
    s_Bt = at("s_Bt", [128, 8 * DH], F32, offset=_o); _o += 2048
    s_betp = at("s_betp", [128, 8], F32, offset=_o); _o += 32
    s_nbetp = at("s_nbetp", [128, 8], F32, offset=_o); _o += 32
    assert _o - a_scr <= 32768, _o - a_scr
    s_Dup = at("s_Dup", [128, 16 * C], BF16, offset=a_r5b)
    s_Dlo = at("s_Dlo", [128, 16 * C], BF16, offset=a_r5b + 4096)
    pG = at("pG", [128, 256], BF16, offset=a_r5a)
    pM1 = at("pM1", [128, 256], BF16, offset=a_r5a + 512)
    pNp = at("pNp", [128, 256], BF16, offset=a_r5a + 1024)
    pMp = [None] + [at(f"pMp{_i}", [128, 256], BF16,
                       offset=a_r5a + 1536 + (_i - 1) * 512) for _i in range(1, 7)]
    pXa = at("pXa", [128, 256], BF16, offset=a_r5a + 4608)
    pXb = at("pXb", [128, 256], BF16, offset=a_r5a + 5120)
    pKt = at("pKt", [128, 128], BF16, offset=a_r5a + 5632)
    pKd = at("pKd", [128, 128], BF16, offset=a_r5a + 5888)
    pFG = at("pFG", [128, 256], BF16, offset=a_r5a + 6144)
    pTT = at("pTT", [128, 256], BF16, offset=a_r5a + 6656)
    pPt = at("pPt", [128, 128], F32, offset=a_r5a + 7168)
    s_id2 = at("s_id2", [128, 256], BF16, offset=a_r5a + 7680)
    s_kgrp = at("s_kgrp", [128, 4 * D], BF16, offset=a_r5a)
    s_vgrp = at("s_vgrp", [128, 4 * D], BF16, offset=a_r5a + 8192)
    s_Mkv = at("s_Mkv", [128, H * DH], F32, offset=a_r5a)
    s_Mcos = at("s_Mcos", [128, 8 * DH], BF16, offset=a_r5a + 4096)
    s_Msin = at("s_Msin", [128, 8 * DH], BF16, offset=a_r5a + 6144)
    s_trep = at("s_trep", [128, TT], F32, offset=a_r5a + 8192)
    s_cfm = at("s_cfm", [128, TT], F32, offset=a_r5a + 10240)
    s_sfm = at("s_sfm", [128, TT], F32, offset=a_r5a + 12288)
    s_qc = at("s_qc", [128, TT], BF16, offset=a_r5a + 14336)
    s_qs = at("s_qs", [128, TT], BF16, offset=a_r5a + 15360)
    s_throw = at("s_throw", [1, D], F32, offset=a_r5b + 4096)
    s_tcos = at("s_tcos", [128, D], BF16, offset=a_r5b + 4096)
    s_tsin = at("s_tsin", [128, D], BF16, offset=a_r5b + 6144)
    s_threp = at("s_threp", [128, D], F32, offset=a_r5b + 8192)
    s_angB = at("s_angB", [128, TT], F32, offset=a_r5a)
    s_kvb = at("s_kvb", [128, 2 * 256], BF16, offset=a_r5b + 12288)
    y1 = at("y1", [128, NF * T], BF16, offset=a_xn)
    tmpA = sb("tmpA", [128, TT], F32)
    s_kst = at("s_kst", [128, D], BF16, offset=nc.lookup_mloc(tmpA).addr)
    tmpC = at("tmpC", [128, TT], BF16, offset=a_r5b + 12288)
    s_sel = s_rt[:, 0:16]
    s_bb = s_rt[:, 16:32]
    s_hit = s_rt[:, 32:48]
    s_m1 = s_rt[:, 48:49]
    s_m2 = s_rt[:, 49:50]

    psh = [ps(f"psb{i}", [128, TT], F32) for i in range(8)]
    psb = [PT(h.ap(), h.bitcast(BF16)) for h in psh]

    for f in range(NF):
        kb.dma("sync", X[:, f * T:(f + 1) * T], x_in[f * 128:(f + 1) * 128, :])
    for dst, src in [(s_consts[:], consts_h), (s_idf[:], ident_f), (s_idb[:], ident_b),
                     (s_iju[:], iju_h), (s_ijl[:], ijl_h), (s_mu[:], mu_h), (s_ml[:], ml_h),
                     (s_or_f[:], ones_r_f), (s_or_b[:], ones_r_b),
                     (s_oc_f[:], ones_c_f), (s_oc_b[:], ones_c_b),
                     (s_ic1[:], icol1_h), (s_icr[:], icolrev_h),
                     (s_odd[:], oddmask), (s_eb[:], ebias_tile),
                     (s_h2[:], h2_h), (s_p2[:], p2_h),
                     (s_br[:], br_tile), (s_knw[:], knwcol),
                     (s_thc[:], thetacol_in), (s_tic[:], tidxcol),
                     (s_lnw[:], lnwcol), (s_lnb[:], lnbcol),
                     (s_boc[:], boutcol), (bq_colS[:], bq_col),
                     (s_onw[:], onw_c), (s_onb[:], onb_c),
                     (s_lam[:], lamrow), (s_eta[:], etarow),
                     (s_be2[:], be2col),
                     (s_bs1[:], bs1col), (s_bs2[:], bs2col)]:
        kb.dma("sync", dst, src)
    tok_x0 = (sem["d1"], sem["d1"].n)
    kb.dma("gpsimd", s_be1[:], be1col)
    kb.dma("gpsimd", s_gw[:], bass.AP(gw_t.tensor, 0, [[16, 128], [16 * 128, NF], [1, 16]]))
    kb.dma("gpsimd", s_wrb[:], bass.AP(Wr.tensor, 0, [[12, 128], [12 * 128, NF], [1, 12]]))
    tok_init_d2 = (sem["d2"], sem["d2"].n)
    kb.em("gpsimd", lambda e, v=sem["d2"].n: e.wait_ge(sem["d2"].h, v))

    t1 = kb.sca(lambda e: e.activation(out=s_lam[:], in_=s_lam[:], func=AF.Sigmoid), waits=[tok_x0])
    t2 = kb.sca(lambda e: e.activation(out=s_eta[:], in_=s_eta[:], func=AF.Tanh), waits=[tok_x0])
    t3 = kb.vec(lambda e: e.tensor_scalar(out=s_beta[:], in0=s_lam[:], scalar1=1.0,
                                          scalar2=None, op0=ALU.subtract), waits=[t1])
    t4 = kb.vec(lambda e: e.tensor_tensor(out=s_beta[:], in0=s_beta[:], in1=s_eta[:],
                                          op=ALU.mult), waits=[t2, t3])
    t5 = kb.vec(lambda e: e.tensor_scalar(out=s_beta[:], in0=s_beta[:],
                                          scalar1=float(np.float32(1.0 / 64)), scalar2=None,
                                          op0=ALU.mult), waits=[t4])
    t6 = kb.sca(lambda e: e.activation(out=s_logl[:], in_=s_lam[:], func=AF.Ln), waits=[t1])
    t7 = kb.sca(lambda e: e.activation(out=s_lamC[:], in_=s_logl[:], func=AF.Exp, scale=128.0),
                waits=[t6])
    t8 = kb.vec(lambda e: e.tensor_scalar(out=s_brs[:], in0=s_br[:],
                                          scalar1=float(np.float32(1.0 / 1.5)), scalar2=None,
                                          op0=ALU.mult), waits=[tok_x0])

    def repl_row(dst, src_row, wait):
        pq = psb[6]
        tk = kb.mm(pq.ap[:, 0:16], s_or_f[0:1, :], src_row, True, True,
                   waits=[wait, pq.free])
        tv = kb.vec(lambda e: e.tensor_copy(out=dst, in_=pq.ap[:, 0:16]), waits=[tk])
        pq.free = tv
        return tv

    tbc = repl_row(s_betac[:], s_beta[0:1, :], t5)
    kb.vec(lambda e: e.tensor_scalar(out=s_nbetac[:], in0=s_betac[:], scalar1=-1.0,
                                     scalar2=None, op0=ALU.mult), waits=[tbc])
    repl_row(s_loglc[:], s_logl[0:1, :], t6)
    repl_row(s_lamCc[:], s_lamC[0:1, :], t7)
    pq = psb[6]
    tk = kb.mm(pq.ap[:, 0:16], s_ic1[0:1, :], s_logl[0:1, :], True, True,
               waits=[t6, pq.free])
    tpl = kb.sca(lambda e: e.activation(out=s_powl[:], in_=pq.ap[:, 0:16], func=AF.Exp),
                 waits=[tk])
    pq.free = tpl
    tk = kb.mm(pq.ap[:, 16:32], s_icr[0:1, :], s_logl[0:1, :], True, True,
               waits=[t6, pq.free])
    tpr = kb.sca(lambda e: e.activation(out=s_powr[:], in_=pq.ap[:, 16:32], func=AF.Exp),
                 waits=[tk])
    pq.free = tpr
    tk = kb.trans(psb[7].ap[0:16, 0:128], s_powl[:], s_idf[:], waits=[tpl, psb[7].free])
    tplT = kb.vec(lambda e: e.tensor_copy(out=s_powlT[:], in_=psb[7].ap[0:16, 0:128]),
                  waits=[tk])
    psb[7].free = tplT
    tgz = kb.vec(lambda e: e.memset(s_bb[:], 0.0))

    def emit_ln(x_tok):
        last = None
        for tt in range(NT):
            pS, pQ = psb[6], psb[7]
            g1 = [(s_oc_f[:, 0:1], X[:, f * T + tt * TT: f * T + (tt + 1) * TT])
                  for f in range(NF)]
            tk1 = kb.mm_group(pS, g1, waits=[x_tok], out=pS.ap[0:1, 0:TT])
            tk2 = None
            prev_mm = None
            for f in range(NF):
                xs = X[:, f * T + tt * TT: f * T + (tt + 1) * TT]
                stok = kb.vec(lambda e, xs=xs: e.tensor_tensor(out=tmpA[:], in0=xs, in1=xs,
                                                               op=ALU.mult),
                              waits=[x_tok, prev_mm, last])
                w = [stok]
                if f == 0:
                    w.append(pQ.free)
                tk2 = kb.mm(pQ.ap[0:1, 0:TT], s_oc_f[:, 0:1], tmpA[:], start=(f == 0),
                            stop=(f == NF - 1), waits=w, force_inc=True)
                prev_mm = tk2
            r_mean = s_mean[0:1, :]
            r_rstd = s_rstd[0:1, :]
            r_mrs = s_mrs[0:1, :]
            tv1 = kb.vec(lambda e, r=r_mean, p=pS.ap: e.tensor_scalar(
                out=r, in0=p[0:1, 0:TT], scalar1=float(np.float32(1.0 / D)), scalar2=None,
                op0=ALU.mult), waits=[tk1, kb.rowtok])
            pS.free = tv1
            tv2 = kb.vec(lambda e, r=r_mean: e.tensor_tensor(
                out=tmpA[0:1, 0:TT], in0=r, in1=r, op=ALU.mult), waits=[tv1, tk2])
            tv3 = kb.vec(lambda e, p=pQ.ap: e.scalar_tensor_tensor(
                out=tmpA[0:1, 0:TT], in0=p[0:1, 0:TT], scalar=float(np.float32(1.0 / D)),
                in1=tmpA[0:1, 0:TT], op0=ALU.mult, op1=ALU.subtract), waits=[tk2, tv2])
            pQ.free = tv3
            ts0 = kb.sca(lambda e: e.activation(
                out=tmpA[0:1, 0:TT], in_=tmpA[0:1, 0:TT], func=AF.Sqrt, bias=1e-5),
                waits=[tv3])
            def _recip(e, r=r_rstd):
                with e.bass.allow_low_precision(reason="xn stored bf16"):
                    return e.reciprocal(out=r, in_=tmpA[0:1, 0:TT])
            ts1 = kb.vec(_recip, waits=[ts0])
            tv4 = kb.vec(lambda e, r1=r_mrs, r2=r_rstd, rm=r_mean: e.tensor_tensor(
                out=r1, in0=rm, in1=r2, op=ALU.mult), waits=[ts1])
            prs, pms = psb[4], psb[5]
            tk3 = kb.mm(prs.ap, s_or_b[0:1, :], r_rstd, True, True,
                        waits=[ts1, prs.free])
            tk4 = kb.mm(pms.ap, s_or_b[0:1, :], r_mrs, True, True,
                        waits=[tv4, pms.free])
            for f in range(NF):
                xs = X[:, f * T + tt * TT: f * T + (tt + 1) * TT]
                xn = XN[:, f * T + tt * TT: f * T + (tt + 1) * TT]
                ta = kb.vec(lambda e, xs=xs: e.tensor_tensor(
                    out=tmpA[:], in0=xs, in1=prs.ap, op=ALU.mult),
                    waits=[tk3, x_tok, last])
                last = kb.vec(lambda e, xn=xn: e.tensor_tensor(
                    out=xn, in0=tmpA[:], in1=pms.ap, op=ALU.subtract),
                    waits=[tk4, ta])
            prs.free = last
            pms.free = last
        return last

    def emit_router(xn_tok, E, topk, wbuf, bias_sel, scale):
        last = None
        for tc in range(NTC):
            pl = psb[6]
            parts = [(XN[:, f * T + tc * 128: f * T + (tc + 1) * 128],
                      wbuf[:, f * E:(f + 1) * E]) for f in range(NF)]
            tkm = kb.mm_group(pl, parts, waits=[xn_tok, tok_init_d2],
                              out=pl.ap[:, 0:E])
            if scale is not None:
                tv0 = kb.vec(lambda e, p=pl.ap[:, 0:E]: e.scalar_tensor_tensor(
                    out=s_tq[:, 0:E], in0=p, scalar=float(np.float32(scale)), in1=s_brs[:, 0:E],
                    op0=ALU.mult, op1=ALU.add), waits=[tkm, t8, last])
            else:
                tv0 = kb.vec(lambda e, p=pl.ap[:, 0:E]: e.tensor_copy(out=s_tq[:, 0:E], in_=p),
                             waits=[tkm, last])
            pl.free = tv0
            if bias_sel is not None:
                tv1 = kb.vec(lambda e: e.tensor_tensor(
                    out=s_bb[:, 0:E], in0=s_tq[:, 0:E], in1=bias_sel[:, 0:E], op=ALU.add),
                    waits=[tv0, tok_x0])
            else:
                tv1 = kb.vec(lambda e: e.tensor_copy(out=s_bb[:, 0:E], in_=s_tq[:, 0:E]),
                             waits=[tv0])
            tsel = None
            for r in range(topk):
                tm = kb.vec(lambda e: e.reduce_max(s_m1[:], s_bb[:, 0:E], axis=AX.X),
                            waits=[tv1])
                th = kb.vec(lambda e: e.tensor_scalar(
                    out=s_hit[:, 0:E], in0=s_bb[:, 0:E], scalar1=s_m1[:, 0:1],
                    scalar2=None, op0=ALU.is_ge), waits=[tm])
                if r == 0:
                    tsel = kb.vec(lambda e: e.tensor_copy(out=s_sel[:, 0:E],
                                                          in_=s_hit[:, 0:E]), waits=[th])
                else:
                    tsel = kb.vec(lambda e: e.tensor_tensor(
                        out=s_sel[:, 0:E], in0=s_sel[:, 0:E], in1=s_hit[:, 0:E],
                        op=ALU.add), waits=[th, tsel])
                if r < topk - 1:
                    tv1 = kb.vec(lambda e: e.scalar_tensor_tensor(
                        out=s_bb[:, 0:E], in0=s_hit[:, 0:E], scalar=float(np.float32(-1e30)),
                        in1=s_bb[:, 0:E], op0=ALU.mult, op1=ALU.add), waits=[th])
            ta = kb.vec(lambda e: e.tensor_tensor(out=s_bb[:, 0:E], in0=s_tq[:, 0:E],
                                                  in1=s_sel[:, 0:E], op=ALU.mult),
                        waits=[tsel])
            tb = kb.vec(lambda e: e.tensor_scalar(
                out=s_hit[:, 0:E], in0=s_sel[:, 0:E], scalar1=float(np.float32(1e30)),
                scalar2=float(np.float32(-1e30)), op0=ALU.mult, op1=ALU.add), waits=[tsel, ta])
            tc_ = kb.vec(lambda e: e.tensor_tensor(out=s_bb[:, 0:E], in0=s_bb[:, 0:E],
                                                   in1=s_hit[:, 0:E], op=ALU.add),
                         waits=[tb])
            tn = kb.vec(lambda e: e.reduce_max(s_m2[:], s_bb[:, 0:E], axis=AX.X,
                                               negate=True), waits=[tc_])
            te = kb.sca(lambda e: e.activation(out=s_bb[:, 0:E], in_=s_bb[:, 0:E],
                                               func=AF.Exp, bias=s_m2[:, 0:1]), waits=[tn])
            tsum = kb.vec(lambda e: e.reduce_sum(s_m1[:], s_bb[:, 0:E], axis=AX.X),
                          waits=[te])
            trc = kb.vec(lambda e: e.reciprocal(out=s_m2[:], in_=s_m1[:]), waits=[tsum])
            gdst = s_bb[:, 0:16]
            tg = kb.vec(lambda e: e.tensor_scalar(
                out=s_bb[:, 0:E], in0=s_bb[:, 0:E], scalar1=s_m2[:, 0:1], scalar2=None,
                op0=ALU.mult), waits=[trc, tgz])
            pq2 = psb[7]
            tk = kb.trans(pq2.ap[0:16, 0:128], gdst, s_idf[:], waits=[tg, pq2.free])
            last = kb.vec(lambda e, p=pq2.ap, tc=tc: e.tensor_copy(
                out=s_gT[0:E, tc * 128:(tc + 1) * 128], in_=p[0:E, 0:128]), waits=[tk])
            pq2.free = last
        return last

    def emit_l0ffn(xn_tok, g_tok, x_tok):
        W13HALF = 2 * NF * 128

        def emit_w13_loads(e_, i_, slot, wait_tok):
            if e_ < 16:
                srcs = [(w1_t.tensor, e_ * D * D + i_ * 128),
                        (w3_t.tensor, e_ * D * D + i_ * 128)]
            else:
                srcs = [(sw1_t.tensor, i_ * 128), (sw3_t.tensor, i_ * 128)]
            stream = "wA" if slot == 0 else "wB"
            ssem = kb.sem[stream]
            bound = (ssem, ssem.n)
            tk = None
            for m, (tens, off) in enumerate(srcs):
                ap_in = bass.AP(tens, off, [[D, 128], [128 * D, NF], [1, 128]])
                dst = w13b[:, slot * W13HALF + m * NF * 128: slot * W13HALF + (m + 1) * NF * 128]
                tk = kb.dma("gpsimd", dst, ap_in, stream=stream,
                            waits=[wait_tok, bound] if m == 0 else [])
            return tk

        def emit_w2_load(e_, wait_tok):
            ssem = kb.sem["w2s"]
            bound = (ssem, ssem.n)
            tk = None
            for i_ in range(NF):
                if e_ < 16:
                    src = bass.AP(w2_t.tensor, e_ * D * D + i_ * 128 * D, [[D, 128], [1, D]])
                else:
                    src = bass.AP(sw2_t.tensor, i_ * 128 * D, [[D, 128], [1, D]])
                tk = kb.dma("gpsimd", w2b[:, i_ * D:(i_ + 1) * D], src, stream="w2s",
                            waits=[wait_tok, bound] if i_ == 0 else [])
            return tk

        prev_out_tok = x_tok
        prev_grep_tok = None
        slot_cons = {0: None, 1: None}
        pend = {(0, 0): emit_w13_loads(0, 0, 0, kb.rtok_w13),
                (0, 1): emit_w13_loads(0, 1, 1, kb.rtok_w13)}
        w2_tok = emit_w2_load(0, kb.rtok_w2)
        xupd = x_tok
        for e_ in range(17):
            if e_ < 16:
                gtv = None
                rowtv = None
                for tt in range(NT):
                    pg = psb[4 + tt % 2]
                    pr = psb[6]
                    tkr = kb.mm(pr.ap[0:1, 0:TT], s_idb[0:16, e_:e_ + 1],
                                s_gT[0:16, tt * TT:(tt + 1) * TT], True, True,
                                waits=[g_tok, pr.free])
                    rowtv = kb.vec(lambda e, p=pr.ap: e.tensor_copy(
                        out=s_mean[0:1, :], in_=p[0:1, 0:TT]),
                        waits=[tkr, kb.rowtok])
                    pr.free = rowtv
                    tk = kb.mm(pg.ap, s_or_b[0:1, :], s_mean[0:1, :],
                               True, True, waits=[rowtv, pg.free])
                    kb.rowtok = tk
                    gtv = kb.vec(lambda e, p=pg.ap, tt=tt: e.tensor_scalar(
                        out=s_grep[:, tt * TT:(tt + 1) * TT], in0=p, scalar1=0.5,
                        scalar2=None, op0=ALU.mult), waits=[tk, prev_grep_tok, gtv])
                    pg.free = gtv
                grep_tok = gtv
            else:
                grep_tok = kb.vec(lambda e: e.memset(s_grep[:], 0.5),
                                  waits=[prev_grep_tok])
            act_last = None
            prev_tva = None
            for i_ in range(NF):
                ldt = pend.pop((e_, i_))
                slot = i_ % 2
                nxt = (e_, i_ + 2) if i_ + 2 < NF else (e_ + 1, i_ + 2 - NF)
                h_last = None
                for tt in range(NT):
                    p1 = psb[tt % 2]
                    p3 = psb[2 + tt % 2]
                    sl = w13b[:, slot * W13HALF:(slot + 1) * W13HALF]
                    parts1 = [(sl[:, kc * 128:(kc + 1) * 128],
                               XN[:, kc * T + tt * TT: kc * T + (tt + 1) * TT])
                              for kc in range(NF)]
                    parts3 = [(sl[:, NF * 128 + kc * 128: NF * 128 + (kc + 1) * 128],
                               XN[:, kc * T + tt * TT: kc * T + (tt + 1) * TT])
                              for kc in range(NF)]
                    tk1 = kb.mm_group(p1, parts1, waits=[ldt, xn_tok])
                    tk3 = kb.mm_group(p3, parts3, waits=[ldt, xn_tok])
                    h_last = tk3
                    tsl = kb.sca(lambda e, p=p1.ap: e.activation(out=tmpC[:], in_=p,
                                                                 func=AF.Silu),
                                 waits=[tk1, prev_tva])
                    p1.free = tsl
                    adst = ACT[:, i_ * T + tt * TT: i_ * T + (tt + 1) * TT]
                    tva = kb.vec(lambda e, a=adst, p=p3.ap: e.tensor_tensor(
                        out=a, in0=tmpC[:], in1=p, op=ALU.mult),
                        waits=[tsl, tk3, prev_out_tok])
                    p3.free = tva
                    act_last = tva
                    prev_tva = tva
                slot_cons[slot] = h_last
                if nxt[0] < 17:
                    pend[nxt] = emit_w13_loads(nxt[0], nxt[1], slot, slot_cons[slot])
            out_last = None
            for o in range(NF):
                for tt in range(NT):
                    po = psb[4 + (o * NT + tt) % 2]
                    parts = [(w2b[:, i_ * D + o * 128: i_ * D + (o + 1) * 128],
                              ACT[:, i_ * T + tt * TT: i_ * T + (tt + 1) * TT])
                             for i_ in range(NF)]
                    tko = kb.mm_group(po, parts, waits=[act_last, w2_tok])
                    out_last = tko
                    xsl = X[:, o * T + tt * TT: o * T + (tt + 1) * TT]
                    tvx1 = kb.vec(lambda e, p=po.ap, tt=tt: e.tensor_tensor(
                        out=tmpA[:], in0=p, in1=s_grep[:, tt * TT:(tt + 1) * TT],
                        op=ALU.mult), waits=[tko, grep_tok, xupd])
                    xupd = kb.vec(lambda e, x=xsl: e.tensor_tensor(
                        out=x, in0=x, in1=tmpA[:], op=ALU.add), waits=[tvx1])
                    po.free = xupd
            prev_out_tok = out_last
            prev_grep_tok = grep_tok
            if e_ + 1 < 17:
                w2_tok = emit_w2_load(e_ + 1, out_last)
        for snm in ("wA", "wB", "w2s"):
            ss = kb.sem[snm]
            kb.em("gpsimd", lambda e, h=ss.h, v=ss.n: e.wait_ge(h, v))
        kb.rtok_w13 = xupd
        kb.rtok_w2 = xupd
        return xupd

    def emit_l1ffn(xn_tok, g_tok, x_tok):
        W13HALF = 2 * NF * 128

        def w1_load(e_, i_, slot, wait_tok):
            if e_ < 12:
                na, wt, ncols = 24, We1.tensor, 6 * D
                base = e_ * D * 6 * D
            else:
                na, wt, ncols = 16, Ws1.tensor, 4 * D
                base = 0
            stream = "wA" if slot == 0 else "wB"
            ssem = kb.sem[stream]
            bound = (ssem, ssem.n)
            tk = None
            for m, col in enumerate((i_, i_ + na)):
                ap_in = bass.AP(wt, base + col * 128,
                                [[ncols, 128], [128 * ncols, NF], [1, 128]])
                dst = w13b[:, slot * W13HALF + m * NF * 128: slot * W13HALF + (m + 1) * NF * 128]
                tk = kb.dma("gpsimd", dst, ap_in, stream=stream,
                            waits=[wait_tok, bound] if m == 0 else [])
            return tk

        def w2_load(e_, ig, wait_tok):
            if e_ < 12:
                wt, base = We2.tensor, e_ * 3 * D * D
            else:
                wt, base = Ws2.tensor, 0
            ssem = kb.sem["w2s"]
            bound = (ssem, ssem.n)
            tk = None
            for j in range(NF):
                r = ig * NF + j
                src = bass.AP(wt, base + r * 128 * D, [[D, 128], [1, D]])
                tk = kb.dma("gpsimd", w2b[:, j * D:(j + 1) * D], src, stream="w2s",
                            waits=[wait_tok, bound] if j == 0 else [])
            return tk

        seq = []
        for e_ in range(13):
            na = 24 if e_ < 12 else 16
            for i_ in range(na):
                seq.append((e_, i_))
        pend = {}
        slot_cons = {0: kb.rtok_w13, 1: kb.rtok_w13}
        pend[seq[0]] = w1_load(seq[0][0], seq[0][1], 0, kb.rtok_w13)
        pend[seq[1]] = w1_load(seq[1][0], seq[1][1], 1, kb.rtok_w13)
        seqpos = 2
        w2_tok = w2_load(0, 0, kb.rtok_w2)
        prev_out_tok = x_tok
        prev_grep_tok = None
        xupd = x_tok
        si = 0
        for e_ in range(13):
            na = 24 if e_ < 12 else 16
            if e_ < 12:
                gtv = None
                for tt in range(NT):
                    pg = psb[4 + tt % 2]
                    pr = psb[6]
                    tkr = kb.mm(pr.ap[0:1, 0:TT], s_idb[0:12, e_:e_ + 1],
                                s_gT[0:12, tt * TT:(tt + 1) * TT], True, True,
                                waits=[g_tok, pr.free])
                    rowtv = kb.vec(lambda e, p=pr.ap: e.tensor_copy(
                        out=s_mean[0:1, :], in_=p[0:1, 0:TT]),
                        waits=[tkr, kb.rowtok])
                    pr.free = rowtv
                    tk = kb.mm(pg.ap, s_or_b[0:1, :], s_mean[0:1, :],
                               True, True, waits=[rowtv, pg.free])
                    kb.rowtok = tk
                    gtv = kb.vec(lambda e, p=pg.ap, tt=tt: e.tensor_scalar(
                        out=s_grep[:, tt * TT:(tt + 1) * TT], in0=p, scalar1=0.5,
                        scalar2=None, op0=ALU.mult), waits=[tk, prev_grep_tok, gtv])
                    pg.free = gtv
                grep_tok = gtv
            else:
                grep_tok = kb.vec(lambda e: e.memset(s_grep[:], 0.5),
                                  waits=[prev_grep_tok])
            for ig in range(na // NF):
                act_last = None
                prev_tva = None
                for i8 in range(NF):
                    i_ = ig * NF + i8
                    ldt = pend.pop((e_, i_))
                    slot = si % 2
                    si += 1
                    h_last = None
                    sl = w13b[:, slot * W13HALF:(slot + 1) * W13HALF]
                    if e_ < 12:
                        bcol_a = be1col[:, 0:1] if False else None
                        ca = e_ * 48 + i_
                        cg = e_ * 48 + i_ + na
                        bias_t = s_be1
                    else:
                        ca, cg = i_, i_ + na
                        bias_t = s_bs1
                    for tt in range(NT):
                        p1 = psb[tt % 2]
                        p3 = psb[2 + tt % 2]
                        parts1 = [(sl[:, kc * 128:(kc + 1) * 128],
                                   XN[:, kc * T + tt * TT: kc * T + (tt + 1) * TT])
                                  for kc in range(NF)]
                        parts3 = [(sl[:, NF * 128 + kc * 128: NF * 128 + (kc + 1) * 128],
                                   XN[:, kc * T + tt * TT: kc * T + (tt + 1) * TT])
                                  for kc in range(NF)]
                        tk1 = kb.mm_group(p1, parts1, waits=[ldt, xn_tok])
                        tk3 = kb.mm_group(p3, parts3, waits=[ldt, xn_tok])
                        h_last = tk3
                        tsl = kb.sca(lambda e, p=p3.ap, cg=cg, b=bias_t: e.activation(
                            out=tmpC[:], in_=p, func=AF.Gelu, bias=b[:, cg:cg + 1]),
                            waits=[tk3, prev_tva, tok_init_d2])
                        p3.free = tsl
                        adst = ACT[:, i8 * T + tt * TT: i8 * T + (tt + 1) * TT]
                        tva = kb.vec(lambda e, a=adst, p=p1.ap, ca=ca, b=bias_t:
                                     e.scalar_tensor_tensor(
                                         out=a, in0=p, scalar=b[:, ca:ca + 1],
                                         in1=tmpC[:], op0=ALU.add, op1=ALU.mult),
                            waits=[tsl, tk1, prev_out_tok, tok_init_d2])
                        p1.free = tva
                        act_last = tva
                        prev_tva = tva
                    slot_cons[slot] = h_last
                    if seqpos < len(seq):
                        nxt = seq[seqpos]
                        seqpos += 1
                        pend[nxt] = w1_load(nxt[0], nxt[1], slot, slot_cons[slot])
                out_last = None
                for o in range(NF):
                    for tt in range(NT):
                        po = psb[4 + (o * NT + tt) % 2]
                        parts = [(w2b[:, j * D + o * 128: j * D + (o + 1) * 128],
                                  ACT[:, j * T + tt * TT: j * T + (tt + 1) * TT])
                                 for j in range(NF)]
                        tko = kb.mm_group(po, parts, waits=[act_last, w2_tok])
                        out_last = tko
                        if e_ < 12:
                            co = e_ * NF + o
                            b2 = s_be2
                        else:
                            co = o
                            b2 = s_bs2
                        xsl = X[:, o * T + tt * TT: o * T + (tt + 1) * TT]
                        if ig == 0:
                            tvx1 = kb.vec(lambda e, p=po.ap, tt=tt, co=co, b2=b2:
                                          e.scalar_tensor_tensor(
                                              out=tmpA[:], in0=p, scalar=b2[:, co:co + 1],
                                              in1=s_grep[:, tt * TT:(tt + 1) * TT],
                                              op0=ALU.add, op1=ALU.mult),
                                          waits=[tko, grep_tok, xupd])
                        else:
                            tvx1 = kb.vec(lambda e, p=po.ap, tt=tt: e.tensor_tensor(
                                out=tmpA[:], in0=p,
                                in1=s_grep[:, tt * TT:(tt + 1) * TT], op=ALU.mult),
                                waits=[tko, grep_tok, xupd])
                        xupd = kb.vec(lambda e, x=xsl: e.tensor_tensor(
                            out=x, in0=x, in1=tmpA[:], op=ALU.add), waits=[tvx1])
                        po.free = xupd
                prev_out_tok = out_last
                if not (e_ == 12 and ig == na // NF - 1):
                    ne, nig = (e_, ig + 1) if ig + 1 < na // NF else (e_ + 1, 0)
                    w2_tok = w2_load(ne, nig, out_last)
            prev_grep_tok = grep_tok
        for snm in ("wA", "wB", "w2s"):
            ss = kb.sem[snm]
            kb.em("gpsimd", lambda e, h=ss.h, v=ss.n: e.wait_ge(h, v))
        kb.rtok_w13 = xupd
        kb.rtok_w2 = xupd
        return xupd

    def emit_l0attn(xn_tok, x_tok):
        rr = [0]

        def pgrab():
            pt = psb[rr[0] % 6]
            rr[0] += 1
            return pt

        def aw_load(i_, slot, wait_tok):
            stream = "wA" if slot == 0 else "wB"
            ssem = kb.sem[stream]
            bound = (ssem, ssem.n)
            ap_in = bass.AP(aW_t.tensor, i_ * 128, [[D, 128], [128 * D, NF], [1, 128]])
            dst = w13b[:, slot * 2 * NF * 128: slot * 2 * NF * 128 + NF * 128]
            return kb.dma("gpsimd", dst, ap_in, stream=stream, waits=[wait_tok, bound])

        slot_cons = {0: kb.rtok_w13, 1: kb.rtok_w13}
        pend = {0: aw_load(0, 0, kb.rtok_w13), 1: aw_load(1, 1, kb.rtok_w13)}
        st_last = None
        for o in range(NF):
            slot = o % 2
            ldt = pend.pop(o)
            sl = w13b[:, slot * 2 * NF * 128: slot * 2 * NF * 128 + NF * 128]
            tko = None
            for tt in range(NT):
                po = pgrab()
                parts = [(sl[:, kc * 128:(kc + 1) * 128],
                          XN[:, kc * T + tt * TT: kc * T + (tt + 1) * TT])
                         for kc in range(NF)]
                tko = kb.mm_group(po, parts, waits=[ldt, xn_tok])
                st_last = kb.vec(lambda e, p=po.ap, o=o, tt=tt: e.tensor_copy(
                    out=ACT[:, o * T + tt * TT: o * T + (tt + 1) * TT], in_=p),
                    waits=[tko, x_tok])
                po.free = st_last
            slot_cons[slot] = tko
            if o + 2 < NF:
                pend[o + 2] = aw_load(o + 2, slot, slot_cons[slot])
        static_tok = st_last

        idtok = kb.vec(lambda e: e.tensor_copy(out=s_id2[:, 0:128], in_=s_idf[:]),
                       waits=[slot_cons[0], slot_cons[1], kb.rowtok, kb.rtok_w2])
        idtok = kb.vec(lambda e: e.tensor_copy(out=s_id2[:, 128:256], in_=s_idf[:]),
                       waits=[idtok])
        bt = idtok
        for j in range(8):
            for hh, d in ((0, 0), (1, 64)):
                h = 2 * j + hh
                bt = kb.vec(lambda e, j=j, h=h, d=d: e.tensor_copy(
                    out=s_betp[d:d + 64, j:j + 1], in_=s_betac[d:d + 64, h:h + 1]),
                    waits=[bt])
                bt = kb.vec(lambda e, j=j, h=h, d=d: e.tensor_copy(
                    out=s_nbetp[d:d + 64, j:j + 1], in_=s_nbetac[d:d + 64, h:h + 1]),
                    waits=[bt])
        mask_tok = None
        for h in range(H):
            pq = pgrab()
            t1 = kb.sca(lambda e, h=h: e.activation(out=tmpA[:, 0:128], in_=s_iju[:],
                                                    func=AF.Exp, scale=s_loglc[:, h:h + 1]),
                        waits=[bt, mask_tok])
            t2 = kb.vec(lambda e, h=h: e.scalar_tensor_tensor(
                out=s_Dup[:, h * C:(h + 1) * C], in0=tmpA[:, 0:128],
                scalar=s_betac[:, h:h + 1], in1=s_mu[:], op0=ALU.mult, op1=ALU.mult),
                waits=[t1])
            t3 = kb.sca(lambda e, h=h: e.activation(out=tmpA[:, 128:256], in_=s_ijl[:],
                                                    func=AF.Exp, scale=s_loglc[:, h:h + 1]),
                        waits=[t2])
            t4 = kb.vec(lambda e, h=h: e.scalar_tensor_tensor(
                out=s_Dlo[:, h * C:(h + 1) * C], in0=tmpA[:, 128:256],
                scalar=s_betac[:, h:h + 1], in1=s_ml[:], op0=ALU.mult, op1=ALU.mult),
                waits=[t3])
            j, d = h // 2, 64 * (h % 2)
            t5 = kb.vec(lambda e, j=j, d=d, h=h: e.tensor_scalar(
                out=s_lamCI[d:d + 64, j * DH:(j + 1) * DH],
                in0=s_idf[d:d + 64, d:d + 64],
                scalar1=s_lamCc[d:d + 64, h:h + 1], scalar2=None, op0=ALU.mult),
                waits=[t4])
            pq2_ = pgrab()
            tkx = kb.mm(pq2_.ap[0:1, 0:128], s_idf[0:16, h:h + 1], s_powlT[0:16, :],
                        True, True, waits=[tplT, pq2_.free])
            trow = kb.vec(lambda e, p=pq2_.ap: e.tensor_copy(
                out=tmpA[0:1, 256:384], in_=p[0:1, 0:128]), waits=[tkx, t5])
            pq2_.free = trow
            tkp = kb.mm(pq.ap[d:d + 64, 0:128], s_or_f[0:1, 0:64], tmpA[0:1, 256:384],
                        True, True, waits=[trow, pq.free])
            mask_tok = kb.vec(lambda e, p=pq.ap, j=j, d=d: e.tensor_copy(
                out=s_prep[d:d + 64, j * C:(j + 1) * C], in_=p[d:d + 64, 0:128]),
                waits=[tkp])
            pq.free = mask_tok
        z_tok = kb.vec(lambda e: e.memset(s_Sexp[:], 0.0), waits=[mask_tok])

        def emit_kc(c, extra_waits):
            prs = [psb[6], psb[7]]
            mm_toks = [None] * NF
            sq_prev = None
            for f in range(NF):
                pr = prs[f // 4]
                xnf = XN[:, f * T + c * C: f * T + (c + 1) * C]
                sq = kb.vec(lambda e, x=xnf: e.tensor_tensor(
                    out=tmpA[:, 256:384], in0=x, in1=x, op=ALU.mult),
                    waits=list(extra_waits) + [mm_toks[f - 1] if f else None, sq_prev])
                sq_prev = sq
                w = [sq]
                if f % 4 == 0:
                    w.append(pr.free)
                mm_toks[f] = kb.mm(pr.ap[0:2, (f % 4) * 128:(f % 4) * 128 + 128],
                                   s_h2[:], tmpA[:, 256:384], True, True, waits=w,
                                   force_inc=True)
            rms_toks = []
            for half in range(2):
                pr = prs[half]
                ts0 = kb.sca(lambda e, p=pr.ap, half=half: e.activation(
                    out=p[0:2, 0:TT], in_=p[0:2, 0:TT], func=AF.Sqrt,
                    bias=float(np.finfo(np.float32).eps), scale=float(1.0 / 64)),
                    waits=[mm_toks[half * 4 + 3], mm_toks[NF - 1]]
                          + ([rms_toks[0]] if half else []))

                def _rc(e, half=half, p=pr.ap):
                    with e.bass.allow_low_precision(reason="rms rows bf16"):
                        return e.reciprocal(out=s_rmsr[0:2, half * TT:(half + 1) * TT],
                                            in_=p[0:2, 0:TT])
                trc = kb.vec(_rc, waits=[ts0])
                pr.free = trc
                rms_toks.append(trc)
            k_tok = None
            for f in range(NF):
                pq = pgrab()
                tke = kb.mm(pq.ap[:, 0:128], s_p2[:],
                            s_rmsr[0:2, f * 128:(f + 1) * 128], True, True,
                            waits=[rms_toks[f // 4], pq.free])
                xnf = XN[:, f * T + c * C: f * T + (c + 1) * C]
                tv = kb.vec(lambda e, p=pq.ap, x=xnf: e.tensor_tensor(
                    out=tmpA[:, 256:384], in0=x, in1=p[:, 0:128], op=ALU.mult),
                    waits=[tke, sq_prev, k_tok])
                k_tok = kb.vec(lambda e, f=f: e.tensor_scalar(
                    out=s_kc[:, f * C:(f + 1) * C], in0=tmpA[:, 256:384],
                    scalar1=s_knw[:, 0:1], scalar2=None, op0=ALU.mult), waits=[tv])
                pq.free = k_tok
            return k_tok

        def solve_pair(c, j, kc_tok, prev):
            h0, h1 = 2 * j, 2 * j + 1
            kf = s_kc[:, j * C:(j + 1) * C]
            p1a, p1b = pgrab(), pgrab()
            kb.trans(p1a.bf[0:128, 0:64], kf[0:64, :], s_idb[0:64, 0:64],
                     waits=[kc_tok, p1a.free, prev])
            tk1b = kb.trans(p1b.bf[0:128, 0:64], kf[64:128, :],
                            s_idb[64:128, 64:128], waits=[p1b.free])
            tv1 = kb.vec(lambda e, p=p1a.bf: e.tensor_copy(out=pKt[:, 0:64],
                                                           in_=p[:, 0:64]),
                         waits=[tk1b, prev])
            tv1b = kb.vec(lambda e, p=p1b.bf: e.tensor_copy(out=pKt[:, 64:128],
                                                            in_=p[:, 0:64]),
                          waits=[tv1])
            p1a.free = tv1
            p1b.free = tv1b
            p2a, p2b = pgrab(), pgrab()
            stat0 = ACT[0:64, j * T + c * C: j * T + (c + 1) * C]
            stat1 = ACT[64:128, j * T + c * C: j * T + (c + 1) * C]
            kb.trans(p2a.bf[0:128, 0:64], stat0, s_idb[0:64, 0:64],
                     waits=[static_tok, p2a.free])
            tk2b = kb.trans(p2b.bf[0:128, 0:64], stat1, s_idb[64:128, 64:128],
                            waits=[p2b.free])
            tv2 = kb.vec(lambda e, p=p2a.bf: e.tensor_scalar(
                out=pXa[:, 0:64], in0=p[0:128, 0:64], scalar1=-1.0, scalar2=None,
                op0=ALU.mult), waits=[tk2b, prev])
            tv2b = kb.vec(lambda e, p=p2b.bf: e.tensor_scalar(
                out=pXa[:, 128:192], in0=p[0:128, 0:64], scalar1=-1.0, scalar2=None,
                op0=ALU.mult), waits=[tv2])
            p2a.free = tv2
            p2b.free = tv2b
            tv3 = kb.vec(lambda e, h=h0: e.tensor_scalar(
                out=pXa[:, 64:128], in0=pKt[:, 0:64], scalar1=s_powl[:, h:h + 1],
                scalar2=None, op0=ALU.mult), waits=[tv1])
            tv3b = kb.vec(lambda e, h=h1: e.tensor_scalar(
                out=pXa[:, 192:256], in0=pKt[:, 64:128], scalar1=s_powl[:, h:h + 1],
                scalar2=None, op0=ALU.mult), waits=[tv3, tv1b])
            tv4 = kb.vec(lambda e, h=h0: e.tensor_scalar(
                out=pKd[:, 0:64], in0=pKt[:, 0:64], scalar1=s_powr[:, h:h + 1],
                scalar2=None, op0=ALU.mult), waits=[tv1, prev])
            tv4b = kb.vec(lambda e, h=h1: e.tensor_scalar(
                out=pKd[:, 64:128], in0=pKt[:, 64:128], scalar1=s_powr[:, h:h + 1],
                scalar2=None, op0=ALU.mult), waits=[tv4, tv1b])
            p3a, p3b = pgrab(), pgrab()
            kb.mm(p3a.ap[:, 0:128], kf[0:64, :], kf[0:64, :], True, True,
                  waits=[kc_tok, p3a.free, prev])
            tk3 = kb.mm(p3b.ap[:, 0:128], kf[64:128, :], kf[64:128, :], True, True,
                        waits=[p3b.free])
            tvGa = kb.vec(lambda e, p=p3a.ap: e.tensor_copy(out=pG[:, 0:128],
                                                            in_=p[:, 0:128]),
                          waits=[tk3, prev])
            tvG = kb.vec(lambda e, p=p3b.ap: e.tensor_copy(out=pG[:, 128:256],
                                                           in_=p[:, 0:128]),
                         waits=[tvGa])
            p3a.free = tvGa
            p3b.free = tvG
            dup = s_Dup[:, h0 * C:(h0 + 2) * C]
            dlo = s_Dlo[:, h0 * C:(h0 + 2) * C]
            tvN = kb.vec(lambda e, dlo=dlo: e.tensor_tensor(
                out=pNp[:], in0=pG[:], in1=dlo, op=ALU.mult), waits=[tvG, prev])
            tvM = kb.vec(lambda e, dup=dup: e.tensor_tensor(
                out=pM1[:], in0=pG[:], in1=dup, op=ALU.mult), waits=[tvG])
            curM, curM_t = pM1, tvM
            curN, curN_t = pNp, tvN
            mp_toks = {0: tvM}
            for lv in range(1, 7):
                pma, pmb = pgrab(), pgrab()
                kb.mm(pma.ap[:, 0:128], curN[:, 0:128], curM[:, 0:128], True, True,
                      waits=[curM_t, curN_t, pma.free])
                tkm = kb.mm(pmb.ap[:, 0:128], curN[:, 128:256], curM[:, 128:256],
                            True, True, waits=[pmb.free])
                tvma = kb.vec(lambda e, p=pma.ap, lv=lv: e.tensor_copy(
                    out=pMp[lv][:, 0:128], in_=p[:, 0:128]), waits=[tkm])
                tvm = kb.vec(lambda e, p=pmb.ap, lv=lv: e.tensor_copy(
                    out=pMp[lv][:, 128:256], in_=p[:, 0:128]), waits=[tvma])
                pma.free = tvma
                pmb.free = tvm
                mp_toks[lv] = tvm
                if lv < 6:
                    pna, pnb = pgrab(), pgrab()
                    kb.mm(pna.ap[:, 0:128], curM[:, 0:128], curN[:, 0:128], True,
                          True, waits=[curM_t, curN_t, pna.free])
                    tkn = kb.mm(pnb.ap[:, 0:128], curM[:, 128:256], curN[:, 128:256],
                                True, True, waits=[pnb.free])
                    tvna = kb.vec(lambda e, p=pna.ap: e.tensor_copy(
                        out=pNp[:, 0:128], in_=p[:, 0:128]),
                        waits=[tkn, tvm, tvN])
                    tvn = kb.vec(lambda e, p=pnb.ap: e.tensor_copy(
                        out=pNp[:, 128:256], in_=p[:, 0:128]), waits=[tvna])
                    pna.free = tvna
                    pnb.free = tvn
                    curN, curN_t = pNp, tvn
                curM, curM_t = pMp[lv], tvm
            xcur, xoth = pXa, pXb
            xa_tok, xb_tok = tv3, tv3b
            for lv in (6, 5, 4, 3, 2, 1, 0):
                Mp = pM1 if lv == 0 else pMp[lv]
                paa, pab = pgrab(), pgrab()
                kb.mm(paa.ap[:, 0:128], Mp[:, 0:128], xcur[:, 0:128], True, True,
                      waits=[paa.free, mp_toks[lv], xa_tok])
                tka = kb.mm(pab.ap[:, 0:128], Mp[:, 128:256], xcur[:, 128:256],
                            True, True, waits=[pab.free, xb_tok])
                op = ALU.subtract if lv == 0 else ALU.add
                tva0 = kb.vec(lambda e, p=paa.ap, xc=xcur, xo=xoth, op=op:
                              e.tensor_tensor(out=xo[:, 0:128], in0=xc[:, 0:128],
                                              in1=p[:, 0:128], op=op), waits=[tka])
                tva1 = kb.vec(lambda e, p=pab.ap, xc=xcur, xo=xoth, op=op:
                              e.tensor_tensor(out=xo[:, 128:256], in0=xc[:, 128:256],
                                              in1=p[:, 0:128], op=op), waits=[tva0])
                paa.free = tva0
                pab.free = tva1
                xcur, xoth = xoth, xcur
                xa_tok, xb_tok = tva0, tva1
            return {"U": xcur, "U_t": xb_tok, "kf": kf, "kf_t": kc_tok,
                    "kd_t": tv4b}

        def pair_AB(j, sol, spill_tok):
            U, U_t = sol["U"], sol["U_t"]
            pa = pgrab()
            pbk = pgrab()
            for hh, d in ((0, 0), (1, 64)):
                u2 = U[:, hh * 128 + 64: hh * 128 + 128]
                u1 = U[:, hh * 128: hh * 128 + 64]
                kd = pKd[:, hh * 64:(hh + 1) * 64]
                kb.mm(pa.ap[d:d + 64, 0:64], u2, kd, True, True,
                      waits=[U_t, sol["kd_t"], pa.free] if hh == 0 else [])
                kb.mm(pbk.ap[d:d + 64, 0:64], kd, u1, True, True,
                      waits=[U_t, sol["kd_t"], pbk.free] if hh == 0 else [])
            tka = (kb.sem["t"], kb.sem["t"].n)
            tA = kb.vec(lambda e, p=pa.ap, j=j: e.scalar_tensor_tensor(
                out=s_At[:, j * DH:(j + 1) * DH], in0=p[0:128, 0:64],
                scalar=s_nbetp[:, j:j + 1],
                in1=s_lamCI[:, j * DH:(j + 1) * DH],
                op0=ALU.mult, op1=ALU.add), waits=[tka, spill_tok])
            pa.free = tA
            tB = kb.vec(lambda e, p=pbk.ap, j=j: e.tensor_scalar(
                out=s_Bt[:, j * DH:(j + 1) * DH], in0=p[0:128, 0:64],
                scalar1=s_betp[:, j:j + 1],
                scalar2=None, op0=ALU.mult), waits=[tka, tA])
            pbk.free = tB
            return tA, tB

        sexp_toks = {j: z_tok for j in range(8)}
        prev_any = z_tok
        sp_tok = None
        for c in range(NTC):
            kc_tok = emit_kc(c, [xn_tok, prev_any])
            ab_toks = []
            u_toks = []
            for j in range(8):
                sol = solve_pair(c, j, kc_tok, prev_any)
                tA, tB = pair_AB(j, sol, sp_tok)
                ab_toks += [tA, tB]
                tu = kb.vec(lambda e, U=sol["U"], j=j: e.tensor_copy(
                    out=s_usl[:, j * 256:(j + 1) * 256], in_=U[:]),
                    waits=[sol["U_t"], sp_tok])
                u_toks.append(tu)
                pc_ = pgrab()
                for hh, d in ((0, 0), (1, 64)):
                    ssl = s_Sexp[d:d + 64, j * DH:(j + 1) * DH]
                    kb.mm(pc_.ap[d:d + 64, 0:64], s_At[d:d + 64, j * DH:(j + 1) * DH],
                          ssl, True, True,
                          waits=[tA, sexp_toks[j], pc_.free] if hh == 0 else [])
                tks = (kb.sem["t"], kb.sem["t"].n)
                tvs = kb.vec(lambda e, p=pc_.ap, j=j: e.tensor_tensor(
                    out=s_Sexp[:, j * DH:(j + 1) * DH], in0=p[0:128, 0:64],
                    in1=s_Bt[:, j * DH:(j + 1) * DH], op=ALU.add),
                    waits=[tks, tB])
                pc_.free = tvs
                sexp_toks[j] = tvs
                prev_any = tvs
            spsem = kb.sem["sp"]
            kb.dma("gpsimd", u_sp.ap()[c * 128:(c + 1) * 128, :], s_usl[:],
                   waits=u_toks + [(spsem, spsem.n)], stream="sp")
            kb.dma("gpsimd", a_sp.ap()[c * 128:(c + 1) * 128, :], s_At[:],
                   waits=ab_toks, stream="sp")
            sp_tok = kb.dma("gpsimd", b_sp.ap()[c * 128:(c + 1) * 128, :], s_Bt[:],
                            waits=[], stream="sp")

        all_exp = [sexp_toks[j] for j in range(8)]
        tex = kb.dma("gpsimd", cc_s_in.ap(), s_Sexp[:], waits=all_exp, stream="dbg")
        tcc = kb.em("gpsimd", lambda e: e.collective_compute(
            "AllGather", ALU.bypass, replica_groups=groups,
            ins=[cc_s_in.ap().opt()], outs=[cc_s_out.ap().opt()]),
            waits=[tex], inc=sem["cc"])
        tld = kb.dma("gpsimd", s_Sexp[:], cc_s_out.ap()[0:128, :], waits=[tcc],
                     stream="dbg")
        tS0 = kb.vec(lambda e: e.tensor_scalar(
            out=s_S[:], in0=s_Sexp[:], scalar1=s_odd[:, 0:1], scalar2=None,
            op0=ALU.mult), waits=[tld])

        s_toks = {j: tS0 for j in range(8)}
        xupd = x_tok
        prev_any = tS0
        y_prev = tS0
        adv = tS0
        for c in range(NTC):
            spsem = kb.sem["sp"]
            kb.dma("gpsimd", s_usl[:], u_sp.ap()[c * 128:(c + 1) * 128, :],
                   waits=[adv, sp_tok, (spsem, spsem.n)], stream="sp")
            kb.dma("gpsimd", s_At[:], a_sp.ap()[c * 128:(c + 1) * 128, :],
                   waits=[], stream="sp")
            uld_tok = kb.dma("gpsimd", s_Bt[:], b_sp.ap()[c * 128:(c + 1) * 128, :],
                             waits=[], stream="sp")
            kc_tok2 = emit_kc(c, [xn_tok, prev_any])
            y_toks = []
            for j in range(8):
                h0 = 2 * j
                kf = s_kc[:, j * C:(j + 1) * C]
                p3a, p3b = pgrab(), pgrab()
                kb.mm(p3a.ap[:, 0:128], kf[0:64, :], kf[0:64, :], True, True,
                      waits=[kc_tok2, p3a.free, prev_any])
                tk3 = kb.mm(p3b.ap[:, 0:128], kf[64:128, :], kf[64:128, :], True,
                            True, waits=[p3b.free])
                tvGa = kb.vec(lambda e, p=p3a.ap: e.tensor_copy(
                    out=pG[:, 0:128], in_=p[:, 0:128]), waits=[tk3, prev_any])
                tvG = kb.vec(lambda e, p=p3b.ap: e.tensor_copy(
                    out=pG[:, 128:256], in_=p[:, 0:128]), waits=[tvGa])
                p3a.free = tvGa
                p3b.free = tvG
                dup = s_Dup[:, h0 * C:(h0 + 2) * C]
                tvM = kb.vec(lambda e, dup=dup: e.tensor_tensor(
                    out=pM1[:], in0=pG[:], in1=dup, op=ALU.mult),
                    waits=[tvG, prev_any])
                tvI = kb.vec(lambda e: e.tensor_tensor(out=pTT[:], in0=pG[:],
                                                       in1=s_id2[:], op=ALU.mult),
                             waits=[tvG])
                tvFa = kb.vec(lambda e, h=h0: e.scalar_tensor_tensor(
                    out=pFG[:, 0:128], in0=pTT[:, 0:128], scalar=s_betac[:, h:h + 1],
                    in1=pM1[:, 0:128], op0=ALU.mult, op1=ALU.add), waits=[tvI, tvM])
                tvF = kb.vec(lambda e, h=h0 + 1: e.scalar_tensor_tensor(
                    out=pFG[:, 128:256], in0=pTT[:, 128:256],
                    scalar=s_betac[:, h:h + 1],
                    in1=pM1[:, 128:256], op0=ALU.mult, op1=ALU.add), waits=[tvFa])
                tvq = kb.vec(lambda e, j=j, kf=kf: e.tensor_tensor(
                    out=pTT[:, 0:128], in0=kf, in1=s_prep[:, j * C:(j + 1) * C],
                    op=ALU.mult), waits=[tvF])
                pp = pgrab()
                for hh, d in ((0, 0), (1, 64)):
                    u2 = s_usl[:, j * 256 + hh * 128 + 64: j * 256 + hh * 128 + 128]
                    kb.mm(pp.ap[d:d + 64, 0:128], u2, pFG[:, hh * 128:(hh + 1) * 128],
                          True, True,
                          waits=[uld_tok, tvF, pp.free] if hh == 0 else [])
                tkp = (kb.sem["t"], kb.sem["t"].n)
                tvp = kb.vec(lambda e, p=pp.ap: e.tensor_tensor(
                    out=pPt[:], in0=pTT[:, 0:128], in1=p[0:128, 0:128],
                    op=ALU.subtract), waits=[tkp, tvq])
                pp.free = tvp
                po1 = pgrab()
                for hh, d in ((0, 0), (1, 64)):
                    u1 = s_usl[:, j * 256 + hh * 128: j * 256 + hh * 128 + 64]
                    kb.mm(po1.ap[d:d + 64, 0:128], u1, pFG[:, hh * 128:(hh + 1) * 128],
                          True, True,
                          waits=[uld_tok, tvF, po1.free] if hh == 0 else [])
                tko1 = (kb.sem["t"], kb.sem["t"].n)
                po2 = pgrab()
                for hh, d in ((0, 0), (1, 64)):
                    ssl = s_S[d:d + 64, j * DH:(j + 1) * DH]
                    kb.mm(po2.ap[d:d + 64, 0:128], ssl, pPt[d:d + 64, 0:128],
                          True, True,
                          waits=[tvp, s_toks[j], po2.free] if hh == 0 else [])
                tko2 = (kb.sem["t"], kb.sem["t"].n)
                ysl = s_yc[:, j * C:(j + 1) * C]
                stat = ACT[:, j * T + c * C: j * T + (c + 1) * C]
                ty0 = kb.sca(lambda e, p1=po1.ap, y=ysl: e.copy(
                    out=y, in_=p1[0:128, 0:128]), waits=[tko1, y_prev])
                ty1 = kb.vec(lambda e, p2=po2.ap, y=ysl: e.tensor_tensor(
                    out=y, in0=y, in1=p2[0:128, 0:128], op=ALU.add),
                    waits=[tko2, ty0])
                ty2 = kb.vec(lambda e, y=ysl, st=stat: e.tensor_tensor(
                    out=y, in0=y, in1=st, op=ALU.add), waits=[ty1, static_tok])
                po1.free = ty2
                po2.free = ty2
                y_toks.append(ty2)
                ps_ = pgrab()
                for hh, d in ((0, 0), (1, 64)):
                    ssl = s_S[d:d + 64, j * DH:(j + 1) * DH]
                    kb.mm(ps_.ap[d:d + 64, 0:64], s_At[d:d + 64, j * DH:(j + 1) * DH],
                          ssl, True, True,
                          waits=[uld_tok, tko2, ps_.free] if hh == 0 else [])
                tks = (kb.sem["t"], kb.sem["t"].n)
                tvs = kb.vec(lambda e, p=ps_.ap, j=j: e.tensor_tensor(
                    out=s_S[:, j * DH:(j + 1) * DH], in0=p[0:128, 0:64],
                    in1=s_Bt[:, j * DH:(j + 1) * DH], op=ALU.add),
                    waits=[tks, uld_tok])
                ps_.free = tvs
                s_toks[j] = tvs
                prev_any = tvs
            adv = prev_any
            pS2, pQ2 = psb[6], psb[7]
            g1 = [(s_oc_f[:, 0:1], s_yc[:, f * C:(f + 1) * C]) for f in range(NF)]
            tstat1 = kb.mm_group(pS2, g1, waits=y_toks, out=pS2.ap[0:1, 0:128])
            mmq = None
            for f in range(NF):
                sq = kb.vec(lambda e, f=f: e.tensor_tensor(
                    out=tmpA[:, 256:384], in0=s_yc[:, f * C:(f + 1) * C],
                    in1=s_yc[:, f * C:(f + 1) * C], op=ALU.mult),
                    waits=y_toks + [mmq])
                w = [sq]
                if f == 0:
                    w.append(pQ2.free)
                mmq = kb.mm(pQ2.ap[0:1, 0:128], s_oc_f[:, 0:1], tmpA[:, 256:384],
                            start=(f == 0), stop=(f == NF - 1), waits=w,
                            force_inc=True)
            tm1 = kb.vec(lambda e, p=pS2.ap: e.tensor_scalar(
                out=tmpA[0:1, 384:512], in0=p[0:1, 0:128],
                scalar1=float(1.0 / D), scalar2=None, op0=ALU.mult),
                waits=[tstat1, mmq])
            pS2.free = tm1
            tm2 = kb.vec(lambda e: e.tensor_tensor(
                out=tmpA[0:1, 0:128], in0=tmpA[0:1, 384:512], in1=tmpA[0:1, 384:512],
                op=ALU.mult), waits=[tm1])
            tm3 = kb.vec(lambda e, p=pQ2.ap: e.scalar_tensor_tensor(
                out=tmpA[0:1, 0:128], in0=p[0:1, 0:128], scalar=float(1.0 / D),
                in1=tmpA[0:1, 0:128], op0=ALU.mult, op1=ALU.subtract),
                waits=[tm2])
            pQ2.free = tm3
            tm4 = kb.sca(lambda e: e.activation(out=tmpA[0:1, 0:128],
                                                in_=tmpA[0:1, 0:128], func=AF.Sqrt,
                                                bias=1e-5), waits=[tm3])

            def _rc2(e):
                with e.bass.allow_low_precision(reason="ln rows"):
                    return e.reciprocal(out=tmpA[0:1, 128:256], in_=tmpA[0:1, 0:128])
            tm5 = kb.vec(_rc2, waits=[tm4])
            tm6 = kb.vec(lambda e: e.tensor_tensor(
                out=tmpA[0:1, 384:512], in0=tmpA[0:1, 384:512], in1=tmpA[0:1, 128:256],
                op=ALU.mult), waits=[tm5])
            prp = pgrab()
            tkr1 = kb.mm(prp.ap[:, 0:128], s_or_f[0:1, :], tmpA[0:1, 128:256],
                         True, True, waits=[tm5, prp.free])
            prm = pgrab()
            tkr2 = kb.mm(prm.ap[:, 0:128], s_or_f[0:1, :], tmpA[0:1, 384:512],
                         True, True, waits=[tm6, prm.free])
            lastf = None
            for f in range(NF):
                ta = kb.vec(lambda e, f=f, p=prp.ap: e.tensor_tensor(
                    out=tmpA[:, 256:384], in0=s_yc[:, f * C:(f + 1) * C],
                    in1=p[:, 0:128], op=ALU.mult), waits=[tkr1, lastf, mmq])
                tb = kb.vec(lambda e, p=prm.ap: e.tensor_tensor(
                    out=tmpA[:, 256:384], in0=tmpA[:, 256:384], in1=p[:, 0:128],
                    op=ALU.subtract), waits=[tkr2, ta])
                tc_ = kb.sca(lambda e, f=f: e.activation(
                    out=tmpA[:, 384:512], in_=tmpA[:, 256:384], func=AF.Identity,
                    bias=s_onb[:, f:f + 1], scale=s_onw[:, f:f + 1]), waits=[tb])
                xsl = X[:, f * T + c * C: f * T + (c + 1) * C]
                lastf = kb.vec(lambda e, x=xsl: e.tensor_tensor(
                    out=x, in0=x, in1=tmpA[:, 384:512], op=ALU.add),
                    waits=[tc_, xupd])
            prp.free = lastf
            prm.free = lastf
            xupd = lastf
            y_prev = lastf
            prev_any = lastf
        kb.rtok_w13 = xupd
        kb.rtok_w2 = xupd
        return xupd

    def emit_l1attn(xn_tok, x_tok):
        rr = [0]

        def pgrab():
            pt = psb[rr[0] % 6]
            rr[0] += 1
            return pt

        W13HALF = 2 * NF * 128

        def col_load(wt, base, ncols, i_, slot, wait_tok):
            stream = "wA" if slot == 0 else "wB"
            ssem = kb.sem[stream]
            bound = (ssem, ssem.n)
            ap_in = bass.AP(wt, base + i_ * 128, [[ncols, 128], [128 * ncols, NF], [1, 128]])
            dst = w13b[:, slot * NF * 128:(slot + 1) * NF * 128]
            return kb.dma("gpsimd", dst, ap_in, stream=stream, waits=[wait_tok, bound])

        fm_jobs = []
        for o in range(NF):
            fm_jobs.append(("q", o))
        for o in range(NF):
            fm_jobs.append(("u", o))
        slot_cons = {0: kb.rtok_w13, 1: kb.rtok_w13}

        def fm_src(kind, o):
            if kind == "q":
                return Wqkvu_t.tensor, o * 128, 4 * D
            return Wqkvu_t.tensor, 3 * D + o * 128, 4 * D

        pend = {}
        for j in range(2):
            kind, o = fm_jobs[j]
            wt, base, ncols = fm_src(kind, o)
            pend[fm_jobs[j]] = col_load(wt, base - o * 128 + o * 128, ncols, 0, j, kb.rtok_w13) \
                if False else col_load(wt, base // 1 - o * 128, ncols, o, j, kb.rtok_w13)
        fm_last = None
        for idx, (kind, o) in enumerate(fm_jobs):
            slot = idx % 2
            ldt = pend.pop((kind, o))
            sl = w13b[:, slot * NF * 128:(slot + 1) * NF * 128]
            tko = None
            for tt in range(NT):
                po = pgrab()
                parts = [(sl[:, kc * 128:(kc + 1) * 128],
                          XN[:, kc * T + tt * TT: kc * T + (tt + 1) * TT])
                         for kc in range(NF)]
                tko = kb.mm_group(po, parts, waits=[ldt, xn_tok], skip_gc=True)
                if kind == "q":
                    fm_last = kb.sca(lambda e, p=po.ap, o=o, tt=tt: e.activation(
                        out=SCR[:, o * T + tt * TT: o * T + (tt + 1) * TT], in_=p,
                        func=AF.Silu, bias=bq_colS[:, o:o + 1]),
                        waits=[tko, x_tok, fm_last])
                else:
                    fm_last = kb.sca(lambda e, p=po.ap, o=o, tt=tt: e.activation(
                        out=ACT[:, o * T + tt * TT: o * T + (tt + 1) * TT], in_=p,
                        func=AF.Identity, bias=bq_colS[:, 24 + o:25 + o]),
                        waits=[tko, x_tok, fm_last])
                po.free = fm_last
            slot_cons[slot] = tko
            if idx + 2 < len(fm_jobs):
                k2, o2 = fm_jobs[idx + 2]
                wt, base, ncols = fm_src(k2, o2)
                pend[(k2, o2)] = col_load(wt, base - o2 * 128, ncols, o2, slot,
                                          slot_cons[slot])
        qfm_tok = fm_last

        th_tok = None
        for f in range(NF):
            pq = pgrab()
            tk = kb.trans(pq.ap[0:1, 0:128], s_thc[:, f:f + 1], s_idf[:],
                          waits=[pq.free, tok_x0, slot_cons[0], slot_cons[1]])
            th_tok = kb.vec(lambda e, p=pq.ap, f=f: e.tensor_copy(
                out=s_throw[0:1, f * 128:(f + 1) * 128], in_=p[0:1, 0:128]),
                waits=[tk, qfm_tok])
            pq.free = th_tok
        rep_toks = []
        for half in range(2):
            pq = pgrab()
            tk = kb.mm(pq.ap[:, 0:TT], s_or_f[0:1, :], s_throw[0:1, half * TT:(half + 1) * TT],
                       True, True, waits=[th_tok, pq.free])
            tv = kb.vec(lambda e, p=pq.ap, half=half: e.tensor_copy(
                out=s_threp[:, half * TT:(half + 1) * TT], in_=p[:, 0:TT]),
                waits=[tk] + (rep_toks[-1:] if rep_toks else []))
            pq.free = tv
            rep_toks.append(tv)
        threp_tok = rep_toks[-1]

        def trig_into(dst, pa, tka, shift):
            t1 = kb.vec(lambda e, p=pa: e.tensor_scalar(
                out=p, in0=p, scalar1=float(np.float32(1.0 / TWO_PI)),
                scalar2=float(np.float32(shift / TWO_PI)), op0=ALU.mult, op1=ALU.add),
                waits=[tka])
            t2 = kb.vec(lambda e, p=pa: e.tensor_scalar(
                out=p, in0=p, scalar1=MAGIC, scalar2=MAGIC, op0=ALU.add,
                op1=ALU.subtract), waits=[t1])
            return t2

        kst_tok = None
        kc_tok = None
        mm_M_toks = []
        prev_gw = {0: slot_cons[0], 1: slot_cons[1]}
        wslot = [0]

        def kv_load(cb, wait_tok):
            slot = wslot[0] % 2
            wslot[0] += 1
            stream = "wA" if slot == 0 else "wB"
            ssem = kb.sem[stream]
            bound = (ssem, ssem.n)
            ap_in = bass.AP(Wqkvu_t.tensor, D + cb * 128,
                            [[4 * D, 128], [128 * 4 * D, NF], [1, 128]])
            dst = w13b[:, slot * NF * 128:(slot + 1) * NF * 128]
            return kb.dma("gpsimd", dst, ap_in, stream=stream,
                          waits=[wait_tok, bound]), slot

        def kvb_load(cb, wait_tok):
            half = cb % 2
            src = bass.AP(bkv_tile.tensor, cb * 128, [[2 * D, 128], [1, 128]])
            return kb.dma("gpsimd", s_kvb[:, half * 256: half * 256 + 128], src,
                          stream="dbg", waits=[wait_tok]), half

        evac_tok = None
        bias_cons = None
        group_trig = []
        m_groups_started = set()
        for grp in range(4):
            for cb in range(16):
                ldt, slot = kv_load(cb, prev_gw[wslot[0] % 2])
                bld, bhalf = kvb_load(cb, bias_cons)
                sl = w13b[:, slot * NF * 128:(slot + 1) * NF * 128]
                for tci in range(4):
                    tc = grp * 4 + tci
                    pk = pgrab()
                    parts = [(XN[:, kc * T + tc * 128: kc * T + (tc + 1) * 128],
                              sl[:, kc * 128:(kc + 1) * 128])
                             for kc in range(NF)]
                    tkk = kb.mm_group(pk, parts, waits=[ldt, xn_tok], skip_gc=True,
                                      out=pk.ap[:, 0:128])
                    if cb < 8:
                        dstk = s_kgrp[:, tci * D + cb * 128: tci * D + (cb + 1) * 128]
                        tb1 = kb.vec(lambda e, p=pk.ap, bh=bhalf: e.tensor_tensor(
                            out=tmpA[:, 0:128], in0=p[:, 0:128],
                            in1=s_kvb[:, bh * 256: bh * 256 + 128], op=ALU.add),
                            waits=[tkk, bld, evac_tok, kst_tok])
                        evac_tok = kb.sca(lambda e, d=dstk: e.activation(
                            out=d, in_=tmpA[:, 0:128], func=AF.Silu), waits=[tb1])
                    else:
                        dstv = s_vgrp[:, tci * D + (cb - 8) * 128:
                                      tci * D + (cb - 7) * 128]
                        evac_tok = kb.vec(lambda e, p=pk.ap, d=dstv, bh=bhalf:
                                          e.tensor_tensor(
                                              out=d, in0=p[:, 0:128],
                                              in1=s_kvb[:, bh * 256: bh * 256 + 128],
                                              op=ALU.add),
                                          waits=[tkk, bld, evac_tok, kst_tok])
                    pk.free = evac_tok
                prev_gw[slot] = tkk
                bias_cons = evac_tok
            kv_done = evac_tok
            for tci in range(4):
                tc = grp * 4 + tci
                trig_toks = []
                for half in range(2):
                    tang = kb.vec(lambda e, half=half, tc=tc: e.tensor_scalar(
                        out=tmpA[:], in0=s_threp[:, half * TT:(half + 1) * TT],
                        scalar1=s_tic[:, tc:tc + 1], scalar2=None, op0=ALU.mult),
                        waits=[threp_tok, kv_done, kst_tok,
                               group_trig[-1] if group_trig else None,
                               trig_toks[-1] if trig_toks else None])
                    for isc, (shift, dstt) in enumerate(
                            [(0.0, s_tsin), (float(np.pi / 2), s_tcos)]):
                        pa = pgrab()
                        t1 = kb.vec(lambda e, p=pa.ap, shift=shift: e.tensor_scalar(
                            out=p[:, 0:TT], in0=tmpA[:],
                            scalar1=float(np.float32(1.0 / TWO_PI)),
                            scalar2=float(np.float32(shift / TWO_PI)),
                            op0=ALU.mult, op1=ALU.add), waits=[tang, pa.free])
                        t2 = kb.vec(lambda e, p=pa.ap: e.tensor_scalar(
                            out=p[:, 0:TT], in0=p[:, 0:TT], scalar1=MAGIC,
                            scalar2=MAGIC, op0=ALU.add, op1=ALU.subtract), waits=[t1])
                        t3 = kb.vec(lambda e, p=pa.ap: e.scalar_tensor_tensor(
                            out=p[:, 0:TT], in0=p[:, 0:TT], scalar=float(-TWO_PI),
                            in1=tmpA[:], op0=ALU.mult, op1=ALU.add), waits=[t2])
                        if shift != 0.0:
                            t3 = kb.vec(lambda e, p=pa.ap, shift=shift: e.tensor_scalar(
                                out=p[:, 0:TT], in0=p[:, 0:TT], scalar1=shift,
                                scalar2=None, op0=ALU.add), waits=[t3])
                        t4 = kb.vec(lambda e, p=pa.ap: e.tensor_scalar(
                            out=p[:, 0:TT], in0=p[:, 0:TT], scalar1=3.141592,
                            scalar2=-3.141592, op0=ALU.min, op1=ALU.max), waits=[t3])
                        t5 = kb.sca(lambda e, p=pa.ap, half=half, d=dstt: e.activation(
                            out=d[:, half * TT:(half + 1) * TT], in_=p[:, 0:TT],
                            func=AF.Sin), waits=[t4])
                        pa.free = t5
                        trig_toks.append(t5)
                tks_ = kb.vec(lambda e, tc=tci: e.tensor_tensor(
                    out=s_kst[:], in0=s_kgrp[:, tc * D:(tc + 1) * D], in1=s_tsin[:],
                    op=ALU.mult), waits=trig_toks + [kv_done, kst_tok])
                tkc_ = kb.vec(lambda e, tc=tci: e.tensor_tensor(
                    out=s_kgrp[:, tc * D:(tc + 1) * D],
                    in0=s_kgrp[:, tc * D:(tc + 1) * D], in1=s_tcos[:],
                    op=ALU.mult), waits=[tks_])
                group_trig.append(tkc_)
                mlast = None
                for h in range(H):
                    pM = psb[6] if h < 8 else psb[7]
                    hh = h % 8
                    kcs = s_kgrp[:, tci * D + h * DH: tci * D + (h + 1) * DH]
                    kss = s_kst[:, h * DH:(h + 1) * DH]
                    vs = s_vgrp[:, tci * D + h * DH: tci * D + (h + 1) * DH]
                    w0 = [tkc_]
                    if tc == 0 and pM.free is not None:
                        w0.append(pM.free)
                    mlast = kb.mm(pM.ap[0:64, hh * DH:(hh + 1) * DH], kcs, vs,
                                  start=(tc == 0), stop=(tc == NTC - 1), waits=w0,
                                  inc=True, skip_gc=True, force_inc=(tc == NTC - 1))
                    mlast = kb.mm(pM.ap[64:128, hh * DH:(hh + 1) * DH], kss, vs,
                                  start=(tc == 0), stop=(tc == NTC - 1), waits=[],
                                  inc=True, skip_gc=True,
                                  force_inc=(tc == NTC - 1 or h == H - 1))
                kst_tok = mlast
        m_done = kst_tok

        tv_m1 = kb.vec(lambda e, p=psb[6].ap: e.tensor_copy(
            out=s_Mkv[:, 0:512], in_=p), waits=[m_done, group_trig[-1]])
        tv_m2 = kb.vec(lambda e, p=psb[7].ap: e.tensor_copy(
            out=s_Mkv[:, 512:1024], in_=p), waits=[m_done, tv_m1])
        psb[6].free = tv_m2
        psb[7].free = tv_m2
        tex = kb.dma("gpsimd", cc_m_in.ap()[0:128, 0:1024], s_Mkv[:], waits=[tv_m2],
                     stream="dbg")
        tcc = kb.em("gpsimd", lambda e: e.collective_compute(
            "AllReduce", ALU.add, replica_groups=groups,
            ins=[cc_m_in.ap().opt()], outs=[cc_m_out.ap().opt()]),
            waits=[tex], inc=sem["cc"])
        mld = None
        for h in range(H):
            d = 64 * (h % 2)
            j = h // 2
            mld = kb.dma("gpsimd", s_Mcos[d:d + 64, j * DH:(j + 1) * DH],
                         cc_m_out.ap()[0:64, h * DH:(h + 1) * DH],
                         waits=[tcc], stream="dbg")
            mld = kb.dma("gpsimd", s_Msin[d:d + 64, j * DH:(j + 1) * DH],
                         cc_m_out.ap()[64:128, h * DH:(h + 1) * DH],
                         waits=[], stream="dbg")
        kb.em("gpsimd", lambda e, h=sem["dbg"].h, v=sem["dbg"].n: e.wait_ge(h, v))
        m_tok = (sem["dbg"], sem["dbg"].n)

        s_o2k = s_rt[:, 64:65]
        o2k_tok = kb.vec(lambda e: e.tensor_scalar(
            out=s_o2k, in0=s_odd[:], scalar1=2048.0, scalar2=None, op0=ALU.mult),
            waits=[m_tok, kb.rowtok] + kb.phase())
        y_toks = []
        st_rows = {}
        for tt in range(NT):
            pS2, pQ2 = psb[6], psb[7]
            tit = kb.gp(lambda e, tt=tt: e.iota(s_trep[:], [[1, TT]], base=tt * TT,
                                                channel_multiplier=0,
                                                allow_small_or_imprecise_dtypes=True),
                        waits=[psb[6].free, psb[7].free, y_toks[-1] if y_toks else None])
            tof = kb.vec(lambda e: e.tensor_scalar(
                out=s_trep[:], in0=s_trep[:], scalar1=s_o2k, scalar2=None,
                op0=ALU.add), waits=[tit, o2k_tok])
            stat1_tok = None
            stat2_tok = None
            ang_tok = None
            for f in range(NF):
                ta0 = kb.vec(lambda e, f=f: e.tensor_scalar(
                    out=s_angB[:], in0=s_trep[:], scalar1=s_thc[:, f:f + 1],
                    scalar2=None, op0=ALU.mult), waits=[tof, ang_tok])
                t5s = t5c = None
                for shift, dst in ((0.0, s_sfm), (float(np.pi / 2), s_cfm)):
                    pa = pgrab()
                    t1 = kb.vec(lambda e, p=pa.ap, shift=shift: e.tensor_scalar(
                        out=p[:, 0:TT], in0=s_angB[:],
                        scalar1=float(np.float32(1.0 / TWO_PI)),
                        scalar2=float(np.float32(shift / TWO_PI)),
                        op0=ALU.mult, op1=ALU.add), waits=[ta0, pa.free])
                    t2 = kb.vec(lambda e, p=pa.ap: e.tensor_scalar(
                        out=p[:, 0:TT], in0=p[:, 0:TT], scalar1=MAGIC,
                        scalar2=MAGIC, op0=ALU.add, op1=ALU.subtract), waits=[t1])
                    t3 = kb.vec(lambda e, p=pa.ap: e.scalar_tensor_tensor(
                        out=p[:, 0:TT], in0=p[:, 0:TT], scalar=float(-TWO_PI),
                        in1=s_angB[:], op0=ALU.mult, op1=ALU.add), waits=[t2])
                    if shift != 0.0:
                        t3 = kb.vec(lambda e, p=pa.ap, shift=shift: e.tensor_scalar(
                            out=p[:, 0:TT], in0=p[:, 0:TT], scalar1=shift,
                            scalar2=None, op0=ALU.add), waits=[t3])
                    t4 = kb.vec(lambda e, p=pa.ap: e.tensor_scalar(
                        out=p[:, 0:TT], in0=p[:, 0:TT], scalar1=3.141592,
                        scalar2=-3.141592, op0=ALU.min, op1=ALU.max), waits=[t3])
                    t5 = kb.sca(lambda e, p=pa.ap, d=dst: e.activation(
                        out=d[:], in_=p[:, 0:TT], func=AF.Sin), waits=[t4])
                    pa.free = t5
                    if shift == 0.0:
                        t5s = t5
                    else:
                        t5c = t5
                ang_tok = t5c
                tqc = kb.vec(lambda e, f=f, tt=tt: e.tensor_tensor(
                    out=s_qc[:], in0=SCR[:, f * T + tt * TT: f * T + (tt + 1) * TT],
                    in1=s_cfm[:], op=ALU.mult), waits=[t5c, qfm_tok])
                tqs = kb.vec(lambda e, f=f, tt=tt: e.tensor_tensor(
                    out=s_qs[:], in0=SCR[:, f * T + tt * TT: f * T + (tt + 1) * TT],
                    in1=s_sfm[:], op=ALU.mult), waits=[t5s, tqc])
                po = pgrab()
                for d in (0, 64):
                    jj = f
                    kb.mm(po.ap[d:d + 64, 0:TT], s_Mcos[d:d + 64, jj * DH:(jj + 1) * DH],
                          s_qc[d:d + 64, :], True, False,
                          waits=[tqc, m_tok, po.free], skip_gc=True)
                    kb.mm(po.ap[d:d + 64, 0:TT], s_Msin[d:d + 64, jj * DH:(jj + 1) * DH],
                          s_qs[d:d + 64, :], False, True,
                          waits=[tqs], skip_gc=True, force_inc=True)
                tko = (kb.sem["t"], kb.sem["t"].n)
                ty = kb.vec(lambda e, p=po.ap, f=f, tt=tt: e.tensor_copy(
                    out=y1[:, f * T + tt * TT: f * T + (tt + 1) * TT], in_=p),
                    waits=[tko])
                tsq = kb.vec(lambda e, f=f, tt=tt: e.tensor_tensor(
                    out=tmpA[:], in0=y1[:, f * T + tt * TT: f * T + (tt + 1) * TT],
                    in1=y1[:, f * T + tt * TT: f * T + (tt + 1) * TT],
                    op=ALU.mult), waits=[tko, ty])
                po.free = tsq
                w1_ = [ty]
                if f == 0:
                    w1_.append(pS2.free)
                stat1_tok = kb.mm(pS2.ap[0:1, 0:TT], s_oc_b[:, 0:1],
                                  y1[:, f * T + tt * TT: f * T + (tt + 1) * TT],
                                  start=(f == 0), stop=(f == NF - 1), waits=w1_,
                                  force_inc=True)
                w2_ = [tsq, stat1_tok]
                if f == 0:
                    w2_.append(pQ2.free)
                stat2_tok = kb.mm(pQ2.ap[0:1, 0:TT], s_oc_f[:, 0:1], tmpA[:],
                                  start=(f == 0), stop=(f == NF - 1), waits=w2_,
                                  force_inc=True)
            st_rows[tt] = (stat1_tok, stat2_tok)
            tv1 = kb.vec(lambda e, p=pS2.ap: e.tensor_scalar(
                out=s_mean[0:1, :], in0=p[0:1, 0:TT], scalar1=float(1.0 / D),
                scalar2=None, op0=ALU.mult), waits=[stat1_tok, kb.rowtok])
            pS2.free = tv1
            tv2 = kb.vec(lambda e: e.tensor_tensor(
                out=tmpA[0:1, 0:TT], in0=s_mean[0:1, :], in1=s_mean[0:1, :],
                op=ALU.mult), waits=[tv1, stat2_tok])
            tv3 = kb.vec(lambda e, p=pQ2.ap: e.scalar_tensor_tensor(
                out=tmpA[0:1, 0:TT], in0=p[0:1, 0:TT], scalar=float(1.0 / D),
                in1=tmpA[0:1, 0:TT], op0=ALU.mult, op1=ALU.subtract),
                waits=[stat2_tok, tv2])
            pQ2.free = tv3
            ts0 = kb.sca(lambda e: e.activation(
                out=tmpA[0:1, 0:TT], in_=tmpA[0:1, 0:TT], func=AF.Sqrt, bias=1e-5),
                waits=[tv3])

            def _rc3(e):
                with e.bass.allow_low_precision(reason="ln rows bf16"):
                    return e.reciprocal(out=s_rstd[0:1, :], in_=tmpA[0:1, 0:TT])
            ts1 = kb.vec(_rc3, waits=[ts0])
            tv4 = kb.vec(lambda e: e.tensor_tensor(
                out=s_mrs[0:1, :], in0=s_mean[0:1, :], in1=s_rstd[0:1, :],
                op=ALU.mult), waits=[ts1])
            kb.rowtok = tv4
            prs = pgrab()
            tk3 = kb.mm(prs.ap, s_or_b[0:1, :], s_rstd[0:1, :], True, True,
                        waits=[ts1, prs.free])
            prm = pgrab()
            tk4 = kb.mm(prm.ap, s_or_b[0:1, :], s_mrs[0:1, :], True, True,
                        waits=[tv4, prm.free])
            lastz = None
            for f in range(NF):
                ysl = y1[:, f * T + tt * TT: f * T + (tt + 1) * TT]
                zsl = ACT[:, f * T + tt * TT: f * T + (tt + 1) * TT]
                ta = kb.vec(lambda e, y=ysl, p=prs.ap: e.tensor_tensor(
                    out=tmpA[:], in0=y, in1=p, op=ALU.mult),
                    waits=[tk3, lastz if lastz else st_rows[tt][1]])
                tb = kb.vec(lambda e, p=prm.ap: e.tensor_tensor(
                    out=tmpA[:], in0=tmpA[:], in1=p, op=ALU.subtract),
                    waits=[tk4, ta])
                tc_ = kb.sca(lambda e, f=f: e.activation(
                    out=tmpA[:], in_=tmpA[:], func=AF.Identity,
                    bias=s_lnb[:, f:f + 1], scale=s_lnw[:, f:f + 1]), waits=[tb])
                lastz = kb.vec(lambda e, z=zsl: e.tensor_tensor(
                    out=z, in0=z, in1=tmpA[:], op=ALU.mult), waits=[tc_, fm_last])
            prs.free = lastz
            prm.free = lastz
            y_toks.append(lastz)
        z_tok = y_toks[-1]

        slot_cons = {0: z_tok, 1: z_tok}
        pend2 = {0: col_load(Wout_t.tensor, 0, D, 0, 0, z_tok),
                 1: col_load(Wout_t.tensor, 0, D, 1, 1, z_tok)}
        xupd = x_tok
        for o in range(NF):
            slot = o % 2
            ldt = pend2.pop(o)
            sl = w13b[:, slot * NF * 128:(slot + 1) * NF * 128]
            tko = None
            for tt in range(NT):
                po = pgrab()
                parts = [(sl[:, kc * 128:(kc + 1) * 128],
                          ACT[:, kc * T + tt * TT: kc * T + (tt + 1) * TT])
                         for kc in range(NF)]
                tko = kb.mm_group(po, parts, waits=[ldt, z_tok], skip_gc=True)
                xsl = X[:, o * T + tt * TT: o * T + (tt + 1) * TT]
                xupd = kb.vec(lambda e, p=po.ap, x=xsl, o=o: e.scalar_tensor_tensor(
                    out=x, in0=p, scalar=s_boc[:, o:o + 1], in1=x,
                    op0=ALU.add, op1=ALU.add), waits=[tko, xupd])
                po.free = xupd
            slot_cons[slot] = tko
            if o + 2 < NF:
                pend2[o + 2] = col_load(Wout_t.tensor, 0, D, o + 2, slot,
                                        slot_cons[slot])
        for snm in ("wA", "wB", "dbg"):
            ss = kb.sem[snm]
            kb.em("gpsimd", lambda e, h=ss.h, v=ss.n: e.wait_ge(h, v))
        kb.rtok_w13 = xupd
        kb.rtok_w2 = xupd
        return xupd

    x_tok = tok_x0
    nsub = 0

    kb.scope = "s1_l0ffn"
    xn_tok = emit_ln(x_tok)
    g_tok = emit_router(xn_tok, 16, 4, s_gw, s_eb, None)
    x_tok = emit_l0ffn(xn_tok, g_tok, x_tok)
    nsub += 1

    if n_stop >= 2:
        kb.scope = "s2_l0attn"
        xn_tok = emit_ln(x_tok)
        x_tok = emit_l0attn(xn_tok, x_tok)
        nsub += 1

    if n_stop >= 3:
        kb.scope = "s3_l0ffn"
        xn_tok = emit_ln(x_tok)
        g_tok = emit_router(xn_tok, 16, 4, s_gw, s_eb, None)
        x_tok = emit_l0ffn(xn_tok, g_tok, x_tok)
        nsub += 1

    if n_stop >= 4:
        kb.scope = "s4_l1ffn"
        xn_tok = emit_ln(x_tok)
        g_tok = emit_router(xn_tok, 12, 2, s_wrb, None, float(np.float32(1.0 / 1.5)))
        x_tok = emit_l1ffn(xn_tok, g_tok, x_tok)
        nsub += 1

    if n_stop >= 5:
        kb.scope = "s5_l1attn"
        xn_tok = emit_ln(x_tok)
        x_tok = emit_l1attn(xn_tok, x_tok)
        nsub += 1

    if n_stop >= 6:
        kb.scope = "s6_l1ffn"
        xn_tok = emit_ln(x_tok)
        g_tok = emit_router(xn_tok, 12, 2, s_wrb, None, float(np.float32(1.0 / 1.5)))
        x_tok = emit_l1ffn(xn_tok, g_tok, x_tok)
        nsub += 1

    if "xn" in dbg:
        oxn = nc.dram_tensor("oxn", [D, T], F32, kind="ExternalOutput").ap()
        for f in range(NF):
            kb.dma("gpsimd", oxn[f * 128:(f + 1) * 128, :], XN[:, f * T:(f + 1) * T],
                   waits=[x_tok], stream="dbg")
    if "gates" in dbg:
        ogt = nc.dram_tensor("ogt", [16, T], F32, kind="ExternalOutput").ap()
        kb.dma("gpsimd", ogt, s_gT[:], waits=[x_tok], stream="dbg")
    if dbg:
        kb.em("gpsimd", lambda e, h=sem["dbg"].h, v=sem["dbg"].n: e.wait_ge(h, v))

    st_toks = []
    for f in range(NF):
        st_toks.append(kb.em(
            "sync", lambda e, f=f: e.dma_start(out=xout[f * 128:(f + 1) * 128, :],
                                               in_=X[:, f * T:(f + 1) * T]),
            waits=[x_tok], inc=sem["st"], k=16))
    kb.em("sync", lambda e: e.nop(), waits=[st_toks[-1]])

    kb._refs = (tidxrow, aW_t, knwcol, Wqkvu_t, bq_col, bkv_tile, Wout_t, boutcol,
                thetarow, lnwcol, lnbcol, Ws1, bs1col, Ws2, bs2col, We1, be1col,
                We2, be2col, cc_s_in, cc_s_out, cc_m_in, cc_m_out, SCR,
                R5A, R5B, s_S, s_Sexp, s_kc, s_yc, s_ijl, s_ml, s_iju, s_mu,
                s_idb, s_oc_b, s_knw, s_onw, s_onb, s_betac, s_loglc, s_lamCc,
                s_powr, s_powlT)
    return kb.finish()


def _prep_inputs(I):
    f32 = np.float32
    ct = np.ascontiguousarray
    m = {}
    m["gw_t"] = ct(I["L0_ffn_gate_w"].T)
    m["ebias_tile"] = ct(np.tile(np.asarray(I["L0_ffn_expert_bias"], f32)[None, :], (128, 1)))
    m["w1_t"] = ct(np.asarray(I["L0_ffn_w1"]).transpose(0, 2, 1))
    m["w2_t"] = ct(np.asarray(I["L0_ffn_w2"]).transpose(0, 2, 1))
    m["w3_t"] = ct(np.asarray(I["L0_ffn_w3"]).transpose(0, 2, 1))
    m["sw1_t"] = ct(np.asarray(I["L0_ffn_sw1"]).T)
    m["sw2_t"] = ct(np.asarray(I["L0_ffn_sw2"]).T)
    m["sw3_t"] = ct(np.asarray(I["L0_ffn_sw3"]).T)
    m["aW_t"] = ct(np.asarray(I["L0_attn_W"]).T)
    m["knwcol"] = ct(np.tile(np.asarray(I["L0_attn_k_norm_w"], f32), 2)[:, None])
    m["onw_c"] = ct(np.asarray(I["L0_attn_o_norm_w"], f32).reshape(NF, 128).T)
    m["onb_c"] = ct(np.asarray(I["L0_attn_o_norm_b"], f32).reshape(NF, 128).T)
    m["lamrow"] = ct(np.asarray(I["L0_attn_lam_logit"], f32)[None, :])
    m["etarow"] = ct(np.asarray(I["L0_attn_eta_raw"], f32)[None, :])
    m["Wqkvu_t"] = ct(np.asarray(I["L1_attn_Wqkvu"]).T)
    m["bq_col"] = ct(np.asarray(I["L1_attn_bqkvu"], f32).reshape(32, 128).T)
    m["bkv_tile"] = ct(np.tile(np.asarray(I["L1_attn_bqkvu"], f32)[None, 1024:3072], (128, 1)))
    m["Wout_t"] = ct(np.asarray(I["L1_attn_Wout"]).T)
    m["boutcol"] = ct(np.asarray(I["L1_attn_bout"], f32).reshape(NF, 128).T)
    m["thetarow"] = ct(np.asarray(I["L1_attn_theta"], f32).reshape(1, D))
    m["thetacol_in"] = ct(np.asarray(I["L1_attn_theta"], f32).reshape(NF, 128).T)
    m["lnwcol"] = ct(np.asarray(I["L1_attn_ln_w"], f32).reshape(NF, 128).T)
    m["lnbcol"] = ct(np.asarray(I["L1_attn_ln_b"], f32).reshape(NF, 128).T)
    m["Ws1"] = ct(np.asarray(I["L1_ffn_Ws1"], f32))
    m["bs1col"] = ct(np.asarray(I["L1_ffn_bs1"], f32).reshape(32, 128).T)
    m["Ws2"] = ct(np.asarray(I["L1_ffn_Ws2"], f32))
    m["bs2col"] = ct(np.asarray(I["L1_ffn_bs2"], f32).reshape(NF, 128).T)
    m["We1"] = ct(np.asarray(I["L1_ffn_We1"], f32))
    m["be1col"] = ct(np.asarray(I["L1_ffn_be1"], f32).reshape(12 * 48, 128).T)
    m["We2"] = ct(np.asarray(I["L1_ffn_We2"], f32))
    m["be2col"] = ct(np.asarray(I["L1_ffn_be2"], f32).reshape(12 * NF, 128).T)
    m["Wr"] = ct(np.asarray(I["L1_ffn_Wr"], f32))
    m["br_tile"] = ct(np.tile(np.asarray(I["L1_ffn_br"], f32)[None, :], (128, 1)))
    return m


def _core_inputs(I, shared, c):
    f32 = np.float32
    b, half = c // 2, c % 2
    xs = np.asarray(I["L0_attn_x"], f32)[b, half * T:(half + 1) * T, :]
    m = dict(shared)
    m["x_fm"] = np.ascontiguousarray(xs.T)
    tix = np.arange(half * T, (half + 1) * T, dtype=f32)
    m["tidxrow"] = tix[None, :].copy()
    m["tidxcol"] = np.ascontiguousarray(tix.reshape(NTC, 128).T)
    m["oddmask"] = np.full((128, 1), float(half), f32)
    return m


def kernel(**inputs):
    n_cores = 8
    shared = _prep_inputs(inputs)
    nc = build()
    in_maps = [_core_inputs(inputs, shared, c) for c in range(n_cores)]
    res = run_bass_kernel_spmd(nc, in_maps, core_ids=list(range(n_cores)))
    out = np.empty((4, 4096, D), np.float32)
    for c in range(n_cores):
        b, half = c // 2, c % 2
        out[b, half * T:(half + 1) * T, :] = res.results[c]["xout"].T
    return out


# version 07
# speedup vs baseline: 1.0867x; 1.0867x over previous
import os
import numpy as np
import ml_dtypes
import concourse.bass as bass
import concourse.mybir as mybir
from concourse.bass_utils import run_bass_kernel_spmd

F32 = mybir.dt.float32
BF16 = mybir.dt.bfloat16
AF = mybir.ActivationFunctionType
ALU = mybir.AluOpType
AX = mybir.AxisListType

T = 2048
NT = 4
TT = 512
NTC = 16
NF = 8
D = 1024
C = 128
H = 16
DH = 64
MAGIC = 12582912.0
TWO_PI = 6.283185307179586
ENGS = ("sync", "gpsimd", "tensor", "vector", "scalar")
GROUPS8 = [[0, 1], [2, 3], [4, 5], [6, 7]]


class Sem:
    __slots__ = ("h", "n")

    def __init__(self, nc, name):
        self.h = nc.alloc_semaphore(name)
        self.n = 0


class PT:
    __slots__ = ("ap", "free", "bf")

    def __init__(self, ap, bf=None):
        self.ap = ap
        self.free = None
        self.bf = bf


class KB:
    def __init__(self):
        nc = self.nc = bass.Bass("TRN2", target_bir_lowering=False, debug=False)
        self.q = {e: [] for e in ENGS}
        self.wm = {e: {} for e in ENGS}
        self.sem = {}
        for name in ("t", "v", "s", "p", "d1", "d2", "cc", "st", "wA", "wB", "w2s", "dbg", "tr", "sp"):
            self.sem[name] = Sem(nc, "sem_" + name)
        self.rowtok = None
        self.rtok_w13 = None
        self.rtok_w2 = None
        self.scope = "init"

    def phase(self):
        return [(self.sem[n], self.sem[n].n) for n in ("t", "v", "s")]

    def em(self, eng, fn, waits=(), inc=None, k=1):
        wl = []
        wm = self.wm[eng]
        for tokv in waits:
            if tokv is None:
                continue
            s, v = tokv
            if v <= 0:
                continue
            if wm.get(s, 0) < v:
                wm[s] = v
                wl.append((s.h, v))
        tok = None
        hk = None
        if inc is not None:
            inc.n += k
            tok = (inc, inc.n)
            hk = (inc.h, k)

        def run(e, wl=wl, fn=fn, hk=hk):
            for h, v in wl:
                e.wait_ge(h, v)
            inst = fn(e)
            if hk is not None:
                inst.then_inc(*hk)
        run._scope = self.scope
        self.q[eng].append(run)
        return tok

    def dma(self, qeng, out, in_, waits=(), stream=None):
        sem = self.sem[stream] if stream else (
            self.sem["d1"] if qeng == "sync" else self.sem["d2"])
        return self.em(qeng, lambda e: e.dma_start(out=out, in_=in_),
                       waits=waits, inc=sem, k=16)

    def mm(self, out, lhsT, rhs, start, stop, waits=(), inc=True, skip_gc=False,
           force_inc=False):
        return self.em(
            "tensor",
            lambda e: e.matmul(out, lhsT, rhs, start=start, stop=stop,
                               skip_group_check=skip_gc),
            waits=waits, inc=self.sem["t"] if (force_inc or (inc and stop)) else None)

    def mm_group(self, pt, parts, waits=(), skip_gc=False, out=None):
        n = len(parts)
        tok = None
        oap = pt.ap if out is None else out
        for j, (lhsT, rhs) in enumerate(parts):
            w = (list(waits) + [pt.free]) if j == 0 else []
            tok = self.mm(oap, lhsT, rhs, start=(j == 0), stop=(j == n - 1),
                          waits=w, inc=True, skip_gc=skip_gc)
        return tok

    def trans(self, out_ap, in_ap, ident, waits=()):
        return self.em("tensor", lambda e: e.transpose(out_ap, in_ap, ident),
                       waits=waits, inc=self.sem["t"])

    def vec(self, fn, waits=(), inc=True):
        return self.em("vector", fn, waits=waits,
                       inc=self.sem["v"] if inc else None)

    def sca(self, fn, waits=(), inc=True):
        return self.em("scalar", fn, waits=waits,
                       inc=self.sem["s"] if inc else None)

    def gp(self, fn, waits=()):
        return self.em("gpsimd", fn, waits=waits, inc=self.sem["p"])

    def finish(self):
        nc = self.nc
        q = self.q

        def runall(e, fns):
            cur = None
            for f in fns:
                s = getattr(f, "_scope", None)
                if s != cur:
                    if cur is not None:
                        nc.pop_named_scope(cur)
                    if s is not None:
                        nc.push_named_scope(s)
                    cur = s
                f(e)
            if cur is not None:
                nc.pop_named_scope(cur)
        with nc.Block() as blk:
            @blk.sync
            def _(e):
                runall(e, q["sync"])

            @blk.gpsimd
            def _(e):
                runall(e, q["gpsimd"])

            @blk.tensor
            def _(e):
                runall(e, q["tensor"])

            @blk.vector
            def _(e):
                runall(e, q["vector"])

            @blk.scalar
            def _(e):
                runall(e, q["scalar"])
        return nc


def build(n_stop=6, groups=GROUPS8, dbg=()):
    kb = KB()
    nc = kb.nc
    sem = kb.sem
    sb = nc.alloc_sbuf_tensor
    ps = nc.alloc_psum_tensor

    def di(name, shape):
        return nc.dram_tensor(name, shape, F32, kind="ExternalInput").ap()

    x_in = di("x_fm", [D, T])
    tidxrow = di("tidxrow", [1, T])
    tidxcol = di("tidxcol", [128, NTC])
    oddmask = di("oddmask", [128, 1])
    gw_t = di("gw_t", [D, 16])
    ebias_tile = di("ebias_tile", [128, 16])
    w1_t = di("w1_t", [16, D, D])
    w2_t = di("w2_t", [16, D, D])
    w3_t = di("w3_t", [16, D, D])
    sw1_t = di("sw1_t", [D, D])
    sw2_t = di("sw2_t", [D, D])
    sw3_t = di("sw3_t", [D, D])
    aW_t = di("aW_t", [D, D])
    knwcol = di("knwcol", [128, 1])
    onw_c = di("onw_c", [128, NF])
    onb_c = di("onb_c", [128, NF])
    lamrow = di("lamrow", [1, 16])
    etarow = di("etarow", [1, 16])
    Wqkvu_t = di("Wqkvu_t", [D, 4 * D])
    bq_col = di("bq_col", [128, 32])
    bkv_tile = di("bkv_tile", [128, 2 * D])
    Wout_t = di("Wout_t", [D, D])
    boutcol = di("boutcol", [128, NF])
    thetarow = di("thetarow", [1, D])
    thetacol_in = di("thetacol_in", [128, NF])
    lnwcol = di("lnwcol", [128, NF])
    lnbcol = di("lnbcol", [128, NF])
    Ws1 = di("Ws1", [D, 4 * D])
    bs1col = di("bs1col", [128, 32])
    Ws2 = di("Ws2", [2 * D, D])
    bs2col = di("bs2col", [128, NF])
    We1 = di("We1", [12, D, 6 * D])
    be1col = di("be1col", [128, 12 * 48])
    We2 = di("We2", [12, 3 * D, D])
    be2col = di("be2col", [128, 12 * NF])
    Wr = di("Wr", [D, 12])
    br_tile = di("br_tile", [128, 12])

    xout = nc.dram_tensor("xout", [D, T], F32, kind="ExternalOutput").ap()

    cc_s_in = nc.dram_tensor("cc_s_in", [128, 8 * DH], F32, kind="Internal")
    cc_s_out = nc.dram_tensor("cc_s_out", [256, 8 * DH], F32, kind="Internal")
    cc_m_in = nc.dram_tensor("cc_m_in", [128, H * DH], F32, kind="Internal")
    cc_m_out = nc.dram_tensor("cc_m_out", [128, H * DH], F32, kind="Internal")
    u_sp = nc.dram_tensor("u_sp", [NTC * 128, 2048], BF16, kind="Internal")
    a_sp = nc.dram_tensor("a_sp", [NTC * 128, 8 * DH], F32, kind="Internal")
    b_sp = nc.dram_tensor("b_sp", [NTC * 128, 8 * DH], F32, kind="Internal")

    ident_f = nc.inline_tensor(np.eye(128, dtype=np.float32), "ident_f").ap()
    ident_b = nc.inline_tensor(np.eye(128, dtype=np.float32).astype(ml_dtypes.bfloat16), "ident_b").ap()
    ij = np.arange(128)[:, None] - np.arange(128)[None, :]
    iju_h = nc.inline_tensor(np.maximum(-ij, 0).astype(ml_dtypes.bfloat16), "iju").ap()
    ijl_h = nc.inline_tensor(np.maximum(ij, 0).astype(ml_dtypes.bfloat16), "ijl").ap()
    mu_h = nc.inline_tensor((ij < 0).astype(ml_dtypes.bfloat16), "mu").ap()
    ml_h = nc.inline_tensor((ij > 0).astype(ml_dtypes.bfloat16), "ml").ap()
    ones_r_f = nc.inline_tensor(np.ones((1, 128), dtype=np.float32), "ones_r_f").ap()
    ones_r_b = nc.inline_tensor(np.ones((1, 128), dtype=np.float32).astype(ml_dtypes.bfloat16), "ones_r_b").ap()
    ones_c_f = nc.inline_tensor(np.ones((128, 1), dtype=np.float32), "ones_c_f").ap()
    ones_c_b = nc.inline_tensor(np.ones((128, 1), dtype=np.float32).astype(ml_dtypes.bfloat16), "ones_c_b").ap()
    icol1_h = nc.inline_tensor(np.arange(1, 129, dtype=np.float32)[None, :], "icol1").ap()
    icolrev_h = nc.inline_tensor(np.arange(127, -1, -1, dtype=np.float32)[None, :], "icolrev").ap()
    h2np = np.zeros((128, 2), dtype=np.float32)
    h2np[0:64, 0] = 1.0
    h2np[64:128, 1] = 1.0
    h2_h = nc.inline_tensor(h2np, "h2c").ap()
    p2_h = nc.inline_tensor(np.ascontiguousarray(h2np.T).astype(ml_dtypes.bfloat16), "p2c").ap()
    cvals = [0.0, 1.0, -1.0, 0.5, 1e-5, float(np.finfo(np.float32).eps)]
    consts_h = nc.inline_tensor(
        np.tile(np.array(cvals, dtype=np.float32)[None, :], (128, 1)), "constsh").ap()

    X = sb("X", [128, NF * T], F32)
    XN = sb("XN", [128, NF * T], BF16)
    ACT = sb("ACT", [128, NF * T], BF16)
    SCR = sb("SCR", [128, NF * T], BF16)
    R5A = sb("R5A", [128, 8192], BF16)
    R5B = sb("R5B", [128, 8192], BF16)
    a_r5a = nc.lookup_mloc(R5A).addr
    a_r5b = nc.lookup_mloc(R5B).addr
    a_scr = nc.lookup_mloc(SCR).addr
    a_xn = nc.lookup_mloc(XN).addr
    a_act = nc.lookup_mloc(ACT).addr
    at = nc.alloc_sbuf_tensor_at
    w2b = at("w2b", [128, NF * D], BF16, offset=a_r5a)
    w13b = at("w13b", [128, 2 * 2 * NF * 128], BF16, offset=a_r5b)
    s_grep = at("s_grep", [128, T], BF16, offset=a_r5b + 8192)

    s_consts = sb("s_consts", [128, len(cvals)], F32)
    for i, v in enumerate(cvals):
        nc.const_aps.register(F32, v, s_consts[:, i:i + 1])
    s_idf = sb("s_idf", [128, 128], F32)
    s_idb = sb("s_idb", [128, 128], BF16)
    s_iju = sb("s_iju", [128, 128], BF16)
    s_ijl = sb("s_ijl", [128, 128], BF16)
    s_mu = sb("s_mu", [128, 128], BF16)
    s_ml = sb("s_ml", [128, 128], BF16)
    s_or_f = sb("s_or_f", [1, 128], F32)
    s_or_b = sb("s_or_b", [1, 128], BF16)
    s_oc_f = sb("s_oc_f", [128, 1], F32)
    s_oc_b = sb("s_oc_b", [128, 1], BF16)
    s_ic1 = sb("s_ic1", [1, 128], F32)
    s_icr = sb("s_icr", [1, 128], F32)
    s_odd = sb("s_odd", [128, 1], F32)
    s_h2 = sb("s_h2", [128, 2], F32)
    s_p2 = sb("s_p2", [2, 128], BF16)
    s_gw = sb("s_gw", [128, NF * 16], BF16)
    s_eb = sb("s_eb", [128, 16], F32)
    s_wrb = sb("s_wrb", [128, NF * 12], BF16)
    s_br = sb("s_br", [128, 12], F32)
    s_brs = sb("s_brs", [128, 12], F32)
    s_knw = sb("s_knw", [128, 1], F32)
    s_thc = sb("s_thc", [128, NF], F32)
    s_tic = sb("s_tic", [128, NTC], F32)
    s_lnw = sb("s_lnw", [128, NF], F32)
    s_lnb = sb("s_lnb", [128, NF], F32)
    s_boc = sb("s_boc", [128, NF], F32)
    bq_colS = sb("bq_colS", [128, 32], F32)
    s_onw = sb("s_onw", [128, NF], F32)
    s_onb = sb("s_onb", [128, NF], F32)
    s_be1 = at("s_be1", [128, 12 * 48], BF16, offset=a_r5b + 13312)
    s_be2 = at("s_be2", [128, 12 * NF], F32, offset=a_r5b + 14464)
    s_bs1 = at("s_bs1", [128, 32], F32, offset=a_r5b + 14848)
    s_bs2 = at("s_bs2", [128, NF], F32, offset=a_r5b + 14976)
    s_lam = sb("s_lam", [1, 16], F32)
    s_eta = sb("s_eta", [1, 16], F32)
    s_beta = sb("s_beta", [1, 16], F32)
    s_logl = sb("s_logl", [1, 16], F32)
    s_lamC = sb("s_lamC", [1, 16], F32)
    s_betac = sb("s_betac", [128, 16], F32)
    s_nbetac = sb("s_nbetac", [128, 16], F32)
    s_loglc = sb("s_loglc", [128, 16], F32)
    s_lamCc = sb("s_lamCc", [128, 16], F32)
    s_powl = sb("s_powl", [128, 16], F32)
    s_powr = sb("s_powr", [128, 16], F32)
    s_powlT = sb("s_powlT", [16, 128], F32)
    s_gT = sb("s_gT", [16, T], BF16)
    s_rstd = sb("s_rstd", [1, TT], BF16)
    s_mrs = sb("s_mrs", [1, TT], BF16)
    s_mean = sb("s_mean", [1, TT], BF16)
    s_rt = sb("s_rt", [128, 66], F32)
    s_tq = s_rt[:, 50:66]
    s_S = at("s_S", [128, 8 * DH], F32, offset=a_scr)
    s_Sexp = at("s_Sexp", [128, 8 * DH], F32, offset=a_scr + 2048)
    s_kc = at("s_kc", [128, NF * C], BF16, offset=a_scr + 4096)
    s_yc = at("s_yc", [128, NF * C], F32, offset=a_scr + 8192)
    _o = a_scr + 12288
    s_rmsr = at("s_rmsr", [2, NF * C], BF16, offset=_o); _o += 2048
    s_lamCI = at("s_lamCI", [128, 8 * DH], BF16, offset=_o); _o += 1024
    s_prep = at("s_prep", [128, 8 * C], BF16, offset=_o); _o += 2048
    s_usl = at("s_usl", [128, 2048], BF16, offset=_o); _o += 4096
    s_At = at("s_At", [128, 8 * DH], F32, offset=_o); _o += 2048
    s_Bt = at("s_Bt", [128, 8 * DH], F32, offset=_o); _o += 2048
    s_betp = at("s_betp", [128, 8], F32, offset=_o); _o += 32
    s_nbetp = at("s_nbetp", [128, 8], F32, offset=_o); _o += 32
    assert _o - a_scr <= 32768, _o - a_scr
    s_Dup = at("s_Dup", [128, 16 * C], BF16, offset=a_r5b)
    s_Dlo = at("s_Dlo", [128, 16 * C], BF16, offset=a_r5b + 4096)
    SLT = []
    for _s in range(2):
        _b = a_r5a + _s * 6144
        SLT.append(dict(
            G=at(f"pG_{_s}", [128, 256], BF16, offset=_b + 0),
            M1=at(f"pM1_{_s}", [128, 256], BF16, offset=_b + 512),
            Np=at(f"pNp_{_s}", [128, 256], BF16, offset=_b + 1024),
            Mp=[None] + [at(f"pMp{_i}_{_s}", [128, 256], BF16,
                            offset=_b + 1536 + (_i - 1) * 512)
                         for _i in range(1, 7)],
            Xa=at(f"pXa_{_s}", [128, 256], BF16, offset=_b + 4608),
            Xb=at(f"pXb_{_s}", [128, 256], BF16, offset=_b + 5120),
            Kt=at(f"pKt_{_s}", [128, 128], BF16, offset=_b + 5632),
            Kd=at(f"pKd_{_s}", [128, 128], BF16, offset=_b + 5888),
            FG=at(f"pFG_{_s}", [128, 256], BF16, offset=a_r5a + 12288 + _s * 1536),
            TT=at(f"pTT_{_s}", [128, 256], BF16, offset=a_r5a + 12800 + _s * 1536),
            Pt=at(f"pPt_{_s}", [128, 128], F32, offset=a_r5a + 13312 + _s * 1536),
        ))
    s_id2 = at("s_id2", [128, 256], BF16, offset=a_r5a + 15360)
    s_kgrp = at("s_kgrp", [128, 4 * D], BF16, offset=a_r5a)
    s_vgrp = at("s_vgrp", [128, 4 * D], BF16, offset=a_r5a + 8192)
    s_Mkv = at("s_Mkv", [128, H * DH], F32, offset=a_r5a)
    s_Mcos = at("s_Mcos", [128, 8 * DH], BF16, offset=a_r5a + 4096)
    s_Msin = at("s_Msin", [128, 8 * DH], BF16, offset=a_r5a + 6144)
    s_trep = at("s_trep", [128, TT], F32, offset=a_r5a + 8192)
    s_cfm = at("s_cfm", [128, TT], F32, offset=a_r5a + 10240)
    s_sfm = at("s_sfm", [128, TT], F32, offset=a_r5a + 12288)
    s_qc = at("s_qc", [128, TT], BF16, offset=a_r5a + 14336)
    s_qs = at("s_qs", [128, TT], BF16, offset=a_r5a + 15360)
    s_throw = at("s_throw", [1, D], F32, offset=a_r5b + 4096)
    s_tcos = at("s_tcos", [128, D], BF16, offset=a_r5b + 4096)
    s_tsin = at("s_tsin", [128, D], BF16, offset=a_r5b + 6144)
    s_threp = at("s_threp", [128, D], F32, offset=a_r5b + 8192)
    s_angB = at("s_angB", [128, TT], F32, offset=a_r5a)
    s_kvb = at("s_kvb", [128, 2 * 256], BF16, offset=a_r5b + 12288)
    y1 = at("y1", [128, NF * T], BF16, offset=a_xn)
    tmpA = sb("tmpA", [128, TT], F32)
    s_kst = at("s_kst", [128, D], BF16, offset=nc.lookup_mloc(tmpA).addr)
    tmpC = at("tmpC", [128, TT], BF16, offset=a_r5b + 12288)
    s_sel = s_rt[:, 0:16]
    s_bb = s_rt[:, 16:32]
    s_hit = s_rt[:, 32:48]
    s_m1 = s_rt[:, 48:49]
    s_m2 = s_rt[:, 49:50]

    psh = [ps(f"psb{i}", [128, TT], F32) for i in range(8)]
    psb = [PT(h.ap(), h.bitcast(BF16)) for h in psh]

    for f in range(NF):
        kb.dma("sync", X[:, f * T:(f + 1) * T], x_in[f * 128:(f + 1) * 128, :])
    for dst, src in [(s_consts[:], consts_h), (s_idf[:], ident_f), (s_idb[:], ident_b),
                     (s_iju[:], iju_h), (s_ijl[:], ijl_h), (s_mu[:], mu_h), (s_ml[:], ml_h),
                     (s_or_f[:], ones_r_f), (s_or_b[:], ones_r_b),
                     (s_oc_f[:], ones_c_f), (s_oc_b[:], ones_c_b),
                     (s_ic1[:], icol1_h), (s_icr[:], icolrev_h),
                     (s_odd[:], oddmask), (s_eb[:], ebias_tile),
                     (s_h2[:], h2_h), (s_p2[:], p2_h),
                     (s_br[:], br_tile), (s_knw[:], knwcol),
                     (s_thc[:], thetacol_in), (s_tic[:], tidxcol),
                     (s_lnw[:], lnwcol), (s_lnb[:], lnbcol),
                     (s_boc[:], boutcol), (bq_colS[:], bq_col),
                     (s_onw[:], onw_c), (s_onb[:], onb_c),
                     (s_lam[:], lamrow), (s_eta[:], etarow),
                     (s_be2[:], be2col),
                     (s_bs1[:], bs1col), (s_bs2[:], bs2col)]:
        kb.dma("sync", dst, src)
    tok_x0 = (sem["d1"], sem["d1"].n)
    kb.dma("gpsimd", s_be1[:], be1col)
    kb.dma("gpsimd", s_gw[:], bass.AP(gw_t.tensor, 0, [[16, 128], [16 * 128, NF], [1, 16]]))
    kb.dma("gpsimd", s_wrb[:], bass.AP(Wr.tensor, 0, [[12, 128], [12 * 128, NF], [1, 12]]))
    tok_init_d2 = (sem["d2"], sem["d2"].n)
    kb.em("gpsimd", lambda e, v=sem["d2"].n: e.wait_ge(sem["d2"].h, v))

    t1 = kb.sca(lambda e: e.activation(out=s_lam[:], in_=s_lam[:], func=AF.Sigmoid), waits=[tok_x0])
    t2 = kb.sca(lambda e: e.activation(out=s_eta[:], in_=s_eta[:], func=AF.Tanh), waits=[tok_x0])
    t3 = kb.vec(lambda e: e.tensor_scalar(out=s_beta[:], in0=s_lam[:], scalar1=1.0,
                                          scalar2=None, op0=ALU.subtract), waits=[t1])
    t4 = kb.vec(lambda e: e.tensor_tensor(out=s_beta[:], in0=s_beta[:], in1=s_eta[:],
                                          op=ALU.mult), waits=[t2, t3])
    t5 = kb.vec(lambda e: e.tensor_scalar(out=s_beta[:], in0=s_beta[:],
                                          scalar1=float(np.float32(1.0 / 64)), scalar2=None,
                                          op0=ALU.mult), waits=[t4])
    t6 = kb.sca(lambda e: e.activation(out=s_logl[:], in_=s_lam[:], func=AF.Ln), waits=[t1])
    t7 = kb.sca(lambda e: e.activation(out=s_lamC[:], in_=s_logl[:], func=AF.Exp, scale=128.0),
                waits=[t6])
    t8 = kb.vec(lambda e: e.tensor_scalar(out=s_brs[:], in0=s_br[:],
                                          scalar1=float(np.float32(1.0 / 1.5)), scalar2=None,
                                          op0=ALU.mult), waits=[tok_x0])

    def repl_row(dst, src_row, wait):
        pq = psb[6]
        tk = kb.mm(pq.ap[:, 0:16], s_or_f[0:1, :], src_row, True, True,
                   waits=[wait, pq.free])
        tv = kb.vec(lambda e: e.tensor_copy(out=dst, in_=pq.ap[:, 0:16]), waits=[tk])
        pq.free = tv
        return tv

    tbc = repl_row(s_betac[:], s_beta[0:1, :], t5)
    kb.vec(lambda e: e.tensor_scalar(out=s_nbetac[:], in0=s_betac[:], scalar1=-1.0,
                                     scalar2=None, op0=ALU.mult), waits=[tbc])
    repl_row(s_loglc[:], s_logl[0:1, :], t6)
    repl_row(s_lamCc[:], s_lamC[0:1, :], t7)
    pq = psb[6]
    tk = kb.mm(pq.ap[:, 0:16], s_ic1[0:1, :], s_logl[0:1, :], True, True,
               waits=[t6, pq.free])
    tpl = kb.sca(lambda e: e.activation(out=s_powl[:], in_=pq.ap[:, 0:16], func=AF.Exp),
                 waits=[tk])
    pq.free = tpl
    tk = kb.mm(pq.ap[:, 16:32], s_icr[0:1, :], s_logl[0:1, :], True, True,
               waits=[t6, pq.free])
    tpr = kb.sca(lambda e: e.activation(out=s_powr[:], in_=pq.ap[:, 16:32], func=AF.Exp),
                 waits=[tk])
    pq.free = tpr
    tk = kb.trans(psb[7].ap[0:16, 0:128], s_powl[:], s_idf[:], waits=[tpl, psb[7].free])
    tplT = kb.vec(lambda e: e.tensor_copy(out=s_powlT[:], in_=psb[7].ap[0:16, 0:128]),
                  waits=[tk])
    psb[7].free = tplT
    tgz = kb.vec(lambda e: e.memset(s_bb[:], 0.0))

    def emit_ln(x_tok):
        last = None
        for tt in range(NT):
            pS, pQ = psb[6], psb[7]
            g1 = [(s_oc_f[:, 0:1], X[:, f * T + tt * TT: f * T + (tt + 1) * TT])
                  for f in range(NF)]
            tk1 = kb.mm_group(pS, g1, waits=[x_tok], out=pS.ap[0:1, 0:TT])
            tk2 = None
            prev_mm = None
            for f in range(NF):
                xs = X[:, f * T + tt * TT: f * T + (tt + 1) * TT]
                stok = kb.vec(lambda e, xs=xs: e.tensor_tensor(out=tmpA[:], in0=xs, in1=xs,
                                                               op=ALU.mult),
                              waits=[x_tok, prev_mm, last])
                w = [stok]
                if f == 0:
                    w.append(pQ.free)
                tk2 = kb.mm(pQ.ap[0:1, 0:TT], s_oc_f[:, 0:1], tmpA[:], start=(f == 0),
                            stop=(f == NF - 1), waits=w, force_inc=True)
                prev_mm = tk2
            r_mean = s_mean[0:1, :]
            r_rstd = s_rstd[0:1, :]
            r_mrs = s_mrs[0:1, :]
            tv1 = kb.vec(lambda e, r=r_mean, p=pS.ap: e.tensor_scalar(
                out=r, in0=p[0:1, 0:TT], scalar1=float(np.float32(1.0 / D)), scalar2=None,
                op0=ALU.mult), waits=[tk1, kb.rowtok])
            pS.free = tv1
            tv2 = kb.vec(lambda e, r=r_mean: e.tensor_tensor(
                out=tmpA[0:1, 0:TT], in0=r, in1=r, op=ALU.mult), waits=[tv1, tk2])
            tv3 = kb.vec(lambda e, p=pQ.ap: e.scalar_tensor_tensor(
                out=tmpA[0:1, 0:TT], in0=p[0:1, 0:TT], scalar=float(np.float32(1.0 / D)),
                in1=tmpA[0:1, 0:TT], op0=ALU.mult, op1=ALU.subtract), waits=[tk2, tv2])
            pQ.free = tv3
            ts0 = kb.sca(lambda e: e.activation(
                out=tmpA[0:1, 0:TT], in_=tmpA[0:1, 0:TT], func=AF.Sqrt, bias=1e-5),
                waits=[tv3])
            def _recip(e, r=r_rstd):
                with e.bass.allow_low_precision(reason="xn stored bf16"):
                    return e.reciprocal(out=r, in_=tmpA[0:1, 0:TT])
            ts1 = kb.vec(_recip, waits=[ts0])
            tv4 = kb.vec(lambda e, r1=r_mrs, r2=r_rstd, rm=r_mean: e.tensor_tensor(
                out=r1, in0=rm, in1=r2, op=ALU.mult), waits=[ts1])
            prs, pms = psb[4], psb[5]
            tk3 = kb.mm(prs.ap, s_or_b[0:1, :], r_rstd, True, True,
                        waits=[ts1, prs.free])
            tk4 = kb.mm(pms.ap, s_or_b[0:1, :], r_mrs, True, True,
                        waits=[tv4, pms.free])
            for f in range(NF):
                xs = X[:, f * T + tt * TT: f * T + (tt + 1) * TT]
                xn = XN[:, f * T + tt * TT: f * T + (tt + 1) * TT]
                ta = kb.vec(lambda e, xs=xs: e.tensor_tensor(
                    out=tmpA[:], in0=xs, in1=prs.ap, op=ALU.mult),
                    waits=[tk3, x_tok, last])
                last = kb.vec(lambda e, xn=xn: e.tensor_tensor(
                    out=xn, in0=tmpA[:], in1=pms.ap, op=ALU.subtract),
                    waits=[tk4, ta])
            prs.free = last
            pms.free = last
        return last

    def emit_router(xn_tok, E, topk, wbuf, bias_sel, scale):
        last = None
        for tc in range(NTC):
            pl = psb[6]
            parts = [(XN[:, f * T + tc * 128: f * T + (tc + 1) * 128],
                      wbuf[:, f * E:(f + 1) * E]) for f in range(NF)]
            tkm = kb.mm_group(pl, parts, waits=[xn_tok, tok_init_d2],
                              out=pl.ap[:, 0:E])
            if scale is not None:
                tv0 = kb.vec(lambda e, p=pl.ap[:, 0:E]: e.scalar_tensor_tensor(
                    out=s_tq[:, 0:E], in0=p, scalar=float(np.float32(scale)), in1=s_brs[:, 0:E],
                    op0=ALU.mult, op1=ALU.add), waits=[tkm, t8, last])
            else:
                tv0 = kb.vec(lambda e, p=pl.ap[:, 0:E]: e.tensor_copy(out=s_tq[:, 0:E], in_=p),
                             waits=[tkm, last])
            pl.free = tv0
            if bias_sel is not None:
                tv1 = kb.vec(lambda e: e.tensor_tensor(
                    out=s_bb[:, 0:E], in0=s_tq[:, 0:E], in1=bias_sel[:, 0:E], op=ALU.add),
                    waits=[tv0, tok_x0])
            else:
                tv1 = kb.vec(lambda e: e.tensor_copy(out=s_bb[:, 0:E], in_=s_tq[:, 0:E]),
                             waits=[tv0])
            tsel = None
            for r in range(topk):
                tm = kb.vec(lambda e: e.reduce_max(s_m1[:], s_bb[:, 0:E], axis=AX.X),
                            waits=[tv1])
                th = kb.vec(lambda e: e.tensor_scalar(
                    out=s_hit[:, 0:E], in0=s_bb[:, 0:E], scalar1=s_m1[:, 0:1],
                    scalar2=None, op0=ALU.is_ge), waits=[tm])
                if r == 0:
                    tsel = kb.vec(lambda e: e.tensor_copy(out=s_sel[:, 0:E],
                                                          in_=s_hit[:, 0:E]), waits=[th])
                else:
                    tsel = kb.vec(lambda e: e.tensor_tensor(
                        out=s_sel[:, 0:E], in0=s_sel[:, 0:E], in1=s_hit[:, 0:E],
                        op=ALU.add), waits=[th, tsel])
                if r < topk - 1:
                    tv1 = kb.vec(lambda e: e.scalar_tensor_tensor(
                        out=s_bb[:, 0:E], in0=s_hit[:, 0:E], scalar=float(np.float32(-1e30)),
                        in1=s_bb[:, 0:E], op0=ALU.mult, op1=ALU.add), waits=[th])
            ta = kb.vec(lambda e: e.tensor_tensor(out=s_bb[:, 0:E], in0=s_tq[:, 0:E],
                                                  in1=s_sel[:, 0:E], op=ALU.mult),
                        waits=[tsel])
            tb = kb.vec(lambda e: e.tensor_scalar(
                out=s_hit[:, 0:E], in0=s_sel[:, 0:E], scalar1=float(np.float32(1e30)),
                scalar2=float(np.float32(-1e30)), op0=ALU.mult, op1=ALU.add), waits=[tsel, ta])
            tc_ = kb.vec(lambda e: e.tensor_tensor(out=s_bb[:, 0:E], in0=s_bb[:, 0:E],
                                                   in1=s_hit[:, 0:E], op=ALU.add),
                         waits=[tb])
            tn = kb.vec(lambda e: e.reduce_max(s_m2[:], s_bb[:, 0:E], axis=AX.X,
                                               negate=True), waits=[tc_])
            te = kb.sca(lambda e: e.activation(out=s_bb[:, 0:E], in_=s_bb[:, 0:E],
                                               func=AF.Exp, bias=s_m2[:, 0:1]), waits=[tn])
            tsum = kb.vec(lambda e: e.reduce_sum(s_m1[:], s_bb[:, 0:E], axis=AX.X),
                          waits=[te])
            trc = kb.vec(lambda e: e.reciprocal(out=s_m2[:], in_=s_m1[:]), waits=[tsum])
            gdst = s_bb[:, 0:16]
            tg = kb.vec(lambda e: e.tensor_scalar(
                out=s_bb[:, 0:E], in0=s_bb[:, 0:E], scalar1=s_m2[:, 0:1], scalar2=None,
                op0=ALU.mult), waits=[trc, tgz])
            pq2 = psb[7]
            tk = kb.trans(pq2.ap[0:16, 0:128], gdst, s_idf[:], waits=[tg, pq2.free])
            last = kb.vec(lambda e, p=pq2.ap, tc=tc: e.tensor_copy(
                out=s_gT[0:E, tc * 128:(tc + 1) * 128], in_=p[0:E, 0:128]), waits=[tk])
            pq2.free = last
        return last

    def emit_l0ffn(xn_tok, g_tok, x_tok):
        W13HALF = 2 * NF * 128

        def emit_w13_loads(e_, i_, slot, wait_tok):
            if e_ < 16:
                srcs = [(w1_t.tensor, e_ * D * D + i_ * 128),
                        (w3_t.tensor, e_ * D * D + i_ * 128)]
            else:
                srcs = [(sw1_t.tensor, i_ * 128), (sw3_t.tensor, i_ * 128)]
            stream = "wA" if slot == 0 else "wB"
            ssem = kb.sem[stream]
            bound = (ssem, ssem.n)
            tk = None
            for m, (tens, off) in enumerate(srcs):
                ap_in = bass.AP(tens, off, [[D, 128], [128 * D, NF], [1, 128]])
                dst = w13b[:, slot * W13HALF + m * NF * 128: slot * W13HALF + (m + 1) * NF * 128]
                tk = kb.dma("gpsimd", dst, ap_in, stream=stream,
                            waits=[wait_tok, bound] if m == 0 else [])
            return tk

        def emit_w2_load(e_, wait_tok):
            ssem = kb.sem["w2s"]
            bound = (ssem, ssem.n)
            tk = None
            for i_ in range(NF):
                if e_ < 16:
                    src = bass.AP(w2_t.tensor, e_ * D * D + i_ * 128 * D, [[D, 128], [1, D]])
                else:
                    src = bass.AP(sw2_t.tensor, i_ * 128 * D, [[D, 128], [1, D]])
                tk = kb.dma("gpsimd", w2b[:, i_ * D:(i_ + 1) * D], src, stream="w2s",
                            waits=[wait_tok, bound] if i_ == 0 else [])
            return tk

        prev_out_tok = x_tok
        prev_grep_tok = None
        slot_cons = {0: None, 1: None}
        pend = {(0, 0): emit_w13_loads(0, 0, 0, kb.rtok_w13),
                (0, 1): emit_w13_loads(0, 1, 1, kb.rtok_w13)}
        w2_tok = emit_w2_load(0, kb.rtok_w2)
        xupd = x_tok
        for e_ in range(17):
            if e_ < 16:
                gtv = None
                rowtv = None
                for tt in range(NT):
                    pg = psb[4 + tt % 2]
                    pr = psb[6]
                    tkr = kb.mm(pr.ap[0:1, 0:TT], s_idb[0:16, e_:e_ + 1],
                                s_gT[0:16, tt * TT:(tt + 1) * TT], True, True,
                                waits=[g_tok, pr.free])
                    rowtv = kb.vec(lambda e, p=pr.ap: e.tensor_copy(
                        out=s_mean[0:1, :], in_=p[0:1, 0:TT]),
                        waits=[tkr, kb.rowtok])
                    pr.free = rowtv
                    tk = kb.mm(pg.ap, s_or_b[0:1, :], s_mean[0:1, :],
                               True, True, waits=[rowtv, pg.free])
                    kb.rowtok = tk
                    gtv = kb.vec(lambda e, p=pg.ap, tt=tt: e.tensor_scalar(
                        out=s_grep[:, tt * TT:(tt + 1) * TT], in0=p, scalar1=0.5,
                        scalar2=None, op0=ALU.mult), waits=[tk, prev_grep_tok, gtv])
                    pg.free = gtv
                grep_tok = gtv
            else:
                grep_tok = kb.vec(lambda e: e.memset(s_grep[:], 0.5),
                                  waits=[prev_grep_tok])
            act_last = None
            prev_tva = None
            for i_ in range(NF):
                ldt = pend.pop((e_, i_))
                slot = i_ % 2
                nxt = (e_, i_ + 2) if i_ + 2 < NF else (e_ + 1, i_ + 2 - NF)
                h_last = None
                for tt in range(NT):
                    p1 = psb[tt % 2]
                    p3 = psb[2 + tt % 2]
                    sl = w13b[:, slot * W13HALF:(slot + 1) * W13HALF]
                    parts1 = [(sl[:, kc * 128:(kc + 1) * 128],
                               XN[:, kc * T + tt * TT: kc * T + (tt + 1) * TT])
                              for kc in range(NF)]
                    parts3 = [(sl[:, NF * 128 + kc * 128: NF * 128 + (kc + 1) * 128],
                               XN[:, kc * T + tt * TT: kc * T + (tt + 1) * TT])
                              for kc in range(NF)]
                    tk1 = kb.mm_group(p1, parts1, waits=[ldt, xn_tok])
                    tk3 = kb.mm_group(p3, parts3, waits=[ldt, xn_tok])
                    h_last = tk3
                    tsl = kb.sca(lambda e, p=p1.ap: e.activation(out=tmpC[:], in_=p,
                                                                 func=AF.Silu),
                                 waits=[tk1, prev_tva])
                    p1.free = tsl
                    adst = ACT[:, i_ * T + tt * TT: i_ * T + (tt + 1) * TT]
                    tva = kb.vec(lambda e, a=adst, p=p3.ap: e.tensor_tensor(
                        out=a, in0=tmpC[:], in1=p, op=ALU.mult),
                        waits=[tsl, tk3, prev_out_tok])
                    p3.free = tva
                    act_last = tva
                    prev_tva = tva
                slot_cons[slot] = h_last
                if nxt[0] < 17:
                    pend[nxt] = emit_w13_loads(nxt[0], nxt[1], slot, slot_cons[slot])
            out_last = None
            for o in range(NF):
                for tt in range(NT):
                    po = psb[4 + (o * NT + tt) % 2]
                    parts = [(w2b[:, i_ * D + o * 128: i_ * D + (o + 1) * 128],
                              ACT[:, i_ * T + tt * TT: i_ * T + (tt + 1) * TT])
                             for i_ in range(NF)]
                    tko = kb.mm_group(po, parts, waits=[act_last, w2_tok])
                    out_last = tko
                    xsl = X[:, o * T + tt * TT: o * T + (tt + 1) * TT]
                    tvx1 = kb.vec(lambda e, p=po.ap, tt=tt: e.tensor_tensor(
                        out=tmpA[:], in0=p, in1=s_grep[:, tt * TT:(tt + 1) * TT],
                        op=ALU.mult), waits=[tko, grep_tok, xupd])
                    xupd = kb.vec(lambda e, x=xsl: e.tensor_tensor(
                        out=x, in0=x, in1=tmpA[:], op=ALU.add), waits=[tvx1])
                    po.free = xupd
            prev_out_tok = out_last
            prev_grep_tok = grep_tok
            if e_ + 1 < 17:
                w2_tok = emit_w2_load(e_ + 1, out_last)
        for snm in ("wA", "wB", "w2s"):
            ss = kb.sem[snm]
            kb.em("gpsimd", lambda e, h=ss.h, v=ss.n: e.wait_ge(h, v))
        kb.rtok_w13 = xupd
        kb.rtok_w2 = xupd
        return xupd

    def emit_l1ffn(xn_tok, g_tok, x_tok):
        W13HALF = 2 * NF * 128

        def w1_load(e_, i_, slot, wait_tok):
            if e_ < 12:
                na, wt, ncols = 24, We1.tensor, 6 * D
                base = e_ * D * 6 * D
            else:
                na, wt, ncols = 16, Ws1.tensor, 4 * D
                base = 0
            stream = "wA" if slot == 0 else "wB"
            ssem = kb.sem[stream]
            bound = (ssem, ssem.n)
            tk = None
            for m, col in enumerate((i_, i_ + na)):
                ap_in = bass.AP(wt, base + col * 128,
                                [[ncols, 128], [128 * ncols, NF], [1, 128]])
                dst = w13b[:, slot * W13HALF + m * NF * 128: slot * W13HALF + (m + 1) * NF * 128]
                tk = kb.dma("gpsimd", dst, ap_in, stream=stream,
                            waits=[wait_tok, bound] if m == 0 else [])
            return tk

        def w2_load(e_, ig, wait_tok):
            if e_ < 12:
                wt, base = We2.tensor, e_ * 3 * D * D
            else:
                wt, base = Ws2.tensor, 0
            ssem = kb.sem["w2s"]
            bound = (ssem, ssem.n)
            tk = None
            for j in range(NF):
                r = ig * NF + j
                src = bass.AP(wt, base + r * 128 * D, [[D, 128], [1, D]])
                tk = kb.dma("gpsimd", w2b[:, j * D:(j + 1) * D], src, stream="w2s",
                            waits=[wait_tok, bound] if j == 0 else [])
            return tk

        seq = []
        for e_ in range(13):
            na = 24 if e_ < 12 else 16
            for i_ in range(na):
                seq.append((e_, i_))
        pend = {}
        slot_cons = {0: kb.rtok_w13, 1: kb.rtok_w13}
        pend[seq[0]] = w1_load(seq[0][0], seq[0][1], 0, kb.rtok_w13)
        pend[seq[1]] = w1_load(seq[1][0], seq[1][1], 1, kb.rtok_w13)
        seqpos = 2
        w2_tok = w2_load(0, 0, kb.rtok_w2)
        prev_out_tok = x_tok
        prev_grep_tok = None
        xupd = x_tok
        si = 0
        for e_ in range(13):
            na = 24 if e_ < 12 else 16
            if e_ < 12:
                gtv = None
                for tt in range(NT):
                    pg = psb[4 + tt % 2]
                    pr = psb[6]
                    tkr = kb.mm(pr.ap[0:1, 0:TT], s_idb[0:12, e_:e_ + 1],
                                s_gT[0:12, tt * TT:(tt + 1) * TT], True, True,
                                waits=[g_tok, pr.free])
                    rowtv = kb.vec(lambda e, p=pr.ap: e.tensor_copy(
                        out=s_mean[0:1, :], in_=p[0:1, 0:TT]),
                        waits=[tkr, kb.rowtok])
                    pr.free = rowtv
                    tk = kb.mm(pg.ap, s_or_b[0:1, :], s_mean[0:1, :],
                               True, True, waits=[rowtv, pg.free])
                    kb.rowtok = tk
                    gtv = kb.vec(lambda e, p=pg.ap, tt=tt: e.tensor_scalar(
                        out=s_grep[:, tt * TT:(tt + 1) * TT], in0=p, scalar1=0.5,
                        scalar2=None, op0=ALU.mult), waits=[tk, prev_grep_tok, gtv])
                    pg.free = gtv
                grep_tok = gtv
            else:
                grep_tok = kb.vec(lambda e: e.memset(s_grep[:], 0.5),
                                  waits=[prev_grep_tok])
            for ig in range(na // NF):
                act_last = None
                prev_tva = None
                for i8 in range(NF):
                    i_ = ig * NF + i8
                    ldt = pend.pop((e_, i_))
                    slot = si % 2
                    si += 1
                    h_last = None
                    sl = w13b[:, slot * W13HALF:(slot + 1) * W13HALF]
                    if e_ < 12:
                        bcol_a = be1col[:, 0:1] if False else None
                        ca = e_ * 48 + i_
                        cg = e_ * 48 + i_ + na
                        bias_t = s_be1
                    else:
                        ca, cg = i_, i_ + na
                        bias_t = s_bs1
                    for tt in range(NT):
                        p1 = psb[tt % 2]
                        p3 = psb[2 + tt % 2]
                        parts1 = [(sl[:, kc * 128:(kc + 1) * 128],
                                   XN[:, kc * T + tt * TT: kc * T + (tt + 1) * TT])
                                  for kc in range(NF)]
                        parts3 = [(sl[:, NF * 128 + kc * 128: NF * 128 + (kc + 1) * 128],
                                   XN[:, kc * T + tt * TT: kc * T + (tt + 1) * TT])
                                  for kc in range(NF)]
                        tk1 = kb.mm_group(p1, parts1, waits=[ldt, xn_tok])
                        tk3 = kb.mm_group(p3, parts3, waits=[ldt, xn_tok])
                        h_last = tk3
                        tsl = kb.sca(lambda e, p=p3.ap, cg=cg, b=bias_t: e.activation(
                            out=tmpC[:], in_=p, func=AF.Gelu, bias=b[:, cg:cg + 1]),
                            waits=[tk3, prev_tva, tok_init_d2])
                        p3.free = tsl
                        adst = ACT[:, i8 * T + tt * TT: i8 * T + (tt + 1) * TT]
                        tva = kb.vec(lambda e, a=adst, p=p1.ap, ca=ca, b=bias_t:
                                     e.scalar_tensor_tensor(
                                         out=a, in0=p, scalar=b[:, ca:ca + 1],
                                         in1=tmpC[:], op0=ALU.add, op1=ALU.mult),
                            waits=[tsl, tk1, prev_out_tok, tok_init_d2])
                        p1.free = tva
                        act_last = tva
                        prev_tva = tva
                    slot_cons[slot] = h_last
                    if seqpos < len(seq):
                        nxt = seq[seqpos]
                        seqpos += 1
                        pend[nxt] = w1_load(nxt[0], nxt[1], slot, slot_cons[slot])
                out_last = None
                for o in range(NF):
                    for tt in range(NT):
                        po = psb[4 + (o * NT + tt) % 2]
                        parts = [(w2b[:, j * D + o * 128: j * D + (o + 1) * 128],
                                  ACT[:, j * T + tt * TT: j * T + (tt + 1) * TT])
                                 for j in range(NF)]
                        tko = kb.mm_group(po, parts, waits=[act_last, w2_tok])
                        out_last = tko
                        if e_ < 12:
                            co = e_ * NF + o
                            b2 = s_be2
                        else:
                            co = o
                            b2 = s_bs2
                        xsl = X[:, o * T + tt * TT: o * T + (tt + 1) * TT]
                        if ig == 0:
                            tvx1 = kb.vec(lambda e, p=po.ap, tt=tt, co=co, b2=b2:
                                          e.scalar_tensor_tensor(
                                              out=tmpA[:], in0=p, scalar=b2[:, co:co + 1],
                                              in1=s_grep[:, tt * TT:(tt + 1) * TT],
                                              op0=ALU.add, op1=ALU.mult),
                                          waits=[tko, grep_tok, xupd])
                        else:
                            tvx1 = kb.vec(lambda e, p=po.ap, tt=tt: e.tensor_tensor(
                                out=tmpA[:], in0=p,
                                in1=s_grep[:, tt * TT:(tt + 1) * TT], op=ALU.mult),
                                waits=[tko, grep_tok, xupd])
                        xupd = kb.vec(lambda e, x=xsl: e.tensor_tensor(
                            out=x, in0=x, in1=tmpA[:], op=ALU.add), waits=[tvx1])
                        po.free = xupd
                prev_out_tok = out_last
                if not (e_ == 12 and ig == na // NF - 1):
                    ne, nig = (e_, ig + 1) if ig + 1 < na // NF else (e_ + 1, 0)
                    w2_tok = w2_load(ne, nig, out_last)
            prev_grep_tok = grep_tok
        for snm in ("wA", "wB", "w2s"):
            ss = kb.sem[snm]
            kb.em("gpsimd", lambda e, h=ss.h, v=ss.n: e.wait_ge(h, v))
        kb.rtok_w13 = xupd
        kb.rtok_w2 = xupd
        return xupd

    def emit_l0attn(xn_tok, x_tok):
        rr = [0]

        def pgrab():
            pt = psb[rr[0] % 6]
            rr[0] += 1
            return pt

        def aw_load(i_, slot, wait_tok):
            stream = "wA" if slot == 0 else "wB"
            ssem = kb.sem[stream]
            bound = (ssem, ssem.n)
            ap_in = bass.AP(aW_t.tensor, i_ * 128, [[D, 128], [128 * D, NF], [1, 128]])
            dst = w13b[:, slot * 2 * NF * 128: slot * 2 * NF * 128 + NF * 128]
            return kb.dma("gpsimd", dst, ap_in, stream=stream, waits=[wait_tok, bound])

        slot_cons = {0: kb.rtok_w13, 1: kb.rtok_w13}
        pend = {0: aw_load(0, 0, kb.rtok_w13), 1: aw_load(1, 1, kb.rtok_w13)}
        st_last = None
        for o in range(NF):
            slot = o % 2
            ldt = pend.pop(o)
            sl = w13b[:, slot * 2 * NF * 128: slot * 2 * NF * 128 + NF * 128]
            tko = None
            for tt in range(NT):
                po = pgrab()
                parts = [(sl[:, kc * 128:(kc + 1) * 128],
                          XN[:, kc * T + tt * TT: kc * T + (tt + 1) * TT])
                         for kc in range(NF)]
                tko = kb.mm_group(po, parts, waits=[ldt, xn_tok])
                st_last = kb.vec(lambda e, p=po.ap, o=o, tt=tt: e.tensor_copy(
                    out=ACT[:, o * T + tt * TT: o * T + (tt + 1) * TT], in_=p),
                    waits=[tko, x_tok])
                po.free = st_last
            slot_cons[slot] = tko
            if o + 2 < NF:
                pend[o + 2] = aw_load(o + 2, slot, slot_cons[slot])
        static_tok = st_last

        idtok = kb.vec(lambda e: e.tensor_copy(out=s_id2[:, 0:128], in_=s_idf[:]),
                       waits=[slot_cons[0], slot_cons[1], kb.rowtok, kb.rtok_w2])
        idtok = kb.vec(lambda e: e.tensor_copy(out=s_id2[:, 128:256], in_=s_idf[:]),
                       waits=[idtok])
        bt = idtok
        for j in range(8):
            for hh, d in ((0, 0), (1, 64)):
                h = 2 * j + hh
                bt = kb.vec(lambda e, j=j, h=h, d=d: e.tensor_copy(
                    out=s_betp[d:d + 64, j:j + 1], in_=s_betac[d:d + 64, h:h + 1]),
                    waits=[bt])
                bt = kb.vec(lambda e, j=j, h=h, d=d: e.tensor_copy(
                    out=s_nbetp[d:d + 64, j:j + 1], in_=s_nbetac[d:d + 64, h:h + 1]),
                    waits=[bt])
        mask_tok = None
        for h in range(H):
            pq = pgrab()
            t1 = kb.sca(lambda e, h=h: e.activation(out=tmpA[:, 0:128], in_=s_iju[:],
                                                    func=AF.Exp, scale=s_loglc[:, h:h + 1]),
                        waits=[bt, mask_tok])
            t2 = kb.vec(lambda e, h=h: e.scalar_tensor_tensor(
                out=s_Dup[:, h * C:(h + 1) * C], in0=tmpA[:, 0:128],
                scalar=s_betac[:, h:h + 1], in1=s_mu[:], op0=ALU.mult, op1=ALU.mult),
                waits=[t1])
            t3 = kb.sca(lambda e, h=h: e.activation(out=tmpA[:, 128:256], in_=s_ijl[:],
                                                    func=AF.Exp, scale=s_loglc[:, h:h + 1]),
                        waits=[t2])
            t4 = kb.vec(lambda e, h=h: e.scalar_tensor_tensor(
                out=s_Dlo[:, h * C:(h + 1) * C], in0=tmpA[:, 128:256],
                scalar=s_betac[:, h:h + 1], in1=s_ml[:], op0=ALU.mult, op1=ALU.mult),
                waits=[t3])
            j, d = h // 2, 64 * (h % 2)
            t5 = kb.vec(lambda e, j=j, d=d, h=h: e.tensor_scalar(
                out=s_lamCI[d:d + 64, j * DH:(j + 1) * DH],
                in0=s_idf[d:d + 64, d:d + 64],
                scalar1=s_lamCc[d:d + 64, h:h + 1], scalar2=None, op0=ALU.mult),
                waits=[t4])
            pq2_ = pgrab()
            tkx = kb.mm(pq2_.ap[0:1, 0:128], s_idf[0:16, h:h + 1], s_powlT[0:16, :],
                        True, True, waits=[tplT, pq2_.free])
            trow = kb.vec(lambda e, p=pq2_.ap: e.tensor_copy(
                out=tmpA[0:1, 256:384], in_=p[0:1, 0:128]), waits=[tkx, t5])
            pq2_.free = trow
            tkp = kb.mm(pq.ap[d:d + 64, 0:128], s_or_f[0:1, 0:64], tmpA[0:1, 256:384],
                        True, True, waits=[trow, pq.free])
            mask_tok = kb.vec(lambda e, p=pq.ap, j=j, d=d: e.tensor_copy(
                out=s_prep[d:d + 64, j * C:(j + 1) * C], in_=p[d:d + 64, 0:128]),
                waits=[tkp])
            pq.free = mask_tok
        z_tok = kb.vec(lambda e: e.memset(s_Sexp[:], 0.0), waits=[mask_tok])

        def emit_kc(c, extra_waits):
            prs = [psb[6], psb[7]]
            mm_toks = [None] * NF
            sq_prev = None
            for f in range(NF):
                pr = prs[f // 4]
                xnf = XN[:, f * T + c * C: f * T + (c + 1) * C]
                sq = kb.vec(lambda e, x=xnf: e.tensor_tensor(
                    out=tmpA[:, 256:384], in0=x, in1=x, op=ALU.mult),
                    waits=list(extra_waits) + [mm_toks[f - 1] if f else None, sq_prev])
                sq_prev = sq
                w = [sq]
                if f % 4 == 0:
                    w.append(pr.free)
                mm_toks[f] = kb.mm(pr.ap[0:2, (f % 4) * 128:(f % 4) * 128 + 128],
                                   s_h2[:], tmpA[:, 256:384], True, True, waits=w,
                                   force_inc=True)
            rms_toks = []
            for half in range(2):
                pr = prs[half]
                ts0 = kb.sca(lambda e, p=pr.ap, half=half: e.activation(
                    out=p[0:2, 0:TT], in_=p[0:2, 0:TT], func=AF.Sqrt,
                    bias=float(np.finfo(np.float32).eps), scale=float(1.0 / 64)),
                    waits=[mm_toks[half * 4 + 3], mm_toks[NF - 1]]
                          + ([rms_toks[0]] if half else []))

                def _rc(e, half=half, p=pr.ap):
                    with e.bass.allow_low_precision(reason="rms rows bf16"):
                        return e.reciprocal(out=s_rmsr[0:2, half * TT:(half + 1) * TT],
                                            in_=p[0:2, 0:TT])
                trc = kb.vec(_rc, waits=[ts0])
                pr.free = trc
                rms_toks.append(trc)
            k_tok = None
            for f in range(NF):
                pq = pgrab()
                tke = kb.mm(pq.ap[:, 0:128], s_p2[:],
                            s_rmsr[0:2, f * 128:(f + 1) * 128], True, True,
                            waits=[rms_toks[f // 4], pq.free])
                xnf = XN[:, f * T + c * C: f * T + (c + 1) * C]
                tv = kb.vec(lambda e, p=pq.ap, x=xnf: e.tensor_tensor(
                    out=tmpA[:, 256:384], in0=x, in1=p[:, 0:128], op=ALU.mult),
                    waits=[tke, sq_prev, k_tok])
                k_tok = kb.vec(lambda e, f=f: e.tensor_scalar(
                    out=s_kc[:, f * C:(f + 1) * C], in0=tmpA[:, 256:384],
                    scalar1=s_knw[:, 0:1], scalar2=None, op0=ALU.mult), waits=[tv])
                pq.free = k_tok
            return k_tok

        def drive(gens):
            gens = list(gens)
            while gens:
                keep = []
                for g in gens:
                    try:
                        next(g)
                        keep.append(g)
                    except StopIteration:
                        pass
                gens = keep

        prev_slot = [z_tok, z_tok]
        sexp_toks = {}
        sp_box = [None]
        u_toks = []
        ab_toks = []

        def pair_chain(c, j, si, kc_tok):
            sl = SLT[si]
            prev = prev_slot[si]
            sp_tok = sp_box[0]
            h0, h1 = 2 * j, 2 * j + 1
            kf = s_kc[:, j * C:(j + 1) * C]
            Kt, Kd, Xa, Xb = sl["Kt"], sl["Kd"], sl["Xa"], sl["Xb"]
            G, M1, Np, Mp = sl["G"], sl["M1"], sl["Np"], sl["Mp"]
            p1a, p1b = pgrab(), pgrab()
            kb.trans(p1a.bf[0:128, 0:64], kf[0:64, :], s_idb[0:64, 0:64],
                     waits=[kc_tok, p1a.free, prev])
            tk1b = kb.trans(p1b.bf[0:128, 0:64], kf[64:128, :],
                            s_idb[64:128, 64:128], waits=[p1b.free])
            tv1 = kb.vec(lambda e, p=p1a.bf, Kt=Kt: e.tensor_copy(
                out=Kt[:, 0:64], in_=p[:, 0:64]), waits=[tk1b, prev])
            tv1b = kb.vec(lambda e, p=p1b.bf, Kt=Kt: e.tensor_copy(
                out=Kt[:, 64:128], in_=p[:, 0:64]), waits=[tv1])
            p1a.free = tv1
            p1b.free = tv1b
            yield
            p2a, p2b = pgrab(), pgrab()
            stat0 = ACT[0:64, j * T + c * C: j * T + (c + 1) * C]
            stat1 = ACT[64:128, j * T + c * C: j * T + (c + 1) * C]
            kb.trans(p2a.bf[0:128, 0:64], stat0, s_idb[0:64, 0:64],
                     waits=[static_tok, p2a.free])
            tk2b = kb.trans(p2b.bf[0:128, 0:64], stat1, s_idb[64:128, 64:128],
                            waits=[p2b.free])
            tv2 = kb.vec(lambda e, p=p2a.bf, Xa=Xa: e.tensor_scalar(
                out=Xa[:, 0:64], in0=p[0:128, 0:64], scalar1=-1.0, scalar2=None,
                op0=ALU.mult), waits=[tk2b, prev])
            tv2b = kb.vec(lambda e, p=p2b.bf, Xa=Xa: e.tensor_scalar(
                out=Xa[:, 128:192], in0=p[0:128, 0:64], scalar1=-1.0, scalar2=None,
                op0=ALU.mult), waits=[tv2])
            p2a.free = tv2
            p2b.free = tv2b
            tv3 = kb.vec(lambda e, h=h0, Kt=Kt, Xa=Xa: e.tensor_scalar(
                out=Xa[:, 64:128], in0=Kt[:, 0:64], scalar1=s_powl[:, h:h + 1],
                scalar2=None, op0=ALU.mult), waits=[tv1])
            tv3b = kb.vec(lambda e, h=h1, Kt=Kt, Xa=Xa: e.tensor_scalar(
                out=Xa[:, 192:256], in0=Kt[:, 64:128], scalar1=s_powl[:, h:h + 1],
                scalar2=None, op0=ALU.mult), waits=[tv3, tv1b])
            tv4 = kb.vec(lambda e, h=h0, Kt=Kt, Kd=Kd: e.tensor_scalar(
                out=Kd[:, 0:64], in0=Kt[:, 0:64], scalar1=s_powr[:, h:h + 1],
                scalar2=None, op0=ALU.mult), waits=[tv1, prev])
            tv4b = kb.vec(lambda e, h=h1, Kt=Kt, Kd=Kd: e.tensor_scalar(
                out=Kd[:, 64:128], in0=Kt[:, 64:128], scalar1=s_powr[:, h:h + 1],
                scalar2=None, op0=ALU.mult), waits=[tv4, tv1b])
            yield
            p3a, p3b = pgrab(), pgrab()
            kb.mm(p3a.ap[:, 0:128], kf[0:64, :], kf[0:64, :], True, True,
                  waits=[kc_tok, p3a.free, prev])
            tk3 = kb.mm(p3b.ap[:, 0:128], kf[64:128, :], kf[64:128, :], True, True,
                        waits=[p3b.free])
            tvGa = kb.vec(lambda e, p=p3a.ap, G=G: e.tensor_copy(
                out=G[:, 0:128], in_=p[:, 0:128]), waits=[tk3, prev])
            tvG = kb.vec(lambda e, p=p3b.ap, G=G: e.tensor_copy(
                out=G[:, 128:256], in_=p[:, 0:128]), waits=[tvGa])
            p3a.free = tvGa
            p3b.free = tvG
            dup = s_Dup[:, h0 * C:(h0 + 2) * C]
            dlo = s_Dlo[:, h0 * C:(h0 + 2) * C]
            tvN = kb.vec(lambda e, dlo=dlo, G=G, Np=Np: e.tensor_tensor(
                out=Np[:], in0=G[:], in1=dlo, op=ALU.mult), waits=[tvG, prev])
            tvM = kb.vec(lambda e, dup=dup, G=G, M1=M1: e.tensor_tensor(
                out=M1[:], in0=G[:], in1=dup, op=ALU.mult), waits=[tvG])
            yield
            curM, curM_t = M1, tvM
            curN, curN_t = Np, tvN
            mp_toks = {0: tvM}
            for lv in range(1, 7):
                pma, pmb = pgrab(), pgrab()
                kb.mm(pma.ap[:, 0:128], curN[:, 0:128], curM[:, 0:128], True, True,
                      waits=[curM_t, curN_t, pma.free])
                tkm = kb.mm(pmb.ap[:, 0:128], curN[:, 128:256], curM[:, 128:256],
                            True, True, waits=[pmb.free])
                tvma = kb.vec(lambda e, p=pma.ap, lv=lv, Mp=Mp: e.tensor_copy(
                    out=Mp[lv][:, 0:128], in_=p[:, 0:128]), waits=[tkm])
                tvm = kb.vec(lambda e, p=pmb.ap, lv=lv, Mp=Mp: e.tensor_copy(
                    out=Mp[lv][:, 128:256], in_=p[:, 0:128]), waits=[tvma])
                pma.free = tvma
                pmb.free = tvm
                mp_toks[lv] = tvm
                yield
                if lv < 6:
                    pna, pnb = pgrab(), pgrab()
                    kb.mm(pna.ap[:, 0:128], curM[:, 0:128], curN[:, 0:128], True,
                          True, waits=[curM_t, curN_t, pna.free])
                    tkn = kb.mm(pnb.ap[:, 0:128], curM[:, 128:256], curN[:, 128:256],
                                True, True, waits=[pnb.free])
                    tvna = kb.vec(lambda e, p=pna.ap, Np=Np: e.tensor_copy(
                        out=Np[:, 0:128], in_=p[:, 0:128]),
                        waits=[tkn, tvm, tvN])
                    tvn = kb.vec(lambda e, p=pnb.ap, Np=Np: e.tensor_copy(
                        out=Np[:, 128:256], in_=p[:, 0:128]), waits=[tvna])
                    pna.free = tvna
                    pnb.free = tvn
                    curN, curN_t = Np, tvn
                    yield
                curM, curM_t = Mp[lv], tvm
            xcur, xoth = Xa, Xb
            xa_tok, xb_tok = tv3, tv3b
            for lv in (6, 5, 4, 3, 2, 1, 0):
                Ml = M1 if lv == 0 else Mp[lv]
                paa, pab = pgrab(), pgrab()
                kb.mm(paa.ap[:, 0:128], Ml[:, 0:128], xcur[:, 0:128], True, True,
                      waits=[paa.free, mp_toks[lv], xa_tok])
                tka = kb.mm(pab.ap[:, 0:128], Ml[:, 128:256], xcur[:, 128:256],
                            True, True, waits=[pab.free, xb_tok])
                op = ALU.subtract if lv == 0 else ALU.add
                tva0 = kb.vec(lambda e, p=paa.ap, xc=xcur, xo=xoth, op=op:
                              e.tensor_tensor(out=xo[:, 0:128], in0=xc[:, 0:128],
                                              in1=p[:, 0:128], op=op), waits=[tka])
                tva1 = kb.vec(lambda e, p=pab.ap, xc=xcur, xo=xoth, op=op:
                              e.tensor_tensor(out=xo[:, 128:256], in0=xc[:, 128:256],
                                              in1=p[:, 0:128], op=op), waits=[tva0])
                paa.free = tva0
                pab.free = tva1
                xcur, xoth = xoth, xcur
                xa_tok, xb_tok = tva0, tva1
                yield
            U, U_t = xcur, xb_tok
            pa = pgrab()
            pbk = pgrab()
            for hh, d in ((0, 0), (1, 64)):
                u2 = U[:, hh * 128 + 64: hh * 128 + 128]
                u1 = U[:, hh * 128: hh * 128 + 64]
                kd = Kd[:, hh * 64:(hh + 1) * 64]
                kb.mm(pa.ap[d:d + 64, 0:64], u2, kd, True, True,
                      waits=[U_t, tv4b, pa.free] if hh == 0 else [])
                kb.mm(pbk.ap[d:d + 64, 0:64], kd, u1, True, True,
                      waits=[U_t, tv4b, pbk.free] if hh == 0 else [])
            tka2 = (kb.sem["t"], kb.sem["t"].n)
            tA = kb.vec(lambda e, p=pa.ap, j=j: e.scalar_tensor_tensor(
                out=s_At[:, j * DH:(j + 1) * DH], in0=p[0:128, 0:64],
                scalar=s_nbetp[:, j:j + 1],
                in1=s_lamCI[:, j * DH:(j + 1) * DH],
                op0=ALU.mult, op1=ALU.add), waits=[tka2, sp_tok])
            pa.free = tA
            tB = kb.vec(lambda e, p=pbk.ap, j=j: e.tensor_scalar(
                out=s_Bt[:, j * DH:(j + 1) * DH], in0=p[0:128, 0:64],
                scalar1=s_betp[:, j:j + 1],
                scalar2=None, op0=ALU.mult), waits=[tka2, tA])
            pbk.free = tB
            ab_toks.extend([tA, tB])
            yield
            tu = kb.vec(lambda e, U=U, j=j: e.tensor_copy(
                out=s_usl[:, j * 256:(j + 1) * 256], in_=U[:]),
                waits=[U_t, sp_tok])
            u_toks.append(tu)
            pc_ = pgrab()
            for hh, d in ((0, 0), (1, 64)):
                ssl = s_Sexp[d:d + 64, j * DH:(j + 1) * DH]
                kb.mm(pc_.ap[d:d + 64, 0:64], s_At[d:d + 64, j * DH:(j + 1) * DH],
                      ssl, True, True,
                      waits=[tA, sexp_toks[j], pc_.free] if hh == 0 else [])
            tks = (kb.sem["t"], kb.sem["t"].n)
            tvs = kb.vec(lambda e, p=pc_.ap, j=j: e.tensor_tensor(
                out=s_Sexp[:, j * DH:(j + 1) * DH], in0=p[0:128, 0:64],
                in1=s_Bt[:, j * DH:(j + 1) * DH], op=ALU.add),
                waits=[tks, tB])
            pc_.free = tvs
            sexp_toks[j] = tvs
            prev_slot[si] = tvs

        for j in range(8):
            sexp_toks[j] = z_tok
        for c in range(NTC):
            kc_tok = emit_kc(c, [xn_tok, prev_slot[0], prev_slot[1]])
            ab_toks.clear()
            u_toks.clear()
            for jj in (0, 2, 4, 6):
                drive([pair_chain(c, jj, 0, kc_tok),
                       pair_chain(c, jj + 1, 1, kc_tok)])
            spsem = kb.sem["sp"]
            kb.dma("gpsimd", u_sp.ap()[c * 128:(c + 1) * 128, :], s_usl[:],
                   waits=list(u_toks) + [(spsem, spsem.n)], stream="sp")
            kb.dma("gpsimd", a_sp.ap()[c * 128:(c + 1) * 128, :], s_At[:],
                   waits=list(ab_toks), stream="sp")
            sp_box[0] = kb.dma("gpsimd", b_sp.ap()[c * 128:(c + 1) * 128, :],
                               s_Bt[:], waits=[], stream="sp")
        sp_tok = sp_box[0]

        all_exp = [sexp_toks[j] for j in range(8)]
        tex = kb.dma("gpsimd", cc_s_in.ap(), s_Sexp[:], waits=all_exp, stream="dbg")
        tcc = kb.em("gpsimd", lambda e: e.collective_compute(
            "AllGather", ALU.bypass, replica_groups=groups,
            ins=[cc_s_in.ap().opt()], outs=[cc_s_out.ap().opt()]),
            waits=[tex], inc=sem["cc"])
        tld = kb.dma("gpsimd", s_Sexp[:], cc_s_out.ap()[0:128, :], waits=[tcc],
                     stream="dbg")
        tS0 = kb.vec(lambda e: e.tensor_scalar(
            out=s_S[:], in0=s_Sexp[:], scalar1=s_odd[:, 0:1], scalar2=None,
            op0=ALU.mult), waits=[tld])

        s_toks = {j: tS0 for j in range(8)}
        xupd = x_tok
        prev2_slot = [tS0, tS0]
        y_box = [tS0]
        y_toks = []

        def p2_chain(c, j, si, kc_tok2, uld_tok):
            sl = SLT[si]
            prev = prev2_slot[si]
            y_prev = y_box[0]
            h0 = 2 * j
            kf = s_kc[:, j * C:(j + 1) * C]
            G, M1, FG, TT, Pt = sl["G"], sl["M1"], sl["FG"], sl["TT"], sl["Pt"]
            p3a, p3b = pgrab(), pgrab()
            kb.mm(p3a.ap[:, 0:128], kf[0:64, :], kf[0:64, :], True, True,
                  waits=[kc_tok2, p3a.free, prev])
            tk3 = kb.mm(p3b.ap[:, 0:128], kf[64:128, :], kf[64:128, :], True, True,
                        waits=[p3b.free])
            tvGa = kb.vec(lambda e, p=p3a.ap, G=G: e.tensor_copy(
                out=G[:, 0:128], in_=p[:, 0:128]), waits=[tk3, prev])
            tvG = kb.vec(lambda e, p=p3b.ap, G=G: e.tensor_copy(
                out=G[:, 128:256], in_=p[:, 0:128]), waits=[tvGa])
            p3a.free = tvGa
            p3b.free = tvG
            dup = s_Dup[:, h0 * C:(h0 + 2) * C]
            tvM = kb.vec(lambda e, dup=dup, G=G, M1=M1: e.tensor_tensor(
                out=M1[:], in0=G[:], in1=dup, op=ALU.mult), waits=[tvG, prev])
            tvI = kb.vec(lambda e, G=G, TT=TT: e.tensor_tensor(
                out=TT[:], in0=G[:], in1=s_id2[:], op=ALU.mult), waits=[tvG])
            tvFa = kb.vec(lambda e, h=h0, TT=TT, M1=M1, FG=FG:
                          e.scalar_tensor_tensor(
                out=FG[:, 0:128], in0=TT[:, 0:128], scalar=s_betac[:, h:h + 1],
                in1=M1[:, 0:128], op0=ALU.mult, op1=ALU.add), waits=[tvI, tvM])
            tvF = kb.vec(lambda e, h=h0 + 1, TT=TT, M1=M1, FG=FG:
                         e.scalar_tensor_tensor(
                out=FG[:, 128:256], in0=TT[:, 128:256],
                scalar=s_betac[:, h:h + 1],
                in1=M1[:, 128:256], op0=ALU.mult, op1=ALU.add), waits=[tvFa])
            tvq = kb.vec(lambda e, j=j, kf=kf, TT=TT: e.tensor_tensor(
                out=TT[:, 0:128], in0=kf, in1=s_prep[:, j * C:(j + 1) * C],
                op=ALU.mult), waits=[tvF])
            yield
            pp = pgrab()
            for hh, d in ((0, 0), (1, 64)):
                u2 = s_usl[:, j * 256 + hh * 128 + 64: j * 256 + hh * 128 + 128]
                kb.mm(pp.ap[d:d + 64, 0:128], u2, FG[:, hh * 128:(hh + 1) * 128],
                      True, True,
                      waits=[uld_tok, tvF, pp.free] if hh == 0 else [])
            tkp = (kb.sem["t"], kb.sem["t"].n)
            tvp = kb.vec(lambda e, p=pp.ap, TT=TT, Pt=Pt: e.tensor_tensor(
                out=Pt[:], in0=TT[:, 0:128], in1=p[0:128, 0:128],
                op=ALU.subtract), waits=[tkp, tvq])
            pp.free = tvp
            po1 = pgrab()
            for hh, d in ((0, 0), (1, 64)):
                u1 = s_usl[:, j * 256 + hh * 128: j * 256 + hh * 128 + 64]
                kb.mm(po1.ap[d:d + 64, 0:128], u1, FG[:, hh * 128:(hh + 1) * 128],
                      True, True,
                      waits=[uld_tok, tvF, po1.free] if hh == 0 else [])
            tko1 = (kb.sem["t"], kb.sem["t"].n)
            yield
            po2 = pgrab()
            for hh, d in ((0, 0), (1, 64)):
                ssl = s_S[d:d + 64, j * DH:(j + 1) * DH]
                kb.mm(po2.ap[d:d + 64, 0:128], ssl, Pt[d:d + 64, 0:128],
                      True, True,
                      waits=[tvp, s_toks[j], po2.free] if hh == 0 else [])
            tko2 = (kb.sem["t"], kb.sem["t"].n)
            ysl = s_yc[:, j * C:(j + 1) * C]
            stat = ACT[:, j * T + c * C: j * T + (c + 1) * C]
            ty0 = kb.sca(lambda e, p1=po1.ap, y=ysl: e.copy(
                out=y, in_=p1[0:128, 0:128]), waits=[tko1, y_prev])
            ty1 = kb.vec(lambda e, p2=po2.ap, y=ysl: e.tensor_tensor(
                out=y, in0=y, in1=p2[0:128, 0:128], op=ALU.add),
                waits=[tko2, ty0])
            ty2 = kb.vec(lambda e, y=ysl, st=stat: e.tensor_tensor(
                out=y, in0=y, in1=st, op=ALU.add), waits=[ty1, static_tok])
            po1.free = ty2
            po2.free = ty2
            y_toks.append(ty2)
            yield
            ps_ = pgrab()
            for hh, d in ((0, 0), (1, 64)):
                ssl = s_S[d:d + 64, j * DH:(j + 1) * DH]
                kb.mm(ps_.ap[d:d + 64, 0:64], s_At[d:d + 64, j * DH:(j + 1) * DH],
                      ssl, True, True,
                      waits=[uld_tok, tko2, ps_.free] if hh == 0 else [])
            tks = (kb.sem["t"], kb.sem["t"].n)
            tvs = kb.vec(lambda e, p=ps_.ap, j=j: e.tensor_tensor(
                out=s_S[:, j * DH:(j + 1) * DH], in0=p[0:128, 0:64],
                in1=s_Bt[:, j * DH:(j + 1) * DH], op=ALU.add),
                waits=[tks, uld_tok])
            ps_.free = tvs
            s_toks[j] = tvs
            prev2_slot[si] = tvs

        for c in range(NTC):
            spsem = kb.sem["sp"]
            kb.dma("gpsimd", s_usl[:], u_sp.ap()[c * 128:(c + 1) * 128, :],
                   waits=[prev2_slot[0], prev2_slot[1], sp_tok,
                          (spsem, spsem.n)], stream="sp")
            kb.dma("gpsimd", s_At[:], a_sp.ap()[c * 128:(c + 1) * 128, :],
                   waits=[], stream="sp")
            uld_tok = kb.dma("gpsimd", s_Bt[:], b_sp.ap()[c * 128:(c + 1) * 128, :],
                             waits=[], stream="sp")
            kc_tok2 = emit_kc(c, [xn_tok, prev2_slot[0], prev2_slot[1], y_box[0]])
            y_toks.clear()
            for jj in (0, 2, 4, 6):
                drive([p2_chain(c, jj, 0, kc_tok2, uld_tok),
                       p2_chain(c, jj + 1, 1, kc_tok2, uld_tok)])
            pS2, pQ2 = psb[6], psb[7]
            g1 = [(s_oc_f[:, 0:1], s_yc[:, f * C:(f + 1) * C]) for f in range(NF)]
            tstat1 = kb.mm_group(pS2, g1, waits=list(y_toks), out=pS2.ap[0:1, 0:128])
            mmq = None
            for f in range(NF):
                sq = kb.vec(lambda e, f=f: e.tensor_tensor(
                    out=tmpA[:, 256:384], in0=s_yc[:, f * C:(f + 1) * C],
                    in1=s_yc[:, f * C:(f + 1) * C], op=ALU.mult),
                    waits=list(y_toks) + [mmq])
                w = [sq]
                if f == 0:
                    w.append(pQ2.free)
                mmq = kb.mm(pQ2.ap[0:1, 0:128], s_oc_f[:, 0:1], tmpA[:, 256:384],
                            start=(f == 0), stop=(f == NF - 1), waits=w,
                            force_inc=True)
            tm1 = kb.vec(lambda e, p=pS2.ap: e.tensor_scalar(
                out=tmpA[0:1, 384:512], in0=p[0:1, 0:128],
                scalar1=float(1.0 / D), scalar2=None, op0=ALU.mult),
                waits=[tstat1, mmq])
            pS2.free = tm1
            tm2 = kb.vec(lambda e: e.tensor_tensor(
                out=tmpA[0:1, 0:128], in0=tmpA[0:1, 384:512], in1=tmpA[0:1, 384:512],
                op=ALU.mult), waits=[tm1])
            tm3 = kb.vec(lambda e, p=pQ2.ap: e.scalar_tensor_tensor(
                out=tmpA[0:1, 0:128], in0=p[0:1, 0:128], scalar=float(1.0 / D),
                in1=tmpA[0:1, 0:128], op0=ALU.mult, op1=ALU.subtract),
                waits=[tm2])
            pQ2.free = tm3
            tm4 = kb.sca(lambda e: e.activation(out=tmpA[0:1, 0:128],
                                                in_=tmpA[0:1, 0:128], func=AF.Sqrt,
                                                bias=1e-5), waits=[tm3])

            def _rc2(e):
                with e.bass.allow_low_precision(reason="ln rows"):
                    return e.reciprocal(out=tmpA[0:1, 128:256], in_=tmpA[0:1, 0:128])
            tm5 = kb.vec(_rc2, waits=[tm4])
            tm6 = kb.vec(lambda e: e.tensor_tensor(
                out=tmpA[0:1, 384:512], in0=tmpA[0:1, 384:512], in1=tmpA[0:1, 128:256],
                op=ALU.mult), waits=[tm5])
            prp = pgrab()
            tkr1 = kb.mm(prp.ap[:, 0:128], s_or_f[0:1, :], tmpA[0:1, 128:256],
                         True, True, waits=[tm5, prp.free])
            prm = pgrab()
            tkr2 = kb.mm(prm.ap[:, 0:128], s_or_f[0:1, :], tmpA[0:1, 384:512],
                         True, True, waits=[tm6, prm.free])
            lastf = None
            for f in range(NF):
                ta = kb.vec(lambda e, f=f, p=prp.ap: e.tensor_tensor(
                    out=tmpA[:, 256:384], in0=s_yc[:, f * C:(f + 1) * C],
                    in1=p[:, 0:128], op=ALU.mult), waits=[tkr1, lastf, mmq])
                tb = kb.vec(lambda e, p=prm.ap: e.tensor_tensor(
                    out=tmpA[:, 256:384], in0=tmpA[:, 256:384], in1=p[:, 0:128],
                    op=ALU.subtract), waits=[tkr2, ta])
                tc_ = kb.sca(lambda e, f=f: e.activation(
                    out=tmpA[:, 384:512], in_=tmpA[:, 256:384], func=AF.Identity,
                    bias=s_onb[:, f:f + 1], scale=s_onw[:, f:f + 1]), waits=[tb])
                xsl = X[:, f * T + c * C: f * T + (c + 1) * C]
                lastf = kb.vec(lambda e, x=xsl: e.tensor_tensor(
                    out=x, in0=x, in1=tmpA[:, 384:512], op=ALU.add),
                    waits=[tc_, xupd])
            prp.free = lastf
            prm.free = lastf
            xupd = lastf
            y_box[0] = lastf
        kb.rtok_w13 = xupd
        kb.rtok_w2 = xupd
        return xupd

    def emit_l1attn(xn_tok, x_tok):
        rr = [0]

        def pgrab():
            pt = psb[rr[0] % 6]
            rr[0] += 1
            return pt

        W13HALF = 2 * NF * 128

        def col_load(wt, base, ncols, i_, slot, wait_tok):
            stream = "wA" if slot == 0 else "wB"
            ssem = kb.sem[stream]
            bound = (ssem, ssem.n)
            ap_in = bass.AP(wt, base + i_ * 128, [[ncols, 128], [128 * ncols, NF], [1, 128]])
            dst = w13b[:, slot * NF * 128:(slot + 1) * NF * 128]
            return kb.dma("gpsimd", dst, ap_in, stream=stream, waits=[wait_tok, bound])

        fm_jobs = []
        for o in range(NF):
            fm_jobs.append(("q", o))
        for o in range(NF):
            fm_jobs.append(("u", o))
        slot_cons = {0: kb.rtok_w13, 1: kb.rtok_w13}

        def fm_src(kind, o):
            if kind == "q":
                return Wqkvu_t.tensor, o * 128, 4 * D
            return Wqkvu_t.tensor, 3 * D + o * 128, 4 * D

        pend = {}
        for j in range(2):
            kind, o = fm_jobs[j]
            wt, base, ncols = fm_src(kind, o)
            pend[fm_jobs[j]] = col_load(wt, base - o * 128 + o * 128, ncols, 0, j, kb.rtok_w13) \
                if False else col_load(wt, base // 1 - o * 128, ncols, o, j, kb.rtok_w13)
        fm_last = None
        for idx, (kind, o) in enumerate(fm_jobs):
            slot = idx % 2
            ldt = pend.pop((kind, o))
            sl = w13b[:, slot * NF * 128:(slot + 1) * NF * 128]
            tko = None
            for tt in range(NT):
                po = pgrab()
                parts = [(sl[:, kc * 128:(kc + 1) * 128],
                          XN[:, kc * T + tt * TT: kc * T + (tt + 1) * TT])
                         for kc in range(NF)]
                tko = kb.mm_group(po, parts, waits=[ldt, xn_tok], skip_gc=True)
                if kind == "q":
                    fm_last = kb.sca(lambda e, p=po.ap, o=o, tt=tt: e.activation(
                        out=SCR[:, o * T + tt * TT: o * T + (tt + 1) * TT], in_=p,
                        func=AF.Silu, bias=bq_colS[:, o:o + 1]),
                        waits=[tko, x_tok, fm_last])
                else:
                    fm_last = kb.sca(lambda e, p=po.ap, o=o, tt=tt: e.activation(
                        out=ACT[:, o * T + tt * TT: o * T + (tt + 1) * TT], in_=p,
                        func=AF.Identity, bias=bq_colS[:, 24 + o:25 + o]),
                        waits=[tko, x_tok, fm_last])
                po.free = fm_last
            slot_cons[slot] = tko
            if idx + 2 < len(fm_jobs):
                k2, o2 = fm_jobs[idx + 2]
                wt, base, ncols = fm_src(k2, o2)
                pend[(k2, o2)] = col_load(wt, base - o2 * 128, ncols, o2, slot,
                                          slot_cons[slot])
        qfm_tok = fm_last

        th_tok = None
        for f in range(NF):
            pq = pgrab()
            tk = kb.trans(pq.ap[0:1, 0:128], s_thc[:, f:f + 1], s_idf[:],
                          waits=[pq.free, tok_x0, slot_cons[0], slot_cons[1]])
            th_tok = kb.vec(lambda e, p=pq.ap, f=f: e.tensor_copy(
                out=s_throw[0:1, f * 128:(f + 1) * 128], in_=p[0:1, 0:128]),
                waits=[tk, qfm_tok])
            pq.free = th_tok
        rep_toks = []
        for half in range(2):
            pq = pgrab()
            tk = kb.mm(pq.ap[:, 0:TT], s_or_f[0:1, :], s_throw[0:1, half * TT:(half + 1) * TT],
                       True, True, waits=[th_tok, pq.free])
            tv = kb.vec(lambda e, p=pq.ap, half=half: e.tensor_copy(
                out=s_threp[:, half * TT:(half + 1) * TT], in_=p[:, 0:TT]),
                waits=[tk] + (rep_toks[-1:] if rep_toks else []))
            pq.free = tv
            rep_toks.append(tv)
        threp_tok = rep_toks[-1]

        def trig_into(dst, pa, tka, shift):
            t1 = kb.vec(lambda e, p=pa: e.tensor_scalar(
                out=p, in0=p, scalar1=float(np.float32(1.0 / TWO_PI)),
                scalar2=float(np.float32(shift / TWO_PI)), op0=ALU.mult, op1=ALU.add),
                waits=[tka])
            t2 = kb.vec(lambda e, p=pa: e.tensor_scalar(
                out=p, in0=p, scalar1=MAGIC, scalar2=MAGIC, op0=ALU.add,
                op1=ALU.subtract), waits=[t1])
            return t2

        kst_tok = None
        kc_tok = None
        mm_M_toks = []
        prev_gw = {0: slot_cons[0], 1: slot_cons[1]}
        wslot = [0]

        def kv_load(cb, wait_tok):
            slot = wslot[0] % 2
            wslot[0] += 1
            stream = "wA" if slot == 0 else "wB"
            ssem = kb.sem[stream]
            bound = (ssem, ssem.n)
            ap_in = bass.AP(Wqkvu_t.tensor, D + cb * 128,
                            [[4 * D, 128], [128 * 4 * D, NF], [1, 128]])
            dst = w13b[:, slot * NF * 128:(slot + 1) * NF * 128]
            return kb.dma("gpsimd", dst, ap_in, stream=stream,
                          waits=[wait_tok, bound]), slot

        def kvb_load(cb, wait_tok):
            half = cb % 2
            src = bass.AP(bkv_tile.tensor, cb * 128, [[2 * D, 128], [1, 128]])
            return kb.dma("gpsimd", s_kvb[:, half * 256: half * 256 + 128], src,
                          stream="dbg", waits=[wait_tok]), half

        evac_tok = None
        bias_cons = None
        group_trig = []
        m_groups_started = set()
        for grp in range(4):
            for cb in range(16):
                ldt, slot = kv_load(cb, prev_gw[wslot[0] % 2])
                bld, bhalf = kvb_load(cb, bias_cons)
                sl = w13b[:, slot * NF * 128:(slot + 1) * NF * 128]
                for tci in range(4):
                    tc = grp * 4 + tci
                    pk = pgrab()
                    parts = [(XN[:, kc * T + tc * 128: kc * T + (tc + 1) * 128],
                              sl[:, kc * 128:(kc + 1) * 128])
                             for kc in range(NF)]
                    tkk = kb.mm_group(pk, parts, waits=[ldt, xn_tok], skip_gc=True,
                                      out=pk.ap[:, 0:128])
                    if cb < 8:
                        dstk = s_kgrp[:, tci * D + cb * 128: tci * D + (cb + 1) * 128]
                        tb1 = kb.vec(lambda e, p=pk.ap, bh=bhalf: e.tensor_tensor(
                            out=tmpA[:, 0:128], in0=p[:, 0:128],
                            in1=s_kvb[:, bh * 256: bh * 256 + 128], op=ALU.add),
                            waits=[tkk, bld, evac_tok, kst_tok])
                        evac_tok = kb.sca(lambda e, d=dstk: e.activation(
                            out=d, in_=tmpA[:, 0:128], func=AF.Silu), waits=[tb1])
                    else:
                        dstv = s_vgrp[:, tci * D + (cb - 8) * 128:
                                      tci * D + (cb - 7) * 128]
                        evac_tok = kb.vec(lambda e, p=pk.ap, d=dstv, bh=bhalf:
                                          e.tensor_tensor(
                                              out=d, in0=p[:, 0:128],
                                              in1=s_kvb[:, bh * 256: bh * 256 + 128],
                                              op=ALU.add),
                                          waits=[tkk, bld, evac_tok, kst_tok])
                    pk.free = evac_tok
                prev_gw[slot] = tkk
                bias_cons = evac_tok
            kv_done = evac_tok
            for tci in range(4):
                tc = grp * 4 + tci
                trig_toks = []
                for half in range(2):
                    tang = kb.vec(lambda e, half=half, tc=tc: e.tensor_scalar(
                        out=tmpA[:], in0=s_threp[:, half * TT:(half + 1) * TT],
                        scalar1=s_tic[:, tc:tc + 1], scalar2=None, op0=ALU.mult),
                        waits=[threp_tok, kv_done, kst_tok,
                               group_trig[-1] if group_trig else None,
                               trig_toks[-1] if trig_toks else None])
                    for isc, (shift, dstt) in enumerate(
                            [(0.0, s_tsin), (float(np.pi / 2), s_tcos)]):
                        pa = pgrab()
                        t1 = kb.vec(lambda e, p=pa.ap, shift=shift: e.tensor_scalar(
                            out=p[:, 0:TT], in0=tmpA[:],
                            scalar1=float(np.float32(1.0 / TWO_PI)),
                            scalar2=float(np.float32(shift / TWO_PI)),
                            op0=ALU.mult, op1=ALU.add), waits=[tang, pa.free])
                        t2 = kb.vec(lambda e, p=pa.ap: e.tensor_scalar(
                            out=p[:, 0:TT], in0=p[:, 0:TT], scalar1=MAGIC,
                            scalar2=MAGIC, op0=ALU.add, op1=ALU.subtract), waits=[t1])
                        t3 = kb.vec(lambda e, p=pa.ap: e.scalar_tensor_tensor(
                            out=p[:, 0:TT], in0=p[:, 0:TT], scalar=float(-TWO_PI),
                            in1=tmpA[:], op0=ALU.mult, op1=ALU.add), waits=[t2])
                        if shift != 0.0:
                            t3 = kb.vec(lambda e, p=pa.ap, shift=shift: e.tensor_scalar(
                                out=p[:, 0:TT], in0=p[:, 0:TT], scalar1=shift,
                                scalar2=None, op0=ALU.add), waits=[t3])
                        t4 = kb.vec(lambda e, p=pa.ap: e.tensor_scalar(
                            out=p[:, 0:TT], in0=p[:, 0:TT], scalar1=3.141592,
                            scalar2=-3.141592, op0=ALU.min, op1=ALU.max), waits=[t3])
                        t5 = kb.sca(lambda e, p=pa.ap, half=half, d=dstt: e.activation(
                            out=d[:, half * TT:(half + 1) * TT], in_=p[:, 0:TT],
                            func=AF.Sin), waits=[t4])
                        pa.free = t5
                        trig_toks.append(t5)
                tks_ = kb.vec(lambda e, tc=tci: e.tensor_tensor(
                    out=s_kst[:], in0=s_kgrp[:, tc * D:(tc + 1) * D], in1=s_tsin[:],
                    op=ALU.mult), waits=trig_toks + [kv_done, kst_tok])
                tkc_ = kb.vec(lambda e, tc=tci: e.tensor_tensor(
                    out=s_kgrp[:, tc * D:(tc + 1) * D],
                    in0=s_kgrp[:, tc * D:(tc + 1) * D], in1=s_tcos[:],
                    op=ALU.mult), waits=[tks_])
                group_trig.append(tkc_)
                mlast = None
                for h in range(H):
                    pM = psb[6] if h < 8 else psb[7]
                    hh = h % 8
                    kcs = s_kgrp[:, tci * D + h * DH: tci * D + (h + 1) * DH]
                    kss = s_kst[:, h * DH:(h + 1) * DH]
                    vs = s_vgrp[:, tci * D + h * DH: tci * D + (h + 1) * DH]
                    w0 = [tkc_]
                    if tc == 0 and pM.free is not None:
                        w0.append(pM.free)
                    mlast = kb.mm(pM.ap[0:64, hh * DH:(hh + 1) * DH], kcs, vs,
                                  start=(tc == 0), stop=(tc == NTC - 1), waits=w0,
                                  inc=True, skip_gc=True, force_inc=(tc == NTC - 1))
                    mlast = kb.mm(pM.ap[64:128, hh * DH:(hh + 1) * DH], kss, vs,
                                  start=(tc == 0), stop=(tc == NTC - 1), waits=[],
                                  inc=True, skip_gc=True,
                                  force_inc=(tc == NTC - 1 or h == H - 1))
                kst_tok = mlast
        m_done = kst_tok

        tv_m1 = kb.vec(lambda e, p=psb[6].ap: e.tensor_copy(
            out=s_Mkv[:, 0:512], in_=p), waits=[m_done, group_trig[-1]])
        tv_m2 = kb.vec(lambda e, p=psb[7].ap: e.tensor_copy(
            out=s_Mkv[:, 512:1024], in_=p), waits=[m_done, tv_m1])
        psb[6].free = tv_m2
        psb[7].free = tv_m2
        tex = kb.dma("gpsimd", cc_m_in.ap()[0:128, 0:1024], s_Mkv[:], waits=[tv_m2],
                     stream="dbg")
        tcc = kb.em("gpsimd", lambda e: e.collective_compute(
            "AllReduce", ALU.add, replica_groups=groups,
            ins=[cc_m_in.ap().opt()], outs=[cc_m_out.ap().opt()]),
            waits=[tex], inc=sem["cc"])
        mld = None
        for h in range(H):
            d = 64 * (h % 2)
            j = h // 2
            mld = kb.dma("gpsimd", s_Mcos[d:d + 64, j * DH:(j + 1) * DH],
                         cc_m_out.ap()[0:64, h * DH:(h + 1) * DH],
                         waits=[tcc], stream="dbg")
            mld = kb.dma("gpsimd", s_Msin[d:d + 64, j * DH:(j + 1) * DH],
                         cc_m_out.ap()[64:128, h * DH:(h + 1) * DH],
                         waits=[], stream="dbg")
        kb.em("gpsimd", lambda e, h=sem["dbg"].h, v=sem["dbg"].n: e.wait_ge(h, v))
        m_tok = (sem["dbg"], sem["dbg"].n)

        s_o2k = s_rt[:, 64:65]
        o2k_tok = kb.vec(lambda e: e.tensor_scalar(
            out=s_o2k, in0=s_odd[:], scalar1=2048.0, scalar2=None, op0=ALU.mult),
            waits=[m_tok, kb.rowtok] + kb.phase())
        y_toks = []
        st_rows = {}
        for tt in range(NT):
            pS2, pQ2 = psb[6], psb[7]
            tit = kb.gp(lambda e, tt=tt: e.iota(s_trep[:], [[1, TT]], base=tt * TT,
                                                channel_multiplier=0,
                                                allow_small_or_imprecise_dtypes=True),
                        waits=[psb[6].free, psb[7].free, y_toks[-1] if y_toks else None])
            tof = kb.vec(lambda e: e.tensor_scalar(
                out=s_trep[:], in0=s_trep[:], scalar1=s_o2k, scalar2=None,
                op0=ALU.add), waits=[tit, o2k_tok])
            stat1_tok = None
            stat2_tok = None
            ang_tok = None
            for f in range(NF):
                ta0 = kb.vec(lambda e, f=f: e.tensor_scalar(
                    out=s_angB[:], in0=s_trep[:], scalar1=s_thc[:, f:f + 1],
                    scalar2=None, op0=ALU.mult), waits=[tof, ang_tok])
                t5s = t5c = None
                for shift, dst in ((0.0, s_sfm), (float(np.pi / 2), s_cfm)):
                    pa = pgrab()
                    t1 = kb.vec(lambda e, p=pa.ap, shift=shift: e.tensor_scalar(
                        out=p[:, 0:TT], in0=s_angB[:],
                        scalar1=float(np.float32(1.0 / TWO_PI)),
                        scalar2=float(np.float32(shift / TWO_PI)),
                        op0=ALU.mult, op1=ALU.add), waits=[ta0, pa.free])
                    t2 = kb.vec(lambda e, p=pa.ap: e.tensor_scalar(
                        out=p[:, 0:TT], in0=p[:, 0:TT], scalar1=MAGIC,
                        scalar2=MAGIC, op0=ALU.add, op1=ALU.subtract), waits=[t1])
                    t3 = kb.vec(lambda e, p=pa.ap: e.scalar_tensor_tensor(
                        out=p[:, 0:TT], in0=p[:, 0:TT], scalar=float(-TWO_PI),
                        in1=s_angB[:], op0=ALU.mult, op1=ALU.add), waits=[t2])
                    if shift != 0.0:
                        t3 = kb.vec(lambda e, p=pa.ap, shift=shift: e.tensor_scalar(
                            out=p[:, 0:TT], in0=p[:, 0:TT], scalar1=shift,
                            scalar2=None, op0=ALU.add), waits=[t3])
                    t4 = kb.vec(lambda e, p=pa.ap: e.tensor_scalar(
                        out=p[:, 0:TT], in0=p[:, 0:TT], scalar1=3.141592,
                        scalar2=-3.141592, op0=ALU.min, op1=ALU.max), waits=[t3])
                    t5 = kb.sca(lambda e, p=pa.ap, d=dst: e.activation(
                        out=d[:], in_=p[:, 0:TT], func=AF.Sin), waits=[t4])
                    pa.free = t5
                    if shift == 0.0:
                        t5s = t5
                    else:
                        t5c = t5
                ang_tok = t5c
                tqc = kb.vec(lambda e, f=f, tt=tt: e.tensor_tensor(
                    out=s_qc[:], in0=SCR[:, f * T + tt * TT: f * T + (tt + 1) * TT],
                    in1=s_cfm[:], op=ALU.mult), waits=[t5c, qfm_tok])
                tqs = kb.vec(lambda e, f=f, tt=tt: e.tensor_tensor(
                    out=s_qs[:], in0=SCR[:, f * T + tt * TT: f * T + (tt + 1) * TT],
                    in1=s_sfm[:], op=ALU.mult), waits=[t5s, tqc])
                po = pgrab()
                for d in (0, 64):
                    jj = f
                    kb.mm(po.ap[d:d + 64, 0:TT], s_Mcos[d:d + 64, jj * DH:(jj + 1) * DH],
                          s_qc[d:d + 64, :], True, False,
                          waits=[tqc, m_tok, po.free], skip_gc=True)
                    kb.mm(po.ap[d:d + 64, 0:TT], s_Msin[d:d + 64, jj * DH:(jj + 1) * DH],
                          s_qs[d:d + 64, :], False, True,
                          waits=[tqs], skip_gc=True, force_inc=True)
                tko = (kb.sem["t"], kb.sem["t"].n)
                ty = kb.vec(lambda e, p=po.ap, f=f, tt=tt: e.tensor_copy(
                    out=y1[:, f * T + tt * TT: f * T + (tt + 1) * TT], in_=p),
                    waits=[tko])
                tsq = kb.vec(lambda e, f=f, tt=tt: e.tensor_tensor(
                    out=tmpA[:], in0=y1[:, f * T + tt * TT: f * T + (tt + 1) * TT],
                    in1=y1[:, f * T + tt * TT: f * T + (tt + 1) * TT],
                    op=ALU.mult), waits=[tko, ty])
                po.free = tsq
                w1_ = [ty]
                if f == 0:
                    w1_.append(pS2.free)
                stat1_tok = kb.mm(pS2.ap[0:1, 0:TT], s_oc_b[:, 0:1],
                                  y1[:, f * T + tt * TT: f * T + (tt + 1) * TT],
                                  start=(f == 0), stop=(f == NF - 1), waits=w1_,
                                  force_inc=True)
                w2_ = [tsq, stat1_tok]
                if f == 0:
                    w2_.append(pQ2.free)
                stat2_tok = kb.mm(pQ2.ap[0:1, 0:TT], s_oc_f[:, 0:1], tmpA[:],
                                  start=(f == 0), stop=(f == NF - 1), waits=w2_,
                                  force_inc=True)
            st_rows[tt] = (stat1_tok, stat2_tok)
            tv1 = kb.vec(lambda e, p=pS2.ap: e.tensor_scalar(
                out=s_mean[0:1, :], in0=p[0:1, 0:TT], scalar1=float(1.0 / D),
                scalar2=None, op0=ALU.mult), waits=[stat1_tok, kb.rowtok])
            pS2.free = tv1
            tv2 = kb.vec(lambda e: e.tensor_tensor(
                out=tmpA[0:1, 0:TT], in0=s_mean[0:1, :], in1=s_mean[0:1, :],
                op=ALU.mult), waits=[tv1, stat2_tok])
            tv3 = kb.vec(lambda e, p=pQ2.ap: e.scalar_tensor_tensor(
                out=tmpA[0:1, 0:TT], in0=p[0:1, 0:TT], scalar=float(1.0 / D),
                in1=tmpA[0:1, 0:TT], op0=ALU.mult, op1=ALU.subtract),
                waits=[stat2_tok, tv2])
            pQ2.free = tv3
            ts0 = kb.sca(lambda e: e.activation(
                out=tmpA[0:1, 0:TT], in_=tmpA[0:1, 0:TT], func=AF.Sqrt, bias=1e-5),
                waits=[tv3])

            def _rc3(e):
                with e.bass.allow_low_precision(reason="ln rows bf16"):
                    return e.reciprocal(out=s_rstd[0:1, :], in_=tmpA[0:1, 0:TT])
            ts1 = kb.vec(_rc3, waits=[ts0])
            tv4 = kb.vec(lambda e: e.tensor_tensor(
                out=s_mrs[0:1, :], in0=s_mean[0:1, :], in1=s_rstd[0:1, :],
                op=ALU.mult), waits=[ts1])
            kb.rowtok = tv4
            prs = pgrab()
            tk3 = kb.mm(prs.ap, s_or_b[0:1, :], s_rstd[0:1, :], True, True,
                        waits=[ts1, prs.free])
            prm = pgrab()
            tk4 = kb.mm(prm.ap, s_or_b[0:1, :], s_mrs[0:1, :], True, True,
                        waits=[tv4, prm.free])
            lastz = None
            for f in range(NF):
                ysl = y1[:, f * T + tt * TT: f * T + (tt + 1) * TT]
                zsl = ACT[:, f * T + tt * TT: f * T + (tt + 1) * TT]
                ta = kb.vec(lambda e, y=ysl, p=prs.ap: e.tensor_tensor(
                    out=tmpA[:], in0=y, in1=p, op=ALU.mult),
                    waits=[tk3, lastz if lastz else st_rows[tt][1]])
                tb = kb.vec(lambda e, p=prm.ap: e.tensor_tensor(
                    out=tmpA[:], in0=tmpA[:], in1=p, op=ALU.subtract),
                    waits=[tk4, ta])
                tc_ = kb.sca(lambda e, f=f: e.activation(
                    out=tmpA[:], in_=tmpA[:], func=AF.Identity,
                    bias=s_lnb[:, f:f + 1], scale=s_lnw[:, f:f + 1]), waits=[tb])
                lastz = kb.vec(lambda e, z=zsl: e.tensor_tensor(
                    out=z, in0=z, in1=tmpA[:], op=ALU.mult), waits=[tc_, fm_last])
            prs.free = lastz
            prm.free = lastz
            y_toks.append(lastz)
        z_tok = y_toks[-1]

        slot_cons = {0: z_tok, 1: z_tok}
        pend2 = {0: col_load(Wout_t.tensor, 0, D, 0, 0, z_tok),
                 1: col_load(Wout_t.tensor, 0, D, 1, 1, z_tok)}
        xupd = x_tok
        for o in range(NF):
            slot = o % 2
            ldt = pend2.pop(o)
            sl = w13b[:, slot * NF * 128:(slot + 1) * NF * 128]
            tko = None
            for tt in range(NT):
                po = pgrab()
                parts = [(sl[:, kc * 128:(kc + 1) * 128],
                          ACT[:, kc * T + tt * TT: kc * T + (tt + 1) * TT])
                         for kc in range(NF)]
                tko = kb.mm_group(po, parts, waits=[ldt, z_tok], skip_gc=True)
                xsl = X[:, o * T + tt * TT: o * T + (tt + 1) * TT]
                xupd = kb.vec(lambda e, p=po.ap, x=xsl, o=o: e.scalar_tensor_tensor(
                    out=x, in0=p, scalar=s_boc[:, o:o + 1], in1=x,
                    op0=ALU.add, op1=ALU.add), waits=[tko, xupd])
                po.free = xupd
            slot_cons[slot] = tko
            if o + 2 < NF:
                pend2[o + 2] = col_load(Wout_t.tensor, 0, D, o + 2, slot,
                                        slot_cons[slot])
        for snm in ("wA", "wB", "dbg"):
            ss = kb.sem[snm]
            kb.em("gpsimd", lambda e, h=ss.h, v=ss.n: e.wait_ge(h, v))
        kb.rtok_w13 = xupd
        kb.rtok_w2 = xupd
        return xupd

    x_tok = tok_x0
    nsub = 0

    kb.scope = "s1_l0ffn"
    xn_tok = emit_ln(x_tok)
    g_tok = emit_router(xn_tok, 16, 4, s_gw, s_eb, None)
    x_tok = emit_l0ffn(xn_tok, g_tok, x_tok)
    nsub += 1

    if n_stop >= 2:
        kb.scope = "s2_l0attn"
        xn_tok = emit_ln(x_tok)
        x_tok = emit_l0attn(xn_tok, x_tok)
        nsub += 1

    if n_stop >= 3:
        kb.scope = "s3_l0ffn"
        xn_tok = emit_ln(x_tok)
        g_tok = emit_router(xn_tok, 16, 4, s_gw, s_eb, None)
        x_tok = emit_l0ffn(xn_tok, g_tok, x_tok)
        nsub += 1

    if n_stop >= 4:
        kb.scope = "s4_l1ffn"
        xn_tok = emit_ln(x_tok)
        g_tok = emit_router(xn_tok, 12, 2, s_wrb, None, float(np.float32(1.0 / 1.5)))
        x_tok = emit_l1ffn(xn_tok, g_tok, x_tok)
        nsub += 1

    if n_stop >= 5:
        kb.scope = "s5_l1attn"
        xn_tok = emit_ln(x_tok)
        x_tok = emit_l1attn(xn_tok, x_tok)
        nsub += 1

    if n_stop >= 6:
        kb.scope = "s6_l1ffn"
        xn_tok = emit_ln(x_tok)
        g_tok = emit_router(xn_tok, 12, 2, s_wrb, None, float(np.float32(1.0 / 1.5)))
        x_tok = emit_l1ffn(xn_tok, g_tok, x_tok)
        nsub += 1

    if "xn" in dbg:
        oxn = nc.dram_tensor("oxn", [D, T], F32, kind="ExternalOutput").ap()
        for f in range(NF):
            kb.dma("gpsimd", oxn[f * 128:(f + 1) * 128, :], XN[:, f * T:(f + 1) * T],
                   waits=[x_tok], stream="dbg")
    if "gates" in dbg:
        ogt = nc.dram_tensor("ogt", [16, T], F32, kind="ExternalOutput").ap()
        kb.dma("gpsimd", ogt, s_gT[:], waits=[x_tok], stream="dbg")
    if dbg:
        kb.em("gpsimd", lambda e, h=sem["dbg"].h, v=sem["dbg"].n: e.wait_ge(h, v))

    st_toks = []
    for f in range(NF):
        st_toks.append(kb.em(
            "sync", lambda e, f=f: e.dma_start(out=xout[f * 128:(f + 1) * 128, :],
                                               in_=X[:, f * T:(f + 1) * T]),
            waits=[x_tok], inc=sem["st"], k=16))
    kb.em("sync", lambda e: e.nop(), waits=[st_toks[-1]])

    kb._refs = (tidxrow, aW_t, knwcol, Wqkvu_t, bq_col, bkv_tile, Wout_t, boutcol,
                thetarow, lnwcol, lnbcol, Ws1, bs1col, Ws2, bs2col, We1, be1col,
                We2, be2col, cc_s_in, cc_s_out, cc_m_in, cc_m_out, SCR,
                R5A, R5B, s_S, s_Sexp, s_kc, s_yc, s_ijl, s_ml, s_iju, s_mu,
                s_idb, s_oc_b, s_knw, s_onw, s_onb, s_betac, s_loglc, s_lamCc,
                s_powr, s_powlT)
    return kb.finish()


def _prep_inputs(I):
    f32 = np.float32
    ct = np.ascontiguousarray
    m = {}
    m["gw_t"] = ct(I["L0_ffn_gate_w"].T)
    m["ebias_tile"] = ct(np.tile(np.asarray(I["L0_ffn_expert_bias"], f32)[None, :], (128, 1)))
    m["w1_t"] = ct(np.asarray(I["L0_ffn_w1"]).transpose(0, 2, 1))
    m["w2_t"] = ct(np.asarray(I["L0_ffn_w2"]).transpose(0, 2, 1))
    m["w3_t"] = ct(np.asarray(I["L0_ffn_w3"]).transpose(0, 2, 1))
    m["sw1_t"] = ct(np.asarray(I["L0_ffn_sw1"]).T)
    m["sw2_t"] = ct(np.asarray(I["L0_ffn_sw2"]).T)
    m["sw3_t"] = ct(np.asarray(I["L0_ffn_sw3"]).T)
    m["aW_t"] = ct(np.asarray(I["L0_attn_W"]).T)
    m["knwcol"] = ct(np.tile(np.asarray(I["L0_attn_k_norm_w"], f32), 2)[:, None])
    m["onw_c"] = ct(np.asarray(I["L0_attn_o_norm_w"], f32).reshape(NF, 128).T)
    m["onb_c"] = ct(np.asarray(I["L0_attn_o_norm_b"], f32).reshape(NF, 128).T)
    m["lamrow"] = ct(np.asarray(I["L0_attn_lam_logit"], f32)[None, :])
    m["etarow"] = ct(np.asarray(I["L0_attn_eta_raw"], f32)[None, :])
    m["Wqkvu_t"] = ct(np.asarray(I["L1_attn_Wqkvu"]).T)
    m["bq_col"] = ct(np.asarray(I["L1_attn_bqkvu"], f32).reshape(32, 128).T)
    m["bkv_tile"] = ct(np.tile(np.asarray(I["L1_attn_bqkvu"], f32)[None, 1024:3072], (128, 1)))
    m["Wout_t"] = ct(np.asarray(I["L1_attn_Wout"]).T)
    m["boutcol"] = ct(np.asarray(I["L1_attn_bout"], f32).reshape(NF, 128).T)
    m["thetarow"] = ct(np.asarray(I["L1_attn_theta"], f32).reshape(1, D))
    m["thetacol_in"] = ct(np.asarray(I["L1_attn_theta"], f32).reshape(NF, 128).T)
    m["lnwcol"] = ct(np.asarray(I["L1_attn_ln_w"], f32).reshape(NF, 128).T)
    m["lnbcol"] = ct(np.asarray(I["L1_attn_ln_b"], f32).reshape(NF, 128).T)
    m["Ws1"] = ct(np.asarray(I["L1_ffn_Ws1"], f32))
    m["bs1col"] = ct(np.asarray(I["L1_ffn_bs1"], f32).reshape(32, 128).T)
    m["Ws2"] = ct(np.asarray(I["L1_ffn_Ws2"], f32))
    m["bs2col"] = ct(np.asarray(I["L1_ffn_bs2"], f32).reshape(NF, 128).T)
    m["We1"] = ct(np.asarray(I["L1_ffn_We1"], f32))
    m["be1col"] = ct(np.asarray(I["L1_ffn_be1"], f32).reshape(12 * 48, 128).T)
    m["We2"] = ct(np.asarray(I["L1_ffn_We2"], f32))
    m["be2col"] = ct(np.asarray(I["L1_ffn_be2"], f32).reshape(12 * NF, 128).T)
    m["Wr"] = ct(np.asarray(I["L1_ffn_Wr"], f32))
    m["br_tile"] = ct(np.tile(np.asarray(I["L1_ffn_br"], f32)[None, :], (128, 1)))
    return m


def _core_inputs(I, shared, c):
    f32 = np.float32
    b, half = c // 2, c % 2
    xs = np.asarray(I["L0_attn_x"], f32)[b, half * T:(half + 1) * T, :]
    m = dict(shared)
    m["x_fm"] = np.ascontiguousarray(xs.T)
    tix = np.arange(half * T, (half + 1) * T, dtype=f32)
    m["tidxrow"] = tix[None, :].copy()
    m["tidxcol"] = np.ascontiguousarray(tix.reshape(NTC, 128).T)
    m["oddmask"] = np.full((128, 1), float(half), f32)
    return m


def kernel(**inputs):
    n_cores = 8
    shared = _prep_inputs(inputs)
    nc = build()
    in_maps = [_core_inputs(inputs, shared, c) for c in range(n_cores)]
    res = run_bass_kernel_spmd(nc, in_maps, core_ids=list(range(n_cores)))
    out = np.empty((4, 4096, D), np.float32)
    for c in range(n_cores):
        b, half = c // 2, c % 2
        out[b, half * T:(half + 1) * T, :] = res.results[c]["xout"].T
    return out
